# version 10
# baseline (speedup 1.0000x reference)
import numpy as np
import ml_dtypes
from contextlib import ExitStack

import concourse.bass as bass
import concourse.mybir as mybir
from concourse.bass_utils import run_bass_kernel_spmd

F32 = mybir.dt.float32
BF16 = mybir.dt.bfloat16
AF = mybir.ActivationFunctionType
ALU = mybir.AluOpType
AX = mybir.AxisListType

D_MODEL = 1024
D_INNER = 2048
NH = 32
HD = 64
DIP = 4384
HIDDEN = 2048
NE_R = 14
NE = 16
TC = 512
NQ = TC // 128
SEQ = 8192
NCHUNK = SEQ // TC

LN_EPS = 1e-5
MRMS_EPS = 1e-5
FRMS_EPS = 1.1920929e-07
GATE_EPS = 1e-9
MASK_NEG = -30000.0

NSLOT_F = 3
FSLOT_COLS = 4096
NSLOT_M = 2
MSLOT_COLS = 3072


class KB:

    COMPUTE = ("t", "v", "s", "p")

    def __init__(self, nc, ctx):
        self.nc = nc
        self.ctx = ctx
        self.eng = {"t": nc.tensor, "v": nc.vector, "s": nc.scalar,
                    "p": nc.gpsimd, "q": nc.sync}
        self.chain = {e: [ctx.enter_context(nc.semaphore(f"ch_{e}")), 0]
                      for e in self.COMPUTE}
        self.dsem = {}
        self.latest = {e: None for e in self.COMPUTE}
        self.last_w = {}
        self.readers = {}
        self.waited = {e: {} for e in self.eng}

    @staticmethod
    def _k(h):
        return h if isinstance(h, str) else h.name

    def _edge(self, rec, ceng):
        if rec is None:
            return
        peng = rec["eng"]
        if rec["dma"]:
            sem, val = rec["sem"], rec["val"]
        else:
            if peng == "t" and ceng == "t":
                return
            tgt = rec if rec["sem"] is not None else self.latest[peng]
            if tgt["sem"] is None:
                cs = self.chain[peng]
                cs[1] += 1
                tgt["ins"].then_inc(cs[0], 1)
                tgt["sem"], tgt["val"] = cs[0], cs[1]
            sem, val = tgt["sem"], tgt["val"]
        w = self.waited[ceng]
        if w.get(sem, 0) < val:
            self.eng[ceng].wait_ge(sem, val)
            w[sem] = val

    def emit(self, ceng, fn, r=(), w=(), dma=False):
        for h in r:
            self._edge(self.last_w.get(self._k(h)), ceng)
        for h in w:
            k = self._k(h)
            self._edge(self.last_w.get(k), ceng)
            for rec in self.readers.get(k, {}).values():
                self._edge(rec, ceng)
        ins = fn()
        rec = {"eng": ceng, "ins": ins, "sem": None, "val": None, "dma": bool(dma)}
        if dma:
            if dma not in self.dsem:
                self.dsem[dma] = [self.ctx.enter_context(self.nc.semaphore(f"ds_{dma}")), 0]
            ds = self.dsem[dma]
            ds[1] += 16
            ins.then_inc(ds[0], 16)
            rec["sem"], rec["val"] = ds[0], ds[1]
        else:
            self.latest[ceng] = rec
        for h in w:
            k = self._k(h)
            self.last_w[k] = rec
            self.readers[k] = {}
        for h in r:
            self.readers.setdefault(self._k(h), {})[ceng] = rec
        return ins

    def fence(self):
        nc = self.nc
        for role, (sem, cnt) in self.dsem.items():
            if cnt:
                nc.sync.wait_ge(sem, cnt)
        for e in self.COMPUTE:
            sem, cnt = self.chain[e]
            if cnt:
                self.eng[e].wait_ge(sem, cnt)
        nc.all_engine_barrier()
        for e in self.COMPUTE:
            nc.sync.sem_clear(self.chain[e][0])
        for role, (sem, cnt) in self.dsem.items():
            nc.sync.sem_clear(sem)
        nc.all_engine_barrier()
        for e in self.COMPUTE:
            self.chain[e][1] = 0
            self.latest[e] = None
        for role in self.dsem:
            self.dsem[role][1] = 0
        self.last_w = {}
        self.readers = {}
        self.waited = {e: {} for e in self.eng}


class Piece:
    def __init__(self, src, K, W, off, scales=None):
        self.src = src
        self.K = K
        self.W = W
        self.off = off
        self.scales = scales


class Ring:

    def __init__(self, kb, nc, wscr, slots, pieces, role):
        self.kb, self.nc, self.wscr = kb, nc, wscr
        self.slots = slots
        self.pieces = pieces
        self.role = role
        self.nxt = 0

    def reset(self):
        self.nxt = 0

    def need(self, i):
        while self.nxt <= min(i + len(self.slots) - 1, len(self.pieces) - 1):
            pc = self.pieces[self.nxt]
            slot = self.slots[self.nxt % len(self.slots)]
            self.kb.emit("q", lambda sl=slot, pc=pc: self.nc.sync.dma_start(
                out=sl[:, 0:pc.K * pc.W], in_=self.wscr[:, pc.off:pc.off + pc.K * pc.W]),
                r=["wscr"], w=[slot], dma=f"{self.role}{self.nxt % len(self.slots)}")
            self.nxt += 1
        return self.slots[i % len(self.slots)], self.pieces[i]


def build_program(n_chunks=NCHUNK, dbg=False):
    nc = bass.Bass("TRN2", target_bir_lowering=False, debug=False)

    x_d = nc.dram_tensor("x", [D_MODEL, SEQ], F32, kind="ExternalInput").ap()
    ipw_d = nc.dram_tensor("ipw", [D_MODEL, DIP], F32, kind="ExternalInput").ap()
    opw_d = nc.dram_tensor("opw", [D_INNER, D_MODEL], F32, kind="ExternalInput").ap()
    w1_d = nc.dram_tensor("w1", [NE_R, D_MODEL, HIDDEN], F32, kind="ExternalInput").ap()
    w2_d = nc.dram_tensor("w2", [NE_R, HIDDEN, D_MODEL], F32, kind="ExternalInput").ap()
    w1s_d = nc.dram_tensor("w1s", [2, D_MODEL, HIDDEN], F32, kind="ExternalInput").ap()
    w2s_d = nc.dram_tensor("w2s", [2, HIDDEN, D_MODEL], F32, kind="ExternalInput").ap()
    cwp_d = nc.dram_tensor("cwp", [128, 72], F32, kind="ExternalInput").ap()
    cbp_d = nc.dram_tensor("cbp", [128, 18], F32, kind="ExternalInput").ap()
    nwp_d = nc.dram_tensor("nwp", [128, 16], F32, kind="ExternalInput").ap()
    rmsp_d = nc.dram_tensor("rmsp", [128, 112], F32, kind="ExternalInput").ap()
    rmssp_d = nc.dram_tensor("rmssp", [128, 16], F32, kind="ExternalInput").ap()
    dcol_d = nc.dram_tensor("dcol", [128, 16], F32, kind="ExternalInput").ap()
    dtb_d = nc.dram_tensor("dtb", [32, 1], F32, kind="ExternalInput").ap()
    alog_d = nc.dram_tensor("alog", [32, 1], F32, kind="ExternalInput").ap()
    rwp_d = nc.dram_tensor("rwp", [128, 112], F32, kind="ExternalInput").ap()
    idf_d = nc.dram_tensor("idf", [128, 128], F32, kind="ExternalInput").ap()
    idb_d = nc.dram_tensor("idb", [128, 128], BF16, kind="ExternalInput").ap()
    mask_d = nc.dram_tensor("mask", [128, 128], F32, kind="ExternalInput").ap()
    out_d = nc.dram_tensor("out", [D_MODEL, SEQ], F32, kind="ExternalOutput").ap()

    total_cols = (11 * 3072 + 1280 + 8 * 2048) + NE * 8 * 4096
    wscr_d = nc.dram_tensor("wscr", [128, total_cols], BF16, kind="Internal").ap()
    cbounce_d = nc.dram_tensor("cbounce", [32, 128], F32, kind="Internal").ap()
    gemb_d = nc.dram_tensor("gemb", [16, TC], F32, kind="Internal").ap()

    dbg_outs = {}
    if dbg:
        for nm, shp in [("d_siluz", [128, 16 * TC]), ("d_xbc", [128, 18 * (TC + 4)]),
                        ("d_dt", [32, TC]), ("d_cc", [32, TC]), ("d_y", [128, 16 * TC]),
                        ("d_x1", [128, 8 * TC]), ("d_gem", [16, TC]),
                        ("d_he", [128, 16 * TC]), ("d_xh", [128, 8 * TC]),
                        ("d_s", [128, NH * HD])]:
            dbg_outs[nm] = nc.dram_tensor(nm, shp, F32, kind="ExternalOutput").ap()

    with ExitStack() as ctx:
        S = lambda *a: ctx.enter_context(nc.sbuf_tensor(*a))
        P = lambda *a: ctx.enter_context(nc.psum_tensor(*a))

        fring = [S(f"fr{i}", [128, FSLOT_COLS], BF16) for i in range(NSLOT_F)]
        mring = [S(f"mr{i}", [128, MSLOT_COLS], BF16) for i in range(NSLOT_M)]
        xf = [S("xf0", [128, 8, TC], F32), S("xf1", [128, 8, TC], F32)]
        x_bf = S("x_bf", [128, 8, TC], BF16)
        fb_bf = S("fb_bf", [128, 8, TC], BF16)
        xhat0 = S("xhat0", [128, 8, TC], BF16)
        siluz = S("siluz", [128, 16, TC], BF16)
        xbc = S("xbc", [128, 18, TC + 4], BF16)
        y = S("y", [128, 16, TC], BF16)
        he = S("he", [128, 16, TC], BF16)
        dt = S("dt", [32, TC], F32)
        cc = S("cc", [32, TC], F32)
        scr = S("scr", [128, 2, TC], F32)
        scrb = S("scrb", [128, 2, TC], BF16)
        arena_m = S("arena_m", [1, 2048], F32)
        arena_f = S("arena_f", [1, 2048], F32)
        gem = S("gem", [16, TC], F32)
        wg = S("wg", [128, TC], BF16)
        st = S("st", [128, NH * HD], BF16)
        m_s = S("m_s", [128, 128], BF16)
        btok = S("btok", [128, 128], BF16)
        lp2 = S("lp2", [128, 4, 128], BF16)
        dl = S("dl", [128, 128], F32)
        xdt2 = S("xdt2", [128, 4, HD], BF16)
        xdw2 = S("xdw2", [128, 4, HD], BF16)
        ytok2 = S("ytok2", [128, 4, HD], BF16)
        ytmp = S("ytmp", [128, 2, HD], BF16)
        dtT = S("dtT", [128, 32], F32)
        cT = S("cT", [128, 32], F32)
        ecT = S("ecT", [128, 32], F32)
        cqb = S("cqb", [128, 64], F32)
        xdf = S("xdf", [128, 32], F32)
        rows = S("rows", [1, 64], F32)
        w16 = S("w16", [128, 16], F32)
        gts = S("gts", [128, 14], F32)
        wtmp = S("wtmp", [128, 14], F32)
        r1 = S("r1", [128, 1], F32)
        r2 = S("r2", [128, 1], F32)
        rq = S("rq", [128, 1], F32)
        idf = S("idf_s", [128, 128], F32)
        idb = S("idb_s", [128, 128], BF16)
        mask = S("mask_s", [128, 128], F32)
        ones_row = S("ones_row", [1, 128], F32)
        ones_col = S("ones_col", [128, 1], BF16)
        zero32 = S("zero32", [32, 128], F32)
        cwp = S("cwp_s", [128, 72], F32)
        cbp = S("cbp_s", [128, 18], F32)
        nwp = S("nwp_s", [128, 16], F32)
        rmsp = S("rmsp_s", [128, 112], F32)
        rmssp = S("rmssp_s", [128, 16], F32)
        dcol = S("dcol_s", [128, 16], F32)
        dtb = S("dtb_s", [32, 1], F32)
        acol = S("acol_s", [32, 1], F32)
        rwp = S("rwp_s", [128, 8, NE_R], F32)

        pb = [P(f"pb{i}", [128, 512], F32) for i in range(6)]
        pb6 = P("pb6", [128, 1024], BF16)
        pb7 = P("pb7", [128, 1024], BF16)

        kb = KB(nc, ctx)
        E = kb.emit

        def scrF():
            return he[:].rearrange("p a b -> p (a b)")[:, 0:1024].bitcast(F32)

        mpieces = []
        fpieces = []
        cursor = [0]

        def add_piece(lst, src, K, W, scales=None):
            lst.append(Piece(src, K, W, cursor[0], scales))
            cursor[0] += K * W

        ipr = ipw_d.rearrange("(k p) f -> p k f", p=128)
        for g in range(12):
            f0 = g * 384
            f1 = min(DIP, f0 + 384)
            add_piece(mpieces, ipr[:, :, f0:f1], 8, f1 - f0)
        opr = opw_d.rearrange("(k p) o -> p k o", p=128)
        for o in range(8):
            add_piece(mpieces, opr[:, :, o * 128:(o + 1) * 128], 16, 128,
                      [nwp[:, k:k + 1] for k in range(16)])
        for e in range(NE):
            if e < NE_R:
                w1e = w1_d[e].rearrange("(k p) h -> p k h", p=128)
                w2e = w2_d[e].rearrange("(k p) o -> p k o", p=128)
                rc = lambda k, e=e: rmsp[:, e * 8 + k:e * 8 + k + 1]
            else:
                w1e = w1s_d[e - NE_R].rearrange("(k p) h -> p k h", p=128)
                w2e = w2s_d[e - NE_R].rearrange("(k p) o -> p k o", p=128)
                rc = lambda k, e=e: rmssp[:, (e - NE_R) * 8 + k:(e - NE_R) * 8 + k + 1]
            for qq in range(4):
                add_piece(fpieces, w1e[:, :, qq * 512:(qq + 1) * 512], 8, 512,
                          [rc(k) for k in range(8)])
            for qq in range(4):
                add_piece(fpieces, w2e[:, :, qq * 256:(qq + 1) * 256], 16, 256)
        assert cursor[0] == total_cols, (cursor[0], total_cols)

        for dst, src in [(cwp, cwp_d), (cbp, cbp_d), (nwp, nwp_d), (rmsp, rmsp_d),
                         (rmssp, rmssp_d), (dcol, dcol_d), (dtb, dtb_d),
                         (idf, idf_d), (idb, idb_d), (mask, mask_d)]:
            E("p", lambda d=dst, s=src: nc.gpsimd.dma_start(out=d[:], in_=s), w=[dst], dma="init")
        E("p", lambda: nc.gpsimd.dma_start(
            out=rwp[:], in_=rwp_d.rearrange("p (k e) -> p k e", k=8)), w=[rwp], dma="init")
        E("p", lambda: nc.gpsimd.dma_start(out=acol[:], in_=alog_d), w=[acol], dma="init")
        E("v", lambda: nc.vector.memset(ones_row[:], 1.0), w=[ones_row])
        E("v", lambda: nc.vector.memset(ones_col[:], 1.0), w=[ones_col])
        E("v", lambda: nc.vector.memset(zero32[:], 0.0), w=[zero32])
        E("v", lambda: nc.vector.memset(w16[:], 0.0), w=[w16])
        E("v", lambda: nc.vector.memset(st[:], 0.0), w=[st])
        E("v", lambda: nc.vector.memset(xbc[:], 0.0), w=[xbc])
        E("s", lambda: nc.scalar.activation(acol[:], acol[:], AF.Exp), r=[acol], w=[acol])
        E("v", lambda: nc.vector.tensor_scalar_mul(acol[:], acol[:], -1.0), r=[acol], w=[acol])

        def emit_convert(idx, pc):
            slot = fring[idx % NSLOT_F]
            v3 = slot[:, 0:pc.K * pc.W].rearrange("p (k w) -> p k w", k=pc.K)
            E("p", lambda v=v3, s=pc.src: nc.gpsimd.dma_start(out=v, in_=s),
              w=[slot], dma=f"cv{idx % NSLOT_F}")
            if pc.scales:
                for k in range(pc.K):
                    E("v", lambda v=v3, k=k, sc=pc.scales[k]:
                      nc.vector.tensor_scalar(v[:, k, :], v[:, k, :], sc, None, op0=ALU.mult),
                      r=[slot], w=[slot])
            E("q", lambda sl=slot, pc=pc: nc.sync.dma_start(
                out=wscr_d[:, pc.off:pc.off + pc.K * pc.W], in_=sl[:, 0:pc.K * pc.W]),
                r=[slot], w=["wscr"], dma=f"cw{idx % NSLOT_F}")
        for idx, pc in enumerate(mpieces + fpieces):
            emit_convert(idx, pc)
        kb.fence()

        RM = Ring(kb, nc, wscr_d, mring, mpieces, "ms")
        RF = Ring(kb, nc, wscr_d, fring, fpieces, "fs")

        xr = x_d.rearrange("(c p) t -> p c t", p=128)
        outr = out_d.rearrange("(c p) t -> p c t", p=128)

        def chunk_ap(base, cix):
            if isinstance(cix, int):
                return base[:, :, cix * TC:(cix + 1) * TC]
            k, sub = cix
            if sub < 2:
                return base[:, :, bass.ts(k, 2 * TC)][:, :, sub * TC:(sub + 1) * TC]
            return base[:, :, bass.ts(k + 1, 2 * TC)][:, :, 0:TC]

        def mamba_thunks(cix, par):
            T = []
            A = T.append
            xt = xf[par]
            A(lambda: E("s", lambda: nc.scalar.dma_start(out=xt[:], in_=chunk_ap(xr, cix)),
                        w=[xt], dma="x"))
            for c8 in range(8):
                A(lambda c8=c8: E("p", lambda: nc.gpsimd.tensor_copy(
                    x_bf[:, c8, :], xt[:, c8, :]), r=[xt], w=[x_bf]))
            for c0 in range(0, 18, 3):
                def halo(c0=c0):
                    for c in range(c0, min(18, c0 + 3)):
                        E("p", lambda c=c: nc.gpsimd.tensor_copy(
                            xbc[:, c, 0:4], xbc[:, c, TC:TC + 4]), r=[xbc], w=[xbc])
                A(halo)

            def ip_col(j):
                g, jl = j // 3, j % 3
                wcol = 128 if j < 34 else 32
                bank = pb[2 + j % 2]
                slot, pc = RM.need(g)
                for k in range(8):
                    base = k * pc.W + jl * 128
                    E("t", lambda b=bank, sl=slot, base=base, w=wcol, k=k:
                      nc.tensor.matmul(b[0:w, 0:TC], sl[:, base:base + w], x_bf[:, k, :],
                                       start=(k == 0), stop=(k == 7)),
                      r=[slot, x_bf], w=[bank])
                if j < 16:
                    E("s", lambda b=bank, j=j: nc.scalar.activation(
                        siluz[:, j, :], b[:, 0:TC], AF.Silu), r=[bank], w=[siluz])
                elif j < 34:
                    E("s", lambda b=bank, j=j: nc.scalar.copy(
                        xbc[:, j - 16, 4:4 + TC], b[:, 0:TC]), r=[bank], w=[xbc])
                else:
                    E("v", lambda b=bank: nc.vector.tensor_scalar(
                        scr[0:32, 0, :], b[0:32, 0:TC], dtb[:], None, op0=ALU.add),
                        r=[bank, dtb], w=["scr0"])
                    E("s", lambda: nc.scalar.activation(cc[:, :], scr[0:32, 0, :], AF.Abs),
                      r=["scr0"], w=[cc])
                    E("s", lambda: nc.scalar.activation(cc[:, :], cc[:, :], AF.Exp, scale=-1.0),
                      r=[cc], w=[cc])
                    E("s", lambda: nc.scalar.activation(cc[:, :], cc[:, :], AF.Ln, bias=1.0),
                      r=[cc], w=[cc])
                    E("v", lambda: nc.vector.tensor_scalar_max(
                        scr[0:32, 0, :], scr[0:32, 0, :], 0.0), r=["scr0"], w=["scr0"])
                    E("v", lambda: nc.vector.tensor_add(dt[:, :], scr[0:32, 0, :], cc[:, :]),
                      r=["scr0", cc], w=[dt])
            for j in range(35):
                A(lambda j=j: ip_col(j))

            def conv_c(c):
                pcy = c % 2
                E("v", lambda pcy=pcy: nc.vector.tensor_scalar(
                    scr[:, pcy, :], xbc[:, c, 1:1 + TC], cwp[:, c * 4 + 3:c * 4 + 4],
                    cbp[:, c:c + 1], op0=ALU.mult, op1=ALU.add),
                    r=[xbc, cwp, cbp], w=[f"scr{pcy}"])
                for jj in (2, 1, 0):
                    E("v", lambda jj=jj, pcy=pcy: nc.vector.scalar_tensor_tensor(
                        scr[:, pcy, :], xbc[:, c, 4 - jj:4 - jj + TC], cwp[:, c * 4 + jj:c * 4 + jj + 1],
                        scr[:, pcy, :], op0=ALU.mult, op1=ALU.add),
                        r=[xbc, cwp, f"scr{pcy}"], w=[f"scr{pcy}"])
                E("s", lambda pcy=pcy: nc.scalar.activation(
                    xbc[:, c, 4:4 + TC], scr[:, pcy, :], AF.Silu),
                  r=[f"scr{pcy}"], w=[xbc])
            for c in range(18):
                A(lambda c=c: conv_c(c))

            def dt_a():
                E("v", lambda: nc.vector.tensor_scalar(
                    scr[0:32, 0, :], dt[:, :], acol[:], None, op0=ALU.mult),
                    r=[dt, acol], w=["scr0"])
                for q in range(NQ):
                    E("v", lambda q=q: nc.vector.tensor_tensor_scan(
                        cc[:, q * 128:(q + 1) * 128], scr[0:32, 0, q * 128:(q + 1) * 128],
                        zero32[:, 0:128], 0.0, op0=ALU.add, op1=ALU.add),
                        r=["scr0", zero32], w=[cc])
            A(dt_a)

            for q in range(NQ):
                qs0 = q * 128
                xq = 4 + q * 128

                def qstart(q=q, qs0=qs0, xq=xq):
                    E("q", lambda: nc.sync.dma_start(out=cbounce_d, in_=cc[:, qs0:qs0 + 128]),
                      r=[cc], w=["cbounce"], dma="cb")
                    E("q", lambda: nc.sync.dma_start(
                        out=arena_m[0:1, :],
                        in_=cbounce_d[0:16].rearrange("h t -> (h t)")[None, :]),
                        r=["cbounce"], w=[arena_m], dma="ar")
                    E("t", lambda: nc.tensor.matmul(
                        pb[4][:, 0:128], xbc[:, 16, xq:xq + 128], xbc[:, 17, xq:xq + 128],
                        start=True, stop=True), r=[xbc], w=[pb[4]])
                    E("v", lambda: nc.vector.tensor_copy(m_s[:], pb[4][:, 0:128]),
                      r=[pb[4]], w=[m_s])
                    E("t", lambda: nc.tensor.transpose(
                        pb7[:, 0:128], xbc[:, 16, xq:xq + 128], idb[:]), r=[xbc, idb], w=[pb7])
                    E("s", lambda: nc.scalar.copy(btok[:], pb7[:, 0:128]), r=[pb7], w=[btok])
                    E("t", lambda: nc.tensor.transpose(
                        pb[4][:, 128:160], dt[:, qs0:qs0 + 128], idf[0:32, 0:32]),
                        r=[dt, idf], w=[pb[4]])
                    E("v", lambda: nc.vector.tensor_copy(dtT[:], pb[4][:, 128:160]),
                      r=[pb[4]], w=[dtT])
                    E("t", lambda: nc.tensor.transpose(
                        pb[4][:, 160:192], cc[:, qs0:qs0 + 128], idf[0:32, 0:32]),
                        r=[cc, idf], w=[pb[4]])
                    E("v", lambda: nc.vector.tensor_copy(cT[:], pb[4][:, 160:192]),
                      r=[pb[4]], w=[cT])
                    E("s", lambda: nc.scalar.activation(ecT[:], pb[4][:, 160:192], AF.Exp),
                      r=[pb[4]], w=[ecT])
                    E("t", lambda: nc.tensor.transpose(
                        pb[4][0:1, 192:224], cc[:, qs0 + 127:qs0 + 128], idf[0:32, 0:32]),
                        r=[cc, idf], w=[pb[4]])
                    E("s", lambda: nc.scalar.copy(rows[0:1, 0:32], pb[4][0:1, 192:224]),
                      r=[pb[4]], w=[rows])
                    E("s", lambda: nc.scalar.activation(rows[0:1, 32:64], pb[4][0:1, 192:224], AF.Exp),
                      r=[pb[4]], w=[rows])
                    E("t", lambda: nc.tensor.matmul(pb[4][:, 224:288], ones_row[:],
                                                    rows[0:1, 0:64], start=True, stop=True),
                      r=[ones_row, rows], w=[pb[4]])
                    E("v", lambda: nc.vector.tensor_copy(cqb[:], pb[4][:, 224:288]),
                      r=[pb[4]], w=[cqb])
                    E("v", lambda: nc.vector.tensor_sub(xdf[:], cqb[:, 0:32], cT[:]),
                      r=[cqb, cT], w=[xdf])
                    E("s", lambda: nc.scalar.activation(xdf[:], xdf[:], AF.Exp), r=[xdf], w=[xdf])
                A(qstart)

                def arena2(q=q):
                    E("q", lambda: nc.sync.dma_start(
                        out=arena_m[0:1, :],
                        in_=cbounce_d[16:32].rearrange("h t -> (h t)")[None, :]),
                        r=["cbounce"], w=[arena_m], dma="ar")

                def tCB(g4):
                    off = ((g4 * 4) % 16) * 128
                    E("t", lambda: nc.tensor.matmul(
                        pb[5][:, 0:512], ones_row[:], arena_m[0:1, off:off + 512],
                        start=True, stop=True), r=[ones_row, arena_m], w=[pb[5]])

                def tA(h, xq=xq):
                    hp = h % 4
                    hc, ho = h // 2, (h % 2) * 64
                    E("v", lambda: nc.vector.scalar_tensor_tensor(
                        dl[:], pb[5][:, (h % 4) * 128:(h % 4 + 1) * 128], cT[:, h:h + 1], mask[:],
                        op0=ALU.subtract, op1=ALU.add), r=[pb[5], cT, mask], w=[dl])
                    E("s", lambda: nc.scalar.activation(lp2[:, hp, :], dl[:], AF.Exp),
                      r=[dl], w=[lp2])
                    E("v", lambda: nc.vector.tensor_mul(lp2[:, hp, :], lp2[:, hp, :], m_s[:]),
                      r=[lp2, m_s], w=[lp2])
                    E("t", lambda: nc.tensor.transpose(
                        pb6[:, 0:64], xbc[ho:ho + 64, hc, xq:xq + 128],
                        idb[ho:ho + 64, ho:ho + 64]), r=[xbc, idb], w=[pb6])
                    E("s", lambda: nc.scalar.mul(xdt2[:, hp, :], pb6[:, 0:64], dtT[:, h:h + 1]),
                      r=[pb6, dtT], w=[xdt2])
                    E("v", lambda: nc.vector.tensor_scalar(
                        xdw2[:, hp, :], xdt2[:, hp, :], xdf[:, h:h + 1], None, op0=ALU.mult),
                        r=[xdt2, xdf], w=[xdw2])

                def tB(h, xq=xq):
                    hp = h % 4
                    E("t", lambda: nc.tensor.matmul(pb[2][:, 0:64], lp2[:, hp, :],
                                                    xdt2[:, hp, :], start=True, stop=True),
                      r=[lp2, xdt2], w=[pb[2]])
                    E("t", lambda: nc.tensor.matmul(
                        pb[3][:, 0:64], xbc[:, 17, xq:xq + 128], st[:, h * 64:(h + 1) * 64],
                        start=True, stop=True), r=[xbc, st], w=[pb[3]])
                    E("s", lambda: nc.scalar.mul(ytmp[:, h % 2, :], pb[3][:, 0:64], ecT[:, h:h + 1]),
                      r=[pb[3], ecT], w=[ytmp])
                    E("v", lambda: nc.vector.tensor_add(
                        ytok2[:, hp, :], pb[2][:, 0:64], ytmp[:, h % 2, :]),
                      r=[pb[2], ytmp], w=[ytok2])
                    E("t", lambda: nc.tensor.matmul(pb[4][:, 0:64], btok[:], xdw2[:, hp, :],
                                                    start=True, stop=True),
                      r=[btok, xdw2], w=[pb[4]])
                    E("v", lambda: nc.vector.scalar_tensor_tensor(
                        st[:, h * 64:(h + 1) * 64], st[:, h * 64:(h + 1) * 64],
                        cqb[:, 32 + h:33 + h], pb[4][:, 0:64], op0=ALU.mult, op1=ALU.add),
                        r=[st, cqb, pb[4]], w=[st])

                def tC(h, qs0=qs0, xq=xq):
                    hp = h % 4
                    hc, ho = h // 2, (h % 2) * 64
                    E("t", lambda: nc.tensor.transpose(pb7[0:64, 128:256], ytok2[:, hp, :], idb[:]),
                      r=[ytok2, idb], w=[pb7])
                    E("v", lambda: nc.vector.scalar_tensor_tensor(
                        y[ho:ho + 64, hc, qs0:qs0 + 128], xbc[ho:ho + 64, hc, xq:xq + 128],
                        dcol[ho:ho + 64, hc:hc + 1], pb7[0:64, 128:256],
                        op0=ALU.mult, op1=ALU.add), r=[xbc, dcol, pb7], w=[y])

                for h in range(NH + 5):
                    if h < NH:
                        if h % 4 == 0:
                            A(lambda g4=h // 4, tCB=tCB: tCB(g4))
                        A(lambda h=h, tA=tA: tA(h))
                    if h == 15:
                        A(arena2)
                    if 3 <= h < NH + 3:
                        A(lambda h=h, tB=tB: tB(h - 3))
                    if h >= 5:
                        A(lambda h=h, tC=tC: tC(h - 5))

            def gat_c(c):
                E("v", lambda: nc.vector.tensor_mul(y[:, c, :], y[:, c, :], siluz[:, c, :]),
                  r=[y, siluz], w=[y])
                E("s", lambda: nc.scalar.activation(scrb[:, c % 2, :], y[:, c, :], AF.Square),
                  r=[y], w=[scrb])
                E("t", lambda: nc.tensor.matmul(
                    pb[5][0:1, 0:TC], ones_col[:], scrb[:, c % 2, :],
                    start=(c == 0), stop=(c == 15)), r=[ones_col, scrb], w=[pb[5]])
            for c in range(16):
                A(lambda c=c: gat_c(c))

            def grms():
                E("v", lambda: nc.vector.tensor_scalar(
                    arena_m[0:1, 0:TC], pb[5][0:1, 0:TC], 1.0 / D_INNER, MRMS_EPS,
                    op0=ALU.mult, op1=ALU.add), r=[pb[5]], w=[arena_m])
                E("s", lambda: nc.scalar.activation(arena_m[0:1, 0:TC], arena_m[0:1, 0:TC], AF.Ln),
                  r=[arena_m], w=[arena_m])
                E("v", lambda: nc.vector.tensor_scalar_mul(
                    arena_m[0:1, 0:TC], arena_m[0:1, 0:TC], -0.5), r=[arena_m], w=[arena_m])
                E("s", lambda: nc.scalar.activation(arena_m[0:1, 0:TC], arena_m[0:1, 0:TC], AF.Exp),
                  r=[arena_m], w=[arena_m])
                E("t", lambda: nc.tensor.matmul(pb[5][:, 0:TC], ones_row[:], arena_m[0:1, 0:TC],
                                                start=True, stop=True),
                  r=[ones_row, arena_m], w=[pb[5]])
            A(grms)
            for c0 in range(0, 16, 4):
                def gapply(c0=c0):
                    for c in range(c0, c0 + 4):
                        E("v", lambda c=c: nc.vector.tensor_mul(
                            y[:, c, :], y[:, c, :], pb[5][:, 0:TC]), r=[y, pb[5]], w=[y])
                A(gapply)

            def op_col(o):
                slot, pc = RM.need(12 + o)
                bank = pb[2 + o % 2]
                for k in range(16):
                    base = k * pc.W
                    E("t", lambda b=bank, sl=slot, base=base, k=k:
                      nc.tensor.matmul(b[:, 0:TC], sl[:, base:base + 128], y[:, k, :],
                                       start=(k == 0), stop=(k == 15)),
                      r=[slot, y], w=[bank])
                E("v", lambda b=bank, o=o: nc.vector.tensor_add(
                    xt[:, o, :], xt[:, o, :], b[:, 0:TC]), r=[bank, xt], w=[xt])
            for o in range(8):
                A(lambda o=o: op_col(o))

            def ln1_c(c):
                E("s", lambda: nc.scalar.copy(x_bf[:, c, :], xt[:, c, :]), r=[xt], w=[x_bf])
                E("t", lambda: nc.tensor.matmul(
                    pb[4][0:1, 0:TC], ones_col[:], x_bf[:, c, :],
                    start=(c == 0), stop=(c == 7)), r=[ones_col, x_bf], w=[pb[4]])
                E("s", lambda: nc.scalar.activation(scrb[:, c % 2, :], x_bf[:, c, :], AF.Square),
                  r=[x_bf], w=[scrb])
                E("t", lambda: nc.tensor.matmul(
                    pb[5][0:1, 0:TC], ones_col[:], scrb[:, c % 2, :],
                    start=(c == 0), stop=(c == 7)), r=[ones_col, scrb], w=[pb[5]])
            for c in range(8):
                A(lambda c=c: ln1_c(c))

            def ln1_rows():
                E("v", lambda: nc.vector.tensor_scalar(
                    arena_m[0:1, 0:TC], pb[4][0:1, 0:TC], 1.0 / D_MODEL, None, op0=ALU.mult),
                    r=[pb[4]], w=[arena_m])
                E("v", lambda: nc.vector.tensor_scalar(
                    arena_m[0:1, TC:2 * TC], pb[5][0:1, 0:TC], 1.0 / D_MODEL, LN_EPS,
                    op0=ALU.mult, op1=ALU.add), r=[pb[5]], w=[arena_m])
                E("s", lambda: nc.scalar.activation(
                    scr[0:1, 0, 0:TC], arena_m[0:1, 0:TC], AF.Square), r=[arena_m], w=["scr0"])
                E("v", lambda: nc.vector.tensor_sub(
                    arena_m[0:1, TC:2 * TC], arena_m[0:1, TC:2 * TC], scr[0:1, 0, 0:TC]),
                    r=[arena_m, "scr0"], w=[arena_m])
                E("s", lambda: nc.scalar.activation(
                    arena_m[0:1, TC:2 * TC], arena_m[0:1, TC:2 * TC], AF.Ln),
                    r=[arena_m], w=[arena_m])
                E("v", lambda: nc.vector.tensor_scalar_mul(
                    arena_m[0:1, TC:2 * TC], arena_m[0:1, TC:2 * TC], -0.5),
                    r=[arena_m], w=[arena_m])
                E("s", lambda: nc.scalar.activation(
                    arena_m[0:1, TC:2 * TC], arena_m[0:1, TC:2 * TC], AF.Exp),
                    r=[arena_m], w=[arena_m])
                E("v", lambda: nc.vector.tensor_mul(
                    arena_m[0:1, 0:TC], arena_m[0:1, 0:TC], arena_m[0:1, TC:2 * TC]),
                    r=[arena_m], w=[arena_m])
                E("t", lambda: nc.tensor.matmul(pb[2][:, 0:TC], ones_row[:],
                                                arena_m[0:1, TC:2 * TC], start=True, stop=True),
                  r=[ones_row, arena_m], w=[pb[2]])
                E("t", lambda: nc.tensor.matmul(pb[3][:, 0:TC], ones_row[:],
                                                arena_m[0:1, 0:TC], start=True, stop=True),
                  r=[ones_row, arena_m], w=[pb[3]])
            A(ln1_rows)
            for c0 in range(0, 8, 2):
                def ln1_apply(c0=c0):
                    for c in range(c0, c0 + 2):
                        E("v", lambda c=c: nc.vector.tensor_mul(
                            scr[:, c % 2, :], xt[:, c, :], pb[2][:, 0:TC]),
                            r=[xt, pb[2]], w=[f"scr{c % 2}"])
                        E("v", lambda c=c: nc.vector.tensor_sub(
                            xt[:, c, :], scr[:, c % 2, :], pb[3][:, 0:TC]),
                            r=[f"scr{c % 2}", pb[3]], w=[xt])
                A(ln1_apply)
            return T

        def emit_ffn(cix, par, pump):
            if pump is None:
                pump = lambda: None
            xt = xf[par]
            for g4 in range(4):
                for k in range(8):
                    E("t", lambda g4=g4, k=k: nc.tensor.matmul(
                        pb[0][:, g4 * 14:(g4 + 1) * 14], xt[:, k, g4 * 128:(g4 + 1) * 128],
                        rwp[:, k, :], start=(k == 0), stop=(k == 7)),
                        r=[xt, rwp], w=[pb[0]])
                pump()
            for g4 in range(4):
                E("v", lambda g4=g4: nc.vector.reduce_max(
                    r1[:], pb[0][:, g4 * 14:(g4 + 1) * 14], axis=AX.X, negate=True),
                    r=[pb[0]], w=[r1])
                E("s", lambda g4=g4: nc.scalar.activation(
                    gts[:], pb[0][:, g4 * 14:(g4 + 1) * 14], AF.Exp, bias=r1[:]),
                    r=[pb[0], r1], w=[gts])
                E("v", lambda: nc.vector.reduce_sum(rq[:], gts[:], axis=AX.X), r=[gts], w=[rq])
                E("v", lambda: nc.vector.reciprocal(rq[:], rq[:]), r=[rq], w=[rq])
                E("v", lambda: nc.vector.tensor_scalar(gts[:], gts[:], rq[:], None, op0=ALU.mult),
                  r=[gts, rq], w=[gts])
                E("v", lambda: nc.vector.reduce_max(r1[:], gts[:], axis=AX.X), r=[gts], w=[r1])
                E("v", lambda: nc.vector.tensor_scalar(wtmp[:], gts[:], r1[:], None, op0=ALU.is_ge),
                  r=[gts, r1], w=[wtmp])
                E("v", lambda: nc.vector.scalar_tensor_tensor(
                    wtmp[:], wtmp[:], -1e30, gts[:], op0=ALU.mult, op1=ALU.add),
                    r=[wtmp, gts], w=[wtmp])
                E("v", lambda: nc.vector.reduce_max(r2[:], wtmp[:], axis=AX.X), r=[wtmp], w=[r2])
                E("v", lambda: nc.vector.tensor_scalar(wtmp[:], gts[:], r2[:], None, op0=ALU.is_ge),
                  r=[gts, r2], w=[wtmp])
                E("v", lambda: nc.vector.tensor_mul(wtmp[:], gts[:], wtmp[:]),
                  r=[gts, wtmp], w=[wtmp])
                E("v", lambda: nc.vector.reduce_sum(rq[:], wtmp[:], axis=AX.X), r=[wtmp], w=[rq])
                E("v", lambda: nc.vector.tensor_scalar_add(rq[:], rq[:], GATE_EPS), r=[rq], w=[rq])
                E("v", lambda: nc.vector.reciprocal(rq[:], rq[:]), r=[rq], w=[rq])
                E("v", lambda: nc.vector.tensor_scalar(
                    w16[:, 0:14], wtmp[:], rq[:], None, op0=ALU.mult), r=[wtmp, rq], w=[w16])
                E("t", lambda: nc.tensor.transpose(pb[1][0:16, 0:128], w16[:], idf[:]),
                  r=[w16, idf], w=[pb[1]])
                E("v", lambda g4=g4: nc.vector.tensor_copy(
                    gem[:, g4 * 128:(g4 + 1) * 128], pb[1][0:16, 0:128]), r=[pb[1]], w=[gem])
                pump()
            E("q", lambda: nc.sync.dma_start(out=gemb_d, in_=gem[:]),
              r=[gem], w=["gemb"], dma="gemb")
            for c in range(8):
                E("s", lambda c=c: nc.scalar.activation(
                    fb_bf[:, c % 2, :], xt[:, c, :], AF.Square), r=[xt], w=[fb_bf])
                E("t", lambda c=c: nc.tensor.matmul(
                    pb[1][0:1, 0:TC], ones_col[:], fb_bf[:, c % 2, :],
                    start=(c == 0), stop=(c == 7)), r=[ones_col, fb_bf], w=[pb[1]])
            E("v", lambda: nc.vector.tensor_scalar(
                arena_f[0:1, 0:TC], pb[1][0:1, 0:TC], 1.0 / D_MODEL, FRMS_EPS,
                op0=ALU.mult, op1=ALU.add), r=[pb[1]], w=[arena_f])
            E("s", lambda: nc.scalar.activation(arena_f[0:1, 0:TC], arena_f[0:1, 0:TC], AF.Ln),
              r=[arena_f], w=[arena_f])
            E("v", lambda: nc.vector.tensor_scalar_mul(
                arena_f[0:1, 0:TC], arena_f[0:1, 0:TC], -0.5), r=[arena_f], w=[arena_f])
            E("s", lambda: nc.scalar.activation(arena_f[0:1, 0:TC], arena_f[0:1, 0:TC], AF.Exp),
              r=[arena_f], w=[arena_f])
            E("t", lambda: nc.tensor.matmul(pb[0][:, 0:TC], ones_row[:], arena_f[0:1, 0:TC],
                                            start=True, stop=True),
              r=[ones_row, arena_f], w=[pb[0]])
            pump()
            for c in range(8):
                E("v", lambda c=c: nc.vector.tensor_mul(
                    xhat0[:, c, :], xt[:, c, :], pb[0][:, 0:TC]), r=[xt, pb[0]], w=[xhat0])
            pump()

            for e in range(NE):
                if e < NE_R and e % 2 == 0:
                    E("q", lambda e=e: nc.sync.dma_start(
                        out=arena_f[0:1, 0:2 * TC],
                        in_=gemb_d[e:e + 2, :].rearrange("a t -> (a t)")[None, :]),
                        r=["gemb"], w=[arena_f], dma="ar2")
                if e < NE_R:
                    E("t", lambda e=e: nc.tensor.matmul(
                        pb[1][:, 0:TC], ones_row[:],
                        arena_f[0:1, (e % 2) * TC:(e % 2 + 1) * TC],
                        start=True, stop=True), r=[ones_row, arena_f], w=[pb[1]])
                    E("s", lambda: nc.scalar.copy(wg[:], pb[1][:, 0:TC]), r=[pb[1]], w=[wg])
                p1a = 8 * e
                for hcol in range(16):
                    slot, pc = RF.need(p1a + hcol // 4)
                    hl = hcol % 4
                    bank = pb[hcol % 2]
                    for k in range(8):
                        base = k * pc.W + hl * 128
                        E("t", lambda b=bank, sl=slot, base=base, k=k:
                          nc.tensor.matmul(b[:, 0:TC], sl[:, base:base + 128],
                                           xhat0[:, k, :], start=(k == 0), stop=(k == 7)),
                          r=[slot, xhat0], w=[bank])
                    E("s", lambda b=bank, hcol=hcol: nc.scalar.activation(
                        he[:, hcol, :], b[:, 0:TC], AF.Silu), r=[bank], w=[f"he{hcol}"])
                    if e < NE_R:
                        E("v", lambda hcol=hcol: nc.vector.tensor_mul(
                            he[:, hcol, :], he[:, hcol, :], wg[:]),
                            r=[f"he{hcol}", wg], w=[f"he{hcol}"])
                    pump()
                p2a = 8 * e + 4
                for o in range(8):
                    slot, pc = RF.need(p2a + o // 2)
                    ol = o % 2
                    bank = pb[o % 2]
                    for k in range(16):
                        base = k * pc.W + ol * 128
                        E("t", lambda b=bank, sl=slot, base=base, k=k:
                          nc.tensor.matmul(b[:, 0:TC], sl[:, base:base + 128],
                                           he[:, k, :], start=(k == 0), stop=(k == 15)),
                          r=[slot, f"he{k}"], w=[bank])
                    E("v", lambda b=bank, o=o: nc.vector.tensor_add(
                        xt[:, o, :], xt[:, o, :], b[:, 0:TC]), r=[bank, xt], w=[xt])
                    pump()

            for c in range(8):
                E("s", lambda c=c: nc.scalar.copy(fb_bf[:, c, :], xt[:, c, :]), r=[xt], w=[fb_bf])
                E("t", lambda c=c: nc.tensor.matmul(
                    pb[0][0:1, 0:TC], ones_col[:], fb_bf[:, c, :],
                    start=(c == 0), stop=(c == 7)), r=[ones_col, fb_bf], w=[pb[0]])
                E("s", lambda c=c: nc.scalar.activation(
                    xhat0[:, c % 2, :], fb_bf[:, c, :], AF.Square), r=[fb_bf], w=[xhat0])
                E("t", lambda c=c: nc.tensor.matmul(
                    pb[1][0:1, 0:TC], ones_col[:], xhat0[:, c % 2, :],
                    start=(c == 0), stop=(c == 7)), r=[ones_col, xhat0], w=[pb[1]])
            pump()
            E("v", lambda: nc.vector.tensor_scalar(
                arena_f[0:1, 0:TC], pb[0][0:1, 0:TC], 1.0 / D_MODEL, None, op0=ALU.mult),
                r=[pb[0]], w=[arena_f])
            E("v", lambda: nc.vector.tensor_scalar(
                arena_f[0:1, TC:2 * TC], pb[1][0:1, 0:TC], 1.0 / D_MODEL, LN_EPS,
                op0=ALU.mult, op1=ALU.add), r=[pb[1]], w=[arena_f])
            E("s", lambda: nc.scalar.activation(
                scrF()[0:1, 0:TC], arena_f[0:1, 0:TC], AF.Square),
                r=[arena_f, "he0", "he1"], w=["he0", "he1"])
            E("v", lambda: nc.vector.tensor_sub(
                arena_f[0:1, TC:2 * TC], arena_f[0:1, TC:2 * TC], scrF()[0:1, 0:TC]),
                r=[arena_f, "he0"], w=[arena_f])
            E("s", lambda: nc.scalar.activation(
                arena_f[0:1, TC:2 * TC], arena_f[0:1, TC:2 * TC], AF.Ln), r=[arena_f], w=[arena_f])
            E("v", lambda: nc.vector.tensor_scalar_mul(
                arena_f[0:1, TC:2 * TC], arena_f[0:1, TC:2 * TC], -0.5), r=[arena_f], w=[arena_f])
            E("s", lambda: nc.scalar.activation(
                arena_f[0:1, TC:2 * TC], arena_f[0:1, TC:2 * TC], AF.Exp), r=[arena_f], w=[arena_f])
            E("v", lambda: nc.vector.tensor_mul(
                arena_f[0:1, 0:TC], arena_f[0:1, 0:TC], arena_f[0:1, TC:2 * TC]),
                r=[arena_f], w=[arena_f])
            E("t", lambda: nc.tensor.matmul(pb[0][:, 0:TC], ones_row[:], arena_f[0:1, TC:2 * TC],
                                            start=True, stop=True), r=[ones_row, arena_f], w=[pb[0]])
            E("t", lambda: nc.tensor.matmul(pb[1][:, 0:TC], ones_row[:], arena_f[0:1, 0:TC],
                                            start=True, stop=True), r=[ones_row, arena_f], w=[pb[1]])
            for c in range(8):
                E("v", lambda c=c: nc.vector.tensor_mul(
                    scrF()[:, 0:TC], xt[:, c, :], pb[0][:, 0:TC]),
                    r=[xt, pb[0], "he0", "he1"], w=["he0", "he1"])
                E("v", lambda c=c: nc.vector.tensor_sub(
                    xt[:, c, :], scrF()[:, 0:TC], pb[1][:, 0:TC]),
                    r=["he0", pb[1]], w=[xt])
            pump()
            E("s", lambda: nc.scalar.dma_start(out=chunk_ap(outr, cix), in_=xt[:]),
              r=[xt], dma="out")

        def run_pipelined(f_cix, f_par, m_cix, m_par):
            RM.reset()
            RF.reset()
            thunks = mamba_thunks(m_cix, m_par) if m_cix is not None else []
            pos = [0.0]
            idx = [0]
            per = (len(thunks) / 420.0) if (thunks and f_cix is not None) else 0.0

            def pump():
                pos[0] += per
                while idx[0] < len(thunks) and idx[0] < pos[0]:
                    thunks[idx[0]]()
                    idx[0] += 1
            if f_cix is not None:
                emit_ffn(f_cix, f_par, pump if thunks else None)
            while idx[0] < len(thunks):
                thunks[idx[0]]()
                idx[0] += 1
            kb.fence()

        if n_chunks == NCHUNK:
            run_pipelined(None, 0, 0, 0)
            with nc.Fori(0, 7) as k:
                run_pipelined((k, 0), 0, (k, 1), 1)
                run_pipelined((k, 1), 1, (k, 2), 0)
            run_pipelined(14, 0, 15, 1)
            run_pipelined(15, 1, None, 0)
        else:
            run_pipelined(None, 0, 0, 0)
            for c in range(n_chunks):
                run_pipelined(c, c % 2, c + 1 if c + 1 < n_chunks else None, (c + 1) % 2)

        if dbg:
            last_par = (n_chunks - 1) % 2
            dumps = [("d_siluz", siluz[:].rearrange("p a b -> p (a b)")),
                     ("d_xbc", xbc[:].rearrange("p a b -> p (a b)")),
                     ("d_dt", dt[:]), ("d_cc", cc[:]),
                     ("d_y", y[:].rearrange("p a b -> p (a b)")),
                     ("d_x1", xf[last_par][:].rearrange("p a b -> p (a b)")),
                     ("d_gem", gem[:]),
                     ("d_he", he[:].rearrange("p a b -> p (a b)")),
                     ("d_xh", xhat0[:].rearrange("p a b -> p (a b)")),
                     ("d_s", st[:])]
            for nm, ap in dumps:
                E("p", lambda nm=nm, ap=ap: nc.gpsimd.dma_start(
                    out=dbg_outs[nm], in_=ap), r=[], dma="dbg")
            kb.fence()

    return nc


def _host_packs(inputs):
    f = np.float32
    conv_w = np.asarray(inputs["L0_attn_conv_w"], f)
    conv_b = np.asarray(inputs["L0_attn_conv_b"], f)
    return {
        "ipw": np.ascontiguousarray(np.asarray(inputs["L0_attn_in_proj_w"], f)),
        "opw": np.ascontiguousarray(np.asarray(inputs["L0_attn_out_proj_w"], f)),
        "w1": np.ascontiguousarray(np.asarray(inputs["L0_ffn_w1"], f)),
        "w2": np.ascontiguousarray(np.asarray(inputs["L0_ffn_w2"], f)),
        "w1s": np.ascontiguousarray(np.asarray(inputs["L0_ffn_w1_s"], f)),
        "w2s": np.ascontiguousarray(np.asarray(inputs["L0_ffn_w2_s"], f)),
        "cwp": np.ascontiguousarray(conv_w.reshape(4, 18, 128).transpose(2, 1, 0).reshape(128, 72)),
        "cbp": np.ascontiguousarray(conv_b.reshape(18, 128).T),
        "nwp": np.ascontiguousarray(np.asarray(inputs["L0_attn_norm_w"], f).reshape(16, 128).T),
        "rmsp": np.ascontiguousarray(
            np.asarray(inputs["L0_ffn_rms_w"], f).reshape(14, 8, 128).transpose(2, 0, 1).reshape(128, 112)),
        "rmssp": np.ascontiguousarray(
            np.asarray(inputs["L0_ffn_rms_w_s"], f).reshape(2, 8, 128).transpose(2, 0, 1).reshape(128, 16)),
        "dcol": np.ascontiguousarray(
            np.repeat(np.asarray(inputs["L0_attn_D"], f), 64).reshape(16, 128).T),
        "dtb": np.ascontiguousarray(np.asarray(inputs["L0_attn_dt_bias"], f).reshape(32, 1)),
        "alog": np.ascontiguousarray(np.asarray(inputs["L0_attn_A_log"], f).reshape(32, 1)),
        "rwp": np.ascontiguousarray(
            np.asarray(inputs["L0_ffn_router_w"], f).reshape(8, 128, 14).transpose(1, 0, 2).reshape(128, 112)),
        "idf": np.eye(128, dtype=f),
        "idb": np.eye(128, dtype=ml_dtypes.bfloat16),
        "mask": np.where(np.arange(128)[None, :] >= np.arange(128)[:, None],
                         np.float32(0.0), np.float32(MASK_NEG)),
    }


def kernel(**inputs):
    x = np.asarray(inputs["L0_attn_x"], np.float32)
    packs = _host_packs(inputs)
    nc = build_program(NCHUNK, dbg=False)
    in_maps = []
    for b in range(8):
        m = dict(packs)
        m["x"] = np.ascontiguousarray(x[b].T)
        in_maps.append(m)
    res = run_bass_kernel_spmd(nc, in_maps, core_ids=list(range(8)))
    return np.stack([np.ascontiguousarray(r["out"].T) for r in res.results], axis=0)


# version 11
# speedup vs baseline: 1.1892x; 1.1892x over previous
import numpy as np
import ml_dtypes
from contextlib import ExitStack

import concourse.bass as bass
import concourse.mybir as mybir
from concourse.bass_utils import run_bass_kernel_spmd

F32 = mybir.dt.float32
BF16 = mybir.dt.bfloat16
AF = mybir.ActivationFunctionType
ALU = mybir.AluOpType
AX = mybir.AxisListType

D_MODEL = 1024
D_INNER = 2048
NH = 32
HD = 64
DIP = 4384
HIDDEN = 2048
NE_R = 14
NE = 16
TC = 512
NQ = TC // 128
SEQ = 8192
NCHUNK = SEQ // TC

LN_EPS = 1e-5
MRMS_EPS = 1e-5
FRMS_EPS = 1.1920929e-07
GATE_EPS = 1e-9
MASK_NEG = -30000.0

NSLOT_F = 3
FSLOT_COLS = 4096
NSLOT_M = 2
MSLOT_COLS = 3072


class KB:

    COMPUTE = ("t", "v", "s", "p")

    def __init__(self, nc, ctx):
        self.nc = nc
        self.ctx = ctx
        self.eng = {"t": nc.tensor, "v": nc.vector, "s": nc.scalar,
                    "p": nc.gpsimd, "q": nc.sync}
        self.stream = "F"
        self.chain = {(st, e): [ctx.enter_context(nc.semaphore(f"ch_{st}{e}")), 0]
                      for st in ("F", "M") for e in self.COMPUTE}
        self.dsem = {}
        self.latest = {(st, e): None for st in ("F", "M") for e in self.COMPUTE}
        self.last_w = {}
        self.readers = {}
        self.waited = {e: {} for e in self.eng}

    @staticmethod
    def _k(h):
        return h if isinstance(h, str) else h.name

    def _edge(self, rec, ceng):
        if rec is None:
            return
        peng = rec["eng"]
        if rec["dma"]:
            sem, val = rec["sem"], rec["val"]
        else:
            if peng == "t" and ceng == "t" and rec["st"] == self.stream:
                return
            key = (rec["st"], peng)
            tgt = rec if rec["sem"] is not None else self.latest[key]
            if tgt["sem"] is None:
                cs = self.chain[key]
                cs[1] += 1
                tgt["ins"].then_inc(cs[0], 1)
                tgt["sem"], tgt["val"] = cs[0], cs[1]
            sem, val = tgt["sem"], tgt["val"]
        w = self.waited[ceng]
        if w.get(sem, 0) < val:
            self.eng[ceng].wait_ge(sem, val)
            w[sem] = val

    def emit(self, ceng, fn, r=(), w=(), dma=False):
        for h in r:
            self._edge(self.last_w.get(self._k(h)), ceng)
        for h in w:
            k = self._k(h)
            self._edge(self.last_w.get(k), ceng)
            for rec in self.readers.get(k, {}).values():
                self._edge(rec, ceng)
        ins = fn()
        rec = {"eng": ceng, "ins": ins, "sem": None, "val": None, "dma": bool(dma),
               "st": self.stream}
        if dma:
            if dma not in self.dsem:
                self.dsem[dma] = [self.ctx.enter_context(self.nc.semaphore(f"ds_{dma}")), 0]
            ds = self.dsem[dma]
            ds[1] += 16
            ins.then_inc(ds[0], 16)
            rec["sem"], rec["val"] = ds[0], ds[1]
        else:
            self.latest[(self.stream, ceng)] = rec
        for h in w:
            k = self._k(h)
            self.last_w[k] = rec
            self.readers[k] = {}
        for h in r:
            self.readers.setdefault(self._k(h), {})[ceng] = rec
        return ins

    def fence(self):
        nc = self.nc
        for role, (sem, cnt) in self.dsem.items():
            if cnt:
                nc.sync.wait_ge(sem, cnt)
        for key, (sem, cnt) in self.chain.items():
            if cnt:
                self.eng[key[1]].wait_ge(sem, cnt)
        nc.all_engine_barrier()
        for key in self.chain:
            nc.sync.sem_clear(self.chain[key][0])
        for role, (sem, cnt) in self.dsem.items():
            nc.sync.sem_clear(sem)
        nc.all_engine_barrier()
        for key in self.chain:
            self.chain[key][1] = 0
            self.latest[key] = None
        for role in self.dsem:
            self.dsem[role][1] = 0
        self.last_w = {}
        self.readers = {}
        self.waited = {e: {} for e in self.eng}


class Piece:
    def __init__(self, src, K, W, off, scales=None):
        self.src = src
        self.K = K
        self.W = W
        self.off = off
        self.scales = scales


class Ring:

    def __init__(self, kb, nc, wscr, slots, pieces, role):
        self.kb, self.nc, self.wscr = kb, nc, wscr
        self.slots = slots
        self.pieces = pieces
        self.role = role
        self.nxt = 0

    def reset(self):
        self.nxt = 0

    def need(self, i):
        while self.nxt <= min(i + len(self.slots) - 1, len(self.pieces) - 1):
            pc = self.pieces[self.nxt]
            slot = self.slots[self.nxt % len(self.slots)]
            self.kb.emit("q", lambda sl=slot, pc=pc: self.nc.sync.dma_start(
                out=sl[:, 0:pc.K * pc.W], in_=self.wscr[:, pc.off:pc.off + pc.K * pc.W]),
                r=["wscr"], w=[slot], dma=f"{self.role}{self.nxt % len(self.slots)}")
            self.nxt += 1
        return self.slots[i % len(self.slots)], self.pieces[i]


def build_program(n_chunks=NCHUNK, dbg=False):
    nc = bass.Bass("TRN2", target_bir_lowering=False, debug=False)

    x_d = nc.dram_tensor("x", [D_MODEL, SEQ], F32, kind="ExternalInput").ap()
    ipw_d = nc.dram_tensor("ipw", [D_MODEL, DIP], F32, kind="ExternalInput").ap()
    opw_d = nc.dram_tensor("opw", [D_INNER, D_MODEL], F32, kind="ExternalInput").ap()
    w1_d = nc.dram_tensor("w1", [NE_R, D_MODEL, HIDDEN], F32, kind="ExternalInput").ap()
    w2_d = nc.dram_tensor("w2", [NE_R, HIDDEN, D_MODEL], F32, kind="ExternalInput").ap()
    w1s_d = nc.dram_tensor("w1s", [2, D_MODEL, HIDDEN], F32, kind="ExternalInput").ap()
    w2s_d = nc.dram_tensor("w2s", [2, HIDDEN, D_MODEL], F32, kind="ExternalInput").ap()
    cwp_d = nc.dram_tensor("cwp", [128, 72], F32, kind="ExternalInput").ap()
    cbp_d = nc.dram_tensor("cbp", [128, 18], F32, kind="ExternalInput").ap()
    nwp_d = nc.dram_tensor("nwp", [128, 16], F32, kind="ExternalInput").ap()
    rmsp_d = nc.dram_tensor("rmsp", [128, 112], F32, kind="ExternalInput").ap()
    rmssp_d = nc.dram_tensor("rmssp", [128, 16], F32, kind="ExternalInput").ap()
    dcol_d = nc.dram_tensor("dcol", [128, 16], F32, kind="ExternalInput").ap()
    dtb_d = nc.dram_tensor("dtb", [32, 1], F32, kind="ExternalInput").ap()
    alog_d = nc.dram_tensor("alog", [32, 1], F32, kind="ExternalInput").ap()
    rwp_d = nc.dram_tensor("rwp", [128, 112], F32, kind="ExternalInput").ap()
    idf_d = nc.dram_tensor("idf", [128, 128], F32, kind="ExternalInput").ap()
    idb_d = nc.dram_tensor("idb", [128, 128], BF16, kind="ExternalInput").ap()
    mask_d = nc.dram_tensor("mask", [128, 128], F32, kind="ExternalInput").ap()
    out_d = nc.dram_tensor("out", [D_MODEL, SEQ], F32, kind="ExternalOutput").ap()

    total_cols = (11 * 3072 + 1280 + 8 * 2048) + NE * 8 * 4096
    wscr_d = nc.dram_tensor("wscr", [128, total_cols], BF16, kind="Internal").ap()
    cbounce_d = nc.dram_tensor("cbounce", [32, 128], F32, kind="Internal").ap()
    gemb_d = nc.dram_tensor("gemb", [16, TC], F32, kind="Internal").ap()

    dbg_outs = {}
    if dbg:
        for nm, shp in [("d_siluz", [128, 16 * TC]), ("d_xbc", [128, 18 * (TC + 4)]),
                        ("d_dt", [32, TC]), ("d_cc", [32, TC]), ("d_y", [128, 16 * TC]),
                        ("d_x1", [128, 8 * TC]), ("d_gem", [16, TC]),
                        ("d_he", [128, 16 * TC]), ("d_xh", [128, 8 * TC]),
                        ("d_s", [128, NH * HD])]:
            dbg_outs[nm] = nc.dram_tensor(nm, shp, F32, kind="ExternalOutput").ap()

    with ExitStack() as ctx:
        S = lambda *a: ctx.enter_context(nc.sbuf_tensor(*a))
        P = lambda *a: ctx.enter_context(nc.psum_tensor(*a))

        fring = [S(f"fr{i}", [128, FSLOT_COLS], BF16) for i in range(NSLOT_F)]
        mring = [S(f"mr{i}", [128, MSLOT_COLS], BF16) for i in range(NSLOT_M)]
        xf = [S("xf0", [128, 8, TC], F32), S("xf1", [128, 8, TC], F32)]
        x_bf = S("x_bf", [128, 8, TC], BF16)
        fb_bf = S("fb_bf", [128, 8, TC], BF16)
        xhat0 = S("xhat0", [128, 8, TC], BF16)
        siluz = S("siluz", [128, 16, TC], BF16)
        xbc = S("xbc", [128, 18, TC + 4], BF16)
        y = S("y", [128, 16, TC], BF16)
        he = S("he", [128, 16, TC], BF16)
        dt = S("dt", [32, TC], F32)
        cc = S("cc", [32, TC], F32)
        scr = S("scr", [128, 2, TC], F32)
        scrb = S("scrb", [128, 2, TC], BF16)
        arena_m = S("arena_m", [1, 2048], F32)
        arena_f = S("arena_f", [1, 2048], F32)
        gem = S("gem", [16, TC], F32)
        wg = S("wg", [128, TC], BF16)
        st = S("st", [128, NH * HD], BF16)
        m_s = S("m_s", [128, 128], BF16)
        btok = S("btok", [128, 128], BF16)
        lp2 = S("lp2", [128, 4, 128], BF16)
        dl = S("dl", [128, 128], F32)
        xdt2 = S("xdt2", [128, 4, HD], BF16)
        xdw2 = S("xdw2", [128, 4, HD], BF16)
        ytok2 = S("ytok2", [128, 4, HD], BF16)
        ytmp = S("ytmp", [128, 2, HD], BF16)
        dtT = S("dtT", [128, 32], F32)
        cT = S("cT", [128, 32], F32)
        ecT = S("ecT", [128, 32], F32)
        cqb = S("cqb", [128, 64], F32)
        xdf = S("xdf", [128, 32], F32)
        rows = S("rows", [1, 64], F32)
        w16 = S("w16", [128, 16], F32)
        gts = S("gts", [128, 14], F32)
        wtmp = S("wtmp", [128, 14], F32)
        r1 = S("r1", [128, 1], F32)
        r2 = S("r2", [128, 1], F32)
        rq = S("rq", [128, 1], F32)
        idf = S("idf_s", [128, 128], F32)
        idb = S("idb_s", [128, 128], BF16)
        mask = S("mask_s", [128, 128], F32)
        ones_row = S("ones_row", [1, 128], F32)
        ones_col = S("ones_col", [128, 1], BF16)
        zero32 = S("zero32", [32, 128], F32)
        cwp = S("cwp_s", [128, 72], F32)
        cbp = S("cbp_s", [128, 18], F32)
        nwp = S("nwp_s", [128, 16], F32)
        rmsp = S("rmsp_s", [128, 112], F32)
        rmssp = S("rmssp_s", [128, 16], F32)
        dcol = S("dcol_s", [128, 16], F32)
        dtb = S("dtb_s", [32, 1], F32)
        acol = S("acol_s", [32, 1], F32)
        rwp = S("rwp_s", [128, 8, NE_R], F32)

        pb = [P(f"pb{i}", [128, 512], F32) for i in range(6)]
        pb6 = P("pb6", [128, 1024], BF16)
        pb7 = P("pb7", [128, 1024], BF16)

        kb = KB(nc, ctx)
        E = kb.emit

        def scrF():
            return he[:].rearrange("p a b -> p (a b)")[:, 0:1024].bitcast(F32)

        mpieces = []
        fpieces = []
        cursor = [0]

        def add_piece(lst, src, K, W, scales=None):
            lst.append(Piece(src, K, W, cursor[0], scales))
            cursor[0] += K * W

        ipr = ipw_d.rearrange("(k p) f -> p k f", p=128)
        for g in range(12):
            f0 = g * 384
            f1 = min(DIP, f0 + 384)
            add_piece(mpieces, ipr[:, :, f0:f1], 8, f1 - f0)
        opr = opw_d.rearrange("(k p) o -> p k o", p=128)
        for o in range(8):
            add_piece(mpieces, opr[:, :, o * 128:(o + 1) * 128], 16, 128,
                      [nwp[:, k:k + 1] for k in range(16)])
        for e in range(NE):
            if e < NE_R:
                w1e = w1_d[e].rearrange("(k p) h -> p k h", p=128)
                w2e = w2_d[e].rearrange("(k p) o -> p k o", p=128)
                rc = lambda k, e=e: rmsp[:, e * 8 + k:e * 8 + k + 1]
            else:
                w1e = w1s_d[e - NE_R].rearrange("(k p) h -> p k h", p=128)
                w2e = w2s_d[e - NE_R].rearrange("(k p) o -> p k o", p=128)
                rc = lambda k, e=e: rmssp[:, (e - NE_R) * 8 + k:(e - NE_R) * 8 + k + 1]
            for qq in range(4):
                add_piece(fpieces, w1e[:, :, qq * 512:(qq + 1) * 512], 8, 512,
                          [rc(k) for k in range(8)])
            for qq in range(4):
                add_piece(fpieces, w2e[:, :, qq * 256:(qq + 1) * 256], 16, 256)
        assert cursor[0] == total_cols, (cursor[0], total_cols)

        for dst, src in [(cwp, cwp_d), (cbp, cbp_d), (nwp, nwp_d), (rmsp, rmsp_d),
                         (rmssp, rmssp_d), (dcol, dcol_d), (dtb, dtb_d),
                         (idf, idf_d), (idb, idb_d), (mask, mask_d)]:
            E("p", lambda d=dst, s=src: nc.gpsimd.dma_start(out=d[:], in_=s), w=[dst], dma="init")
        E("p", lambda: nc.gpsimd.dma_start(
            out=rwp[:], in_=rwp_d.rearrange("p (k e) -> p k e", k=8)), w=[rwp], dma="init")
        E("p", lambda: nc.gpsimd.dma_start(out=acol[:], in_=alog_d), w=[acol], dma="init")
        E("v", lambda: nc.vector.memset(ones_row[:], 1.0), w=[ones_row])
        E("v", lambda: nc.vector.memset(ones_col[:], 1.0), w=[ones_col])
        E("v", lambda: nc.vector.memset(zero32[:], 0.0), w=[zero32])
        E("v", lambda: nc.vector.memset(w16[:], 0.0), w=[w16])
        E("v", lambda: nc.vector.memset(st[:], 0.0), w=[st])
        E("v", lambda: nc.vector.memset(xbc[:], 0.0), w=[xbc])
        E("s", lambda: nc.scalar.activation(acol[:], acol[:], AF.Exp), r=[acol], w=[acol])
        E("v", lambda: nc.vector.tensor_scalar_mul(acol[:], acol[:], -1.0), r=[acol], w=[acol])

        def emit_convert(idx, pc):
            slot = fring[idx % NSLOT_F]
            v3 = slot[:, 0:pc.K * pc.W].rearrange("p (k w) -> p k w", k=pc.K)
            E("p", lambda v=v3, s=pc.src: nc.gpsimd.dma_start(out=v, in_=s),
              w=[slot], dma=f"cv{idx % NSLOT_F}")
            if pc.scales:
                for k in range(pc.K):
                    E("v", lambda v=v3, k=k, sc=pc.scales[k]:
                      nc.vector.tensor_scalar(v[:, k, :], v[:, k, :], sc, None, op0=ALU.mult),
                      r=[slot], w=[slot])
            E("q", lambda sl=slot, pc=pc: nc.sync.dma_start(
                out=wscr_d[:, pc.off:pc.off + pc.K * pc.W], in_=sl[:, 0:pc.K * pc.W]),
                r=[slot], w=["wscr"], dma=f"cw{idx % NSLOT_F}")
        for idx, pc in enumerate(mpieces + fpieces):
            emit_convert(idx, pc)
        kb.fence()

        RM = Ring(kb, nc, wscr_d, mring, mpieces, "ms")
        RF = Ring(kb, nc, wscr_d, fring, fpieces, "fs")

        xr = x_d.rearrange("(c p) t -> p c t", p=128)
        outr = out_d.rearrange("(c p) t -> p c t", p=128)

        def chunk_ap(base, cix):
            if isinstance(cix, int):
                return base[:, :, cix * TC:(cix + 1) * TC]
            k, sub = cix
            if sub < 2:
                return base[:, :, bass.ts(k, 2 * TC)][:, :, sub * TC:(sub + 1) * TC]
            return base[:, :, bass.ts(k + 1, 2 * TC)][:, :, 0:TC]

        def mamba_thunks(cix, par):
            T = []
            A = T.append
            xt = xf[par]
            A(lambda: E("s", lambda: nc.scalar.dma_start(out=xt[:], in_=chunk_ap(xr, cix)),
                        w=[xt], dma="x"))
            for c8 in range(8):
                A(lambda c8=c8: E("p", lambda: nc.gpsimd.tensor_copy(
                    x_bf[:, c8, :], xt[:, c8, :]), r=[xt], w=[x_bf]))
            for c0 in range(0, 18, 3):
                def halo(c0=c0):
                    for c in range(c0, min(18, c0 + 3)):
                        E("p", lambda c=c: nc.gpsimd.tensor_copy(
                            xbc[:, c, 0:4], xbc[:, c, TC:TC + 4]), r=[xbc], w=[xbc])
                A(halo)

            def ip_col(j):
                g, jl = j // 3, j % 3
                wcol = 128 if j < 34 else 32
                bank = pb[2 + j % 2]
                slot, pc = RM.need(g)
                for k in range(8):
                    base = k * pc.W + jl * 128
                    E("t", lambda b=bank, sl=slot, base=base, w=wcol, k=k:
                      nc.tensor.matmul(b[0:w, 0:TC], sl[:, base:base + w], x_bf[:, k, :],
                                       start=(k == 0), stop=(k == 7)),
                      r=[slot, x_bf], w=[bank])
                if j < 16:
                    E("s", lambda b=bank, j=j: nc.scalar.activation(
                        siluz[:, j, :], b[:, 0:TC], AF.Silu), r=[bank], w=[siluz])
                elif j < 34:
                    E("s", lambda b=bank, j=j: nc.scalar.copy(
                        xbc[:, j - 16, 4:4 + TC], b[:, 0:TC]), r=[bank], w=[xbc])
                else:
                    E("v", lambda b=bank: nc.vector.tensor_scalar(
                        scr[0:32, 0, :], b[0:32, 0:TC], dtb[:], None, op0=ALU.add),
                        r=[bank, dtb], w=["scr0"])
                    E("s", lambda: nc.scalar.activation(cc[:, :], scr[0:32, 0, :], AF.Abs),
                      r=["scr0"], w=[cc])
                    E("s", lambda: nc.scalar.activation(cc[:, :], cc[:, :], AF.Exp, scale=-1.0),
                      r=[cc], w=[cc])
                    E("s", lambda: nc.scalar.activation(cc[:, :], cc[:, :], AF.Ln, bias=1.0),
                      r=[cc], w=[cc])
                    E("v", lambda: nc.vector.tensor_scalar_max(
                        scr[0:32, 0, :], scr[0:32, 0, :], 0.0), r=["scr0"], w=["scr0"])
                    E("v", lambda: nc.vector.tensor_add(dt[:, :], scr[0:32, 0, :], cc[:, :]),
                      r=["scr0", cc], w=[dt])
            for j in range(35):
                A(lambda j=j: ip_col(j))

            def conv_c(c):
                pcy = c % 2
                E("v", lambda pcy=pcy: nc.vector.tensor_scalar(
                    scr[:, pcy, :], xbc[:, c, 1:1 + TC], cwp[:, c * 4 + 3:c * 4 + 4],
                    cbp[:, c:c + 1], op0=ALU.mult, op1=ALU.add),
                    r=[xbc, cwp, cbp], w=[f"scr{pcy}"])
                for jj in (2, 1, 0):
                    E("v", lambda jj=jj, pcy=pcy: nc.vector.scalar_tensor_tensor(
                        scr[:, pcy, :], xbc[:, c, 4 - jj:4 - jj + TC], cwp[:, c * 4 + jj:c * 4 + jj + 1],
                        scr[:, pcy, :], op0=ALU.mult, op1=ALU.add),
                        r=[xbc, cwp, f"scr{pcy}"], w=[f"scr{pcy}"])
                E("s", lambda pcy=pcy: nc.scalar.activation(
                    xbc[:, c, 4:4 + TC], scr[:, pcy, :], AF.Silu),
                  r=[f"scr{pcy}"], w=[xbc])
            for c in range(18):
                A(lambda c=c: conv_c(c))

            def dt_a():
                E("v", lambda: nc.vector.tensor_scalar(
                    scr[0:32, 0, :], dt[:, :], acol[:], None, op0=ALU.mult),
                    r=[dt, acol], w=["scr0"])
                for q in range(NQ):
                    E("v", lambda q=q: nc.vector.tensor_tensor_scan(
                        cc[:, q * 128:(q + 1) * 128], scr[0:32, 0, q * 128:(q + 1) * 128],
                        zero32[:, 0:128], 0.0, op0=ALU.add, op1=ALU.add),
                        r=["scr0", zero32], w=[cc])
            A(dt_a)

            for q in range(NQ):
                qs0 = q * 128
                xq = 4 + q * 128

                def qstart(q=q, qs0=qs0, xq=xq):
                    E("q", lambda: nc.sync.dma_start(out=cbounce_d, in_=cc[:, qs0:qs0 + 128]),
                      r=[cc], w=["cbounce"], dma="cb")
                    E("q", lambda: nc.sync.dma_start(
                        out=arena_m[0:1, :],
                        in_=cbounce_d[0:16].rearrange("h t -> (h t)")[None, :]),
                        r=["cbounce"], w=[arena_m], dma="ar")
                    E("t", lambda: nc.tensor.matmul(
                        pb[4][:, 0:128], xbc[:, 16, xq:xq + 128], xbc[:, 17, xq:xq + 128],
                        start=True, stop=True), r=[xbc], w=[pb[4]])
                    E("v", lambda: nc.vector.tensor_copy(m_s[:], pb[4][:, 0:128]),
                      r=[pb[4]], w=[m_s])
                    E("t", lambda: nc.tensor.transpose(
                        pb7[:, 0:128], xbc[:, 16, xq:xq + 128], idb[:]), r=[xbc, idb], w=[pb7])
                    E("s", lambda: nc.scalar.copy(btok[:], pb7[:, 0:128]), r=[pb7], w=[btok])
                    E("t", lambda: nc.tensor.transpose(
                        pb[4][:, 128:160], dt[:, qs0:qs0 + 128], idf[0:32, 0:32]),
                        r=[dt, idf], w=[pb[4]])
                    E("v", lambda: nc.vector.tensor_copy(dtT[:], pb[4][:, 128:160]),
                      r=[pb[4]], w=[dtT])
                    E("t", lambda: nc.tensor.transpose(
                        pb[4][:, 160:192], cc[:, qs0:qs0 + 128], idf[0:32, 0:32]),
                        r=[cc, idf], w=[pb[4]])
                    E("v", lambda: nc.vector.tensor_copy(cT[:], pb[4][:, 160:192]),
                      r=[pb[4]], w=[cT])
                    E("s", lambda: nc.scalar.activation(ecT[:], pb[4][:, 160:192], AF.Exp),
                      r=[pb[4]], w=[ecT])
                    E("t", lambda: nc.tensor.transpose(
                        pb[4][0:1, 192:224], cc[:, qs0 + 127:qs0 + 128], idf[0:32, 0:32]),
                        r=[cc, idf], w=[pb[4]])
                    E("s", lambda: nc.scalar.copy(rows[0:1, 0:32], pb[4][0:1, 192:224]),
                      r=[pb[4]], w=[rows])
                    E("s", lambda: nc.scalar.activation(rows[0:1, 32:64], pb[4][0:1, 192:224], AF.Exp),
                      r=[pb[4]], w=[rows])
                    E("t", lambda: nc.tensor.matmul(pb[4][:, 224:288], ones_row[:],
                                                    rows[0:1, 0:64], start=True, stop=True),
                      r=[ones_row, rows], w=[pb[4]])
                    E("v", lambda: nc.vector.tensor_copy(cqb[:], pb[4][:, 224:288]),
                      r=[pb[4]], w=[cqb])
                    E("v", lambda: nc.vector.tensor_sub(xdf[:], cqb[:, 0:32], cT[:]),
                      r=[cqb, cT], w=[xdf])
                    E("s", lambda: nc.scalar.activation(xdf[:], xdf[:], AF.Exp), r=[xdf], w=[xdf])
                A(qstart)

                def arena2(q=q):
                    E("q", lambda: nc.sync.dma_start(
                        out=arena_m[0:1, :],
                        in_=cbounce_d[16:32].rearrange("h t -> (h t)")[None, :]),
                        r=["cbounce"], w=[arena_m], dma="ar")

                def tCB(g4):
                    off = ((g4 * 4) % 16) * 128
                    E("t", lambda: nc.tensor.matmul(
                        pb[5][:, 0:512], ones_row[:], arena_m[0:1, off:off + 512],
                        start=True, stop=True), r=[ones_row, arena_m], w=[pb[5]])

                def tA(h, xq=xq):
                    hp = h % 4
                    hc, ho = h // 2, (h % 2) * 64
                    E("v", lambda: nc.vector.scalar_tensor_tensor(
                        dl[:], pb[5][:, (h % 4) * 128:(h % 4 + 1) * 128], cT[:, h:h + 1], mask[:],
                        op0=ALU.subtract, op1=ALU.add), r=[pb[5], cT, mask], w=[dl])
                    E("s", lambda: nc.scalar.activation(lp2[:, hp, :], dl[:], AF.Exp),
                      r=[dl], w=[lp2])
                    E("v", lambda: nc.vector.tensor_mul(lp2[:, hp, :], lp2[:, hp, :], m_s[:]),
                      r=[lp2, m_s], w=[lp2])
                    E("t", lambda: nc.tensor.transpose(
                        pb6[:, 0:64], xbc[ho:ho + 64, hc, xq:xq + 128],
                        idb[ho:ho + 64, ho:ho + 64]), r=[xbc, idb], w=[pb6])
                    E("s", lambda: nc.scalar.mul(xdt2[:, hp, :], pb6[:, 0:64], dtT[:, h:h + 1]),
                      r=[pb6, dtT], w=[xdt2])
                    E("v", lambda: nc.vector.tensor_scalar(
                        xdw2[:, hp, :], xdt2[:, hp, :], xdf[:, h:h + 1], None, op0=ALU.mult),
                        r=[xdt2, xdf], w=[xdw2])

                def tB(h, xq=xq):
                    hp = h % 4
                    E("t", lambda: nc.tensor.matmul(pb[2][:, 0:64], lp2[:, hp, :],
                                                    xdt2[:, hp, :], start=True, stop=True),
                      r=[lp2, xdt2], w=[pb[2]])
                    E("t", lambda: nc.tensor.matmul(
                        pb[3][:, 0:64], xbc[:, 17, xq:xq + 128], st[:, h * 64:(h + 1) * 64],
                        start=True, stop=True), r=[xbc, st], w=[pb[3]])
                    E("s", lambda: nc.scalar.mul(ytmp[:, h % 2, :], pb[3][:, 0:64], ecT[:, h:h + 1]),
                      r=[pb[3], ecT], w=[ytmp])
                    E("v", lambda: nc.vector.tensor_add(
                        ytok2[:, hp, :], pb[2][:, 0:64], ytmp[:, h % 2, :]),
                      r=[pb[2], ytmp], w=[ytok2])
                    E("t", lambda: nc.tensor.matmul(pb[4][:, 0:64], btok[:], xdw2[:, hp, :],
                                                    start=True, stop=True),
                      r=[btok, xdw2], w=[pb[4]])
                    E("v", lambda: nc.vector.scalar_tensor_tensor(
                        st[:, h * 64:(h + 1) * 64], st[:, h * 64:(h + 1) * 64],
                        cqb[:, 32 + h:33 + h], pb[4][:, 0:64], op0=ALU.mult, op1=ALU.add),
                        r=[st, cqb, pb[4]], w=[st])

                def tC(h, qs0=qs0, xq=xq):
                    hp = h % 4
                    hc, ho = h // 2, (h % 2) * 64
                    E("t", lambda: nc.tensor.transpose(pb7[0:64, 128:256], ytok2[:, hp, :], idb[:]),
                      r=[ytok2, idb], w=[pb7])
                    E("v", lambda: nc.vector.scalar_tensor_tensor(
                        y[ho:ho + 64, hc, qs0:qs0 + 128], xbc[ho:ho + 64, hc, xq:xq + 128],
                        dcol[ho:ho + 64, hc:hc + 1], pb7[0:64, 128:256],
                        op0=ALU.mult, op1=ALU.add), r=[xbc, dcol, pb7], w=[y])

                for h in range(NH + 5):
                    if h < NH:
                        if h % 4 == 0:
                            A(lambda g4=h // 4, tCB=tCB: tCB(g4))
                        A(lambda h=h, tA=tA: tA(h))
                    if h == 15:
                        A(arena2)
                    if 3 <= h < NH + 3:
                        A(lambda h=h, tB=tB: tB(h - 3))
                    if h >= 5:
                        A(lambda h=h, tC=tC: tC(h - 5))

            def gat_c(c):
                E("v", lambda: nc.vector.tensor_mul(y[:, c, :], y[:, c, :], siluz[:, c, :]),
                  r=[y, siluz], w=[y])
                E("s", lambda: nc.scalar.activation(scrb[:, c % 2, :], y[:, c, :], AF.Square),
                  r=[y], w=[scrb])
                E("t", lambda: nc.tensor.matmul(
                    pb[5][0:1, 0:TC], ones_col[:], scrb[:, c % 2, :],
                    start=(c == 0), stop=(c == 15)), r=[ones_col, scrb], w=[pb[5]])
            for c in range(16):
                A(lambda c=c: gat_c(c))

            def grms():
                E("v", lambda: nc.vector.tensor_scalar(
                    arena_m[0:1, 0:TC], pb[5][0:1, 0:TC], 1.0 / D_INNER, MRMS_EPS,
                    op0=ALU.mult, op1=ALU.add), r=[pb[5]], w=[arena_m])
                E("s", lambda: nc.scalar.activation(arena_m[0:1, 0:TC], arena_m[0:1, 0:TC], AF.Ln),
                  r=[arena_m], w=[arena_m])
                E("v", lambda: nc.vector.tensor_scalar_mul(
                    arena_m[0:1, 0:TC], arena_m[0:1, 0:TC], -0.5), r=[arena_m], w=[arena_m])
                E("s", lambda: nc.scalar.activation(arena_m[0:1, 0:TC], arena_m[0:1, 0:TC], AF.Exp),
                  r=[arena_m], w=[arena_m])
                E("t", lambda: nc.tensor.matmul(pb[5][:, 0:TC], ones_row[:], arena_m[0:1, 0:TC],
                                                start=True, stop=True),
                  r=[ones_row, arena_m], w=[pb[5]])
            A(grms)
            for c0 in range(0, 16, 4):
                def gapply(c0=c0):
                    for c in range(c0, c0 + 4):
                        E("v", lambda c=c: nc.vector.tensor_mul(
                            y[:, c, :], y[:, c, :], pb[5][:, 0:TC]), r=[y, pb[5]], w=[y])
                A(gapply)

            def op_col(o):
                slot, pc = RM.need(12 + o)
                bank = pb[2 + o % 2]
                for k in range(16):
                    base = k * pc.W
                    E("t", lambda b=bank, sl=slot, base=base, k=k:
                      nc.tensor.matmul(b[:, 0:TC], sl[:, base:base + 128], y[:, k, :],
                                       start=(k == 0), stop=(k == 15)),
                      r=[slot, y], w=[bank])
                E("v", lambda b=bank, o=o: nc.vector.tensor_add(
                    xt[:, o, :], xt[:, o, :], b[:, 0:TC]), r=[bank, xt], w=[xt])
            for o in range(8):
                A(lambda o=o: op_col(o))

            def ln1_c(c):
                E("s", lambda: nc.scalar.copy(x_bf[:, c, :], xt[:, c, :]), r=[xt], w=[x_bf])
                E("t", lambda: nc.tensor.matmul(
                    pb[4][0:1, 0:TC], ones_col[:], x_bf[:, c, :],
                    start=(c == 0), stop=(c == 7)), r=[ones_col, x_bf], w=[pb[4]])
                E("s", lambda: nc.scalar.activation(scrb[:, c % 2, :], x_bf[:, c, :], AF.Square),
                  r=[x_bf], w=[scrb])
                E("t", lambda: nc.tensor.matmul(
                    pb[5][0:1, 0:TC], ones_col[:], scrb[:, c % 2, :],
                    start=(c == 0), stop=(c == 7)), r=[ones_col, scrb], w=[pb[5]])
            for c in range(8):
                A(lambda c=c: ln1_c(c))

            def ln1_rows():
                E("v", lambda: nc.vector.tensor_scalar(
                    arena_m[0:1, 0:TC], pb[4][0:1, 0:TC], 1.0 / D_MODEL, None, op0=ALU.mult),
                    r=[pb[4]], w=[arena_m])
                E("v", lambda: nc.vector.tensor_scalar(
                    arena_m[0:1, TC:2 * TC], pb[5][0:1, 0:TC], 1.0 / D_MODEL, LN_EPS,
                    op0=ALU.mult, op1=ALU.add), r=[pb[5]], w=[arena_m])
                E("s", lambda: nc.scalar.activation(
                    scr[0:1, 0, 0:TC], arena_m[0:1, 0:TC], AF.Square), r=[arena_m], w=["scr0"])
                E("v", lambda: nc.vector.tensor_sub(
                    arena_m[0:1, TC:2 * TC], arena_m[0:1, TC:2 * TC], scr[0:1, 0, 0:TC]),
                    r=[arena_m, "scr0"], w=[arena_m])
                E("s", lambda: nc.scalar.activation(
                    arena_m[0:1, TC:2 * TC], arena_m[0:1, TC:2 * TC], AF.Ln),
                    r=[arena_m], w=[arena_m])
                E("v", lambda: nc.vector.tensor_scalar_mul(
                    arena_m[0:1, TC:2 * TC], arena_m[0:1, TC:2 * TC], -0.5),
                    r=[arena_m], w=[arena_m])
                E("s", lambda: nc.scalar.activation(
                    arena_m[0:1, TC:2 * TC], arena_m[0:1, TC:2 * TC], AF.Exp),
                    r=[arena_m], w=[arena_m])
                E("v", lambda: nc.vector.tensor_mul(
                    arena_m[0:1, 0:TC], arena_m[0:1, 0:TC], arena_m[0:1, TC:2 * TC]),
                    r=[arena_m], w=[arena_m])
                E("t", lambda: nc.tensor.matmul(pb[2][:, 0:TC], ones_row[:],
                                                arena_m[0:1, TC:2 * TC], start=True, stop=True),
                  r=[ones_row, arena_m], w=[pb[2]])
                E("t", lambda: nc.tensor.matmul(pb[3][:, 0:TC], ones_row[:],
                                                arena_m[0:1, 0:TC], start=True, stop=True),
                  r=[ones_row, arena_m], w=[pb[3]])
            A(ln1_rows)
            for c0 in range(0, 8, 2):
                def ln1_apply(c0=c0):
                    for c in range(c0, c0 + 2):
                        E("v", lambda c=c: nc.vector.tensor_mul(
                            scr[:, c % 2, :], xt[:, c, :], pb[2][:, 0:TC]),
                            r=[xt, pb[2]], w=[f"scr{c % 2}"])
                        E("v", lambda c=c: nc.vector.tensor_sub(
                            xt[:, c, :], scr[:, c % 2, :], pb[3][:, 0:TC]),
                            r=[f"scr{c % 2}", pb[3]], w=[xt])
                A(ln1_apply)
            return T

        def emit_ffn(cix, par, pump):
            if pump is None:
                pump = lambda: None
            xt = xf[par]
            for g4 in range(4):
                for k in range(8):
                    E("t", lambda g4=g4, k=k: nc.tensor.matmul(
                        pb[0][:, g4 * 14:(g4 + 1) * 14], xt[:, k, g4 * 128:(g4 + 1) * 128],
                        rwp[:, k, :], start=(k == 0), stop=(k == 7)),
                        r=[xt, rwp], w=[pb[0]])
                pump()
            for g4 in range(4):
                E("v", lambda g4=g4: nc.vector.reduce_max(
                    r1[:], pb[0][:, g4 * 14:(g4 + 1) * 14], axis=AX.X, negate=True),
                    r=[pb[0]], w=[r1])
                E("s", lambda g4=g4: nc.scalar.activation(
                    gts[:], pb[0][:, g4 * 14:(g4 + 1) * 14], AF.Exp, bias=r1[:]),
                    r=[pb[0], r1], w=[gts])
                E("v", lambda: nc.vector.reduce_sum(rq[:], gts[:], axis=AX.X), r=[gts], w=[rq])
                E("v", lambda: nc.vector.reciprocal(rq[:], rq[:]), r=[rq], w=[rq])
                E("v", lambda: nc.vector.tensor_scalar(gts[:], gts[:], rq[:], None, op0=ALU.mult),
                  r=[gts, rq], w=[gts])
                E("v", lambda: nc.vector.reduce_max(r1[:], gts[:], axis=AX.X), r=[gts], w=[r1])
                E("v", lambda: nc.vector.tensor_scalar(wtmp[:], gts[:], r1[:], None, op0=ALU.is_ge),
                  r=[gts, r1], w=[wtmp])
                E("v", lambda: nc.vector.scalar_tensor_tensor(
                    wtmp[:], wtmp[:], -1e30, gts[:], op0=ALU.mult, op1=ALU.add),
                    r=[wtmp, gts], w=[wtmp])
                E("v", lambda: nc.vector.reduce_max(r2[:], wtmp[:], axis=AX.X), r=[wtmp], w=[r2])
                E("v", lambda: nc.vector.tensor_scalar(wtmp[:], gts[:], r2[:], None, op0=ALU.is_ge),
                  r=[gts, r2], w=[wtmp])
                E("v", lambda: nc.vector.tensor_mul(wtmp[:], gts[:], wtmp[:]),
                  r=[gts, wtmp], w=[wtmp])
                E("v", lambda: nc.vector.reduce_sum(rq[:], wtmp[:], axis=AX.X), r=[wtmp], w=[rq])
                E("v", lambda: nc.vector.tensor_scalar_add(rq[:], rq[:], GATE_EPS), r=[rq], w=[rq])
                E("v", lambda: nc.vector.reciprocal(rq[:], rq[:]), r=[rq], w=[rq])
                E("v", lambda: nc.vector.tensor_scalar(
                    w16[:, 0:14], wtmp[:], rq[:], None, op0=ALU.mult), r=[wtmp, rq], w=[w16])
                E("t", lambda: nc.tensor.transpose(pb[1][0:16, 0:128], w16[:], idf[:]),
                  r=[w16, idf], w=[pb[1]])
                E("v", lambda g4=g4: nc.vector.tensor_copy(
                    gem[:, g4 * 128:(g4 + 1) * 128], pb[1][0:16, 0:128]), r=[pb[1]], w=[gem])
                pump()
            E("q", lambda: nc.sync.dma_start(out=gemb_d, in_=gem[:]),
              r=[gem], w=["gemb"], dma="gemb")
            for c in range(8):
                E("s", lambda c=c: nc.scalar.activation(
                    fb_bf[:, c % 2, :], xt[:, c, :], AF.Square), r=[xt], w=[fb_bf])
                E("t", lambda c=c: nc.tensor.matmul(
                    pb[1][0:1, 0:TC], ones_col[:], fb_bf[:, c % 2, :],
                    start=(c == 0), stop=(c == 7)), r=[ones_col, fb_bf], w=[pb[1]])
            E("v", lambda: nc.vector.tensor_scalar(
                arena_f[0:1, 0:TC], pb[1][0:1, 0:TC], 1.0 / D_MODEL, FRMS_EPS,
                op0=ALU.mult, op1=ALU.add), r=[pb[1]], w=[arena_f])
            E("s", lambda: nc.scalar.activation(arena_f[0:1, 0:TC], arena_f[0:1, 0:TC], AF.Ln),
              r=[arena_f], w=[arena_f])
            E("v", lambda: nc.vector.tensor_scalar_mul(
                arena_f[0:1, 0:TC], arena_f[0:1, 0:TC], -0.5), r=[arena_f], w=[arena_f])
            E("s", lambda: nc.scalar.activation(arena_f[0:1, 0:TC], arena_f[0:1, 0:TC], AF.Exp),
              r=[arena_f], w=[arena_f])
            E("t", lambda: nc.tensor.matmul(pb[0][:, 0:TC], ones_row[:], arena_f[0:1, 0:TC],
                                            start=True, stop=True),
              r=[ones_row, arena_f], w=[pb[0]])
            pump()
            for c in range(8):
                E("v", lambda c=c: nc.vector.tensor_mul(
                    xhat0[:, c, :], xt[:, c, :], pb[0][:, 0:TC]), r=[xt, pb[0]], w=[xhat0])
            pump()

            for e in range(NE):
                if e < NE_R and e % 2 == 0:
                    E("q", lambda e=e: nc.sync.dma_start(
                        out=arena_f[0:1, 0:2 * TC],
                        in_=gemb_d[e:e + 2, :].rearrange("a t -> (a t)")[None, :]),
                        r=["gemb"], w=[arena_f], dma="ar2")
                if e < NE_R:
                    E("t", lambda e=e: nc.tensor.matmul(
                        pb[1][:, 0:TC], ones_row[:],
                        arena_f[0:1, (e % 2) * TC:(e % 2 + 1) * TC],
                        start=True, stop=True), r=[ones_row, arena_f], w=[pb[1]])
                    E("s", lambda: nc.scalar.copy(wg[:], pb[1][:, 0:TC]), r=[pb[1]], w=[wg])
                p1a = 8 * e
                for hcol in range(16):
                    slot, pc = RF.need(p1a + hcol // 4)
                    hl = hcol % 4
                    bank = pb[hcol % 2]
                    for k in range(8):
                        base = k * pc.W + hl * 128
                        E("t", lambda b=bank, sl=slot, base=base, k=k:
                          nc.tensor.matmul(b[:, 0:TC], sl[:, base:base + 128],
                                           xhat0[:, k, :], start=(k == 0), stop=(k == 7)),
                          r=[slot, xhat0], w=[bank])
                    E("s", lambda b=bank, hcol=hcol: nc.scalar.activation(
                        he[:, hcol, :], b[:, 0:TC], AF.Silu), r=[bank], w=[f"he{hcol}"])
                    if e < NE_R:
                        E("v", lambda hcol=hcol: nc.vector.tensor_mul(
                            he[:, hcol, :], he[:, hcol, :], wg[:]),
                            r=[f"he{hcol}", wg], w=[f"he{hcol}"])
                    pump()
                p2a = 8 * e + 4
                for o in range(8):
                    slot, pc = RF.need(p2a + o // 2)
                    ol = o % 2
                    bank = pb[o % 2]
                    for k in range(16):
                        base = k * pc.W + ol * 128
                        E("t", lambda b=bank, sl=slot, base=base, k=k:
                          nc.tensor.matmul(b[:, 0:TC], sl[:, base:base + 128],
                                           he[:, k, :], start=(k == 0), stop=(k == 15)),
                          r=[slot, f"he{k}"], w=[bank])
                    E("v", lambda b=bank, o=o: nc.vector.tensor_add(
                        xt[:, o, :], xt[:, o, :], b[:, 0:TC]), r=[bank, xt], w=[xt])
                    pump()

            for c in range(8):
                E("s", lambda c=c: nc.scalar.copy(fb_bf[:, c, :], xt[:, c, :]), r=[xt], w=[fb_bf])
                E("t", lambda c=c: nc.tensor.matmul(
                    pb[0][0:1, 0:TC], ones_col[:], fb_bf[:, c, :],
                    start=(c == 0), stop=(c == 7)), r=[ones_col, fb_bf], w=[pb[0]])
                E("s", lambda c=c: nc.scalar.activation(
                    xhat0[:, c % 2, :], fb_bf[:, c, :], AF.Square), r=[fb_bf], w=[xhat0])
                E("t", lambda c=c: nc.tensor.matmul(
                    pb[1][0:1, 0:TC], ones_col[:], xhat0[:, c % 2, :],
                    start=(c == 0), stop=(c == 7)), r=[ones_col, xhat0], w=[pb[1]])
            pump()
            E("v", lambda: nc.vector.tensor_scalar(
                arena_f[0:1, 0:TC], pb[0][0:1, 0:TC], 1.0 / D_MODEL, None, op0=ALU.mult),
                r=[pb[0]], w=[arena_f])
            E("v", lambda: nc.vector.tensor_scalar(
                arena_f[0:1, TC:2 * TC], pb[1][0:1, 0:TC], 1.0 / D_MODEL, LN_EPS,
                op0=ALU.mult, op1=ALU.add), r=[pb[1]], w=[arena_f])
            E("s", lambda: nc.scalar.activation(
                scrF()[0:1, 0:TC], arena_f[0:1, 0:TC], AF.Square),
                r=[arena_f, "he0", "he1"], w=["he0", "he1"])
            E("v", lambda: nc.vector.tensor_sub(
                arena_f[0:1, TC:2 * TC], arena_f[0:1, TC:2 * TC], scrF()[0:1, 0:TC]),
                r=[arena_f, "he0"], w=[arena_f])
            E("s", lambda: nc.scalar.activation(
                arena_f[0:1, TC:2 * TC], arena_f[0:1, TC:2 * TC], AF.Ln), r=[arena_f], w=[arena_f])
            E("v", lambda: nc.vector.tensor_scalar_mul(
                arena_f[0:1, TC:2 * TC], arena_f[0:1, TC:2 * TC], -0.5), r=[arena_f], w=[arena_f])
            E("s", lambda: nc.scalar.activation(
                arena_f[0:1, TC:2 * TC], arena_f[0:1, TC:2 * TC], AF.Exp), r=[arena_f], w=[arena_f])
            E("v", lambda: nc.vector.tensor_mul(
                arena_f[0:1, 0:TC], arena_f[0:1, 0:TC], arena_f[0:1, TC:2 * TC]),
                r=[arena_f], w=[arena_f])
            E("t", lambda: nc.tensor.matmul(pb[0][:, 0:TC], ones_row[:], arena_f[0:1, TC:2 * TC],
                                            start=True, stop=True), r=[ones_row, arena_f], w=[pb[0]])
            E("t", lambda: nc.tensor.matmul(pb[1][:, 0:TC], ones_row[:], arena_f[0:1, 0:TC],
                                            start=True, stop=True), r=[ones_row, arena_f], w=[pb[1]])
            for c in range(8):
                E("v", lambda c=c: nc.vector.tensor_mul(
                    scrF()[:, 0:TC], xt[:, c, :], pb[0][:, 0:TC]),
                    r=[xt, pb[0], "he0", "he1"], w=["he0", "he1"])
                E("v", lambda c=c: nc.vector.tensor_sub(
                    xt[:, c, :], scrF()[:, 0:TC], pb[1][:, 0:TC]),
                    r=["he0", pb[1]], w=[xt])
            pump()
            E("s", lambda: nc.scalar.dma_start(out=chunk_ap(outr, cix), in_=xt[:]),
              r=[xt], dma="out")

        def run_pipelined(f_cix, f_par, m_cix, m_par):
            RM.reset()
            RF.reset()
            thunks = mamba_thunks(m_cix, m_par) if m_cix is not None else []
            pos = [0.0]
            idx = [0]
            per = 0.0

            def pump():
                pos[0] += per
                kb.stream = "M"
                while idx[0] < len(thunks) and idx[0] < pos[0]:
                    thunks[idx[0]]()
                    idx[0] += 1
                kb.stream = "F"
            if f_cix is not None:
                kb.stream = "F"
                emit_ffn(f_cix, f_par, pump if thunks else None)
            kb.stream = "M"
            while idx[0] < len(thunks):
                thunks[idx[0]]()
                idx[0] += 1
            kb.stream = "F"
            kb.fence()

        if n_chunks == NCHUNK:
            run_pipelined(None, 0, 0, 0)
            with nc.Fori(0, 7) as k:
                run_pipelined((k, 0), 0, (k, 1), 1)
                run_pipelined((k, 1), 1, (k, 2), 0)
            run_pipelined(14, 0, 15, 1)
            run_pipelined(15, 1, None, 0)
        else:
            run_pipelined(None, 0, 0, 0)
            for c in range(n_chunks):
                run_pipelined(c, c % 2, c + 1 if c + 1 < n_chunks else None, (c + 1) % 2)

        if dbg:
            last_par = (n_chunks - 1) % 2
            dumps = [("d_siluz", siluz[:].rearrange("p a b -> p (a b)")),
                     ("d_xbc", xbc[:].rearrange("p a b -> p (a b)")),
                     ("d_dt", dt[:]), ("d_cc", cc[:]),
                     ("d_y", y[:].rearrange("p a b -> p (a b)")),
                     ("d_x1", xf[last_par][:].rearrange("p a b -> p (a b)")),
                     ("d_gem", gem[:]),
                     ("d_he", he[:].rearrange("p a b -> p (a b)")),
                     ("d_xh", xhat0[:].rearrange("p a b -> p (a b)")),
                     ("d_s", st[:])]
            for nm, ap in dumps:
                E("p", lambda nm=nm, ap=ap: nc.gpsimd.dma_start(
                    out=dbg_outs[nm], in_=ap), r=[], dma="dbg")
            kb.fence()

    return nc


def _host_packs(inputs):
    f = np.float32
    conv_w = np.asarray(inputs["L0_attn_conv_w"], f)
    conv_b = np.asarray(inputs["L0_attn_conv_b"], f)
    return {
        "ipw": np.ascontiguousarray(np.asarray(inputs["L0_attn_in_proj_w"], f)),
        "opw": np.ascontiguousarray(np.asarray(inputs["L0_attn_out_proj_w"], f)),
        "w1": np.ascontiguousarray(np.asarray(inputs["L0_ffn_w1"], f)),
        "w2": np.ascontiguousarray(np.asarray(inputs["L0_ffn_w2"], f)),
        "w1s": np.ascontiguousarray(np.asarray(inputs["L0_ffn_w1_s"], f)),
        "w2s": np.ascontiguousarray(np.asarray(inputs["L0_ffn_w2_s"], f)),
        "cwp": np.ascontiguousarray(conv_w.reshape(4, 18, 128).transpose(2, 1, 0).reshape(128, 72)),
        "cbp": np.ascontiguousarray(conv_b.reshape(18, 128).T),
        "nwp": np.ascontiguousarray(np.asarray(inputs["L0_attn_norm_w"], f).reshape(16, 128).T),
        "rmsp": np.ascontiguousarray(
            np.asarray(inputs["L0_ffn_rms_w"], f).reshape(14, 8, 128).transpose(2, 0, 1).reshape(128, 112)),
        "rmssp": np.ascontiguousarray(
            np.asarray(inputs["L0_ffn_rms_w_s"], f).reshape(2, 8, 128).transpose(2, 0, 1).reshape(128, 16)),
        "dcol": np.ascontiguousarray(
            np.repeat(np.asarray(inputs["L0_attn_D"], f), 64).reshape(16, 128).T),
        "dtb": np.ascontiguousarray(np.asarray(inputs["L0_attn_dt_bias"], f).reshape(32, 1)),
        "alog": np.ascontiguousarray(np.asarray(inputs["L0_attn_A_log"], f).reshape(32, 1)),
        "rwp": np.ascontiguousarray(
            np.asarray(inputs["L0_ffn_router_w"], f).reshape(8, 128, 14).transpose(1, 0, 2).reshape(128, 112)),
        "idf": np.eye(128, dtype=f),
        "idb": np.eye(128, dtype=ml_dtypes.bfloat16),
        "mask": np.where(np.arange(128)[None, :] >= np.arange(128)[:, None],
                         np.float32(0.0), np.float32(MASK_NEG)),
    }


def kernel(**inputs):
    x = np.asarray(inputs["L0_attn_x"], np.float32)
    packs = _host_packs(inputs)
    nc = build_program(NCHUNK, dbg=False)
    in_maps = []
    for b in range(8):
        m = dict(packs)
        m["x"] = np.ascontiguousarray(x[b].T)
        in_maps.append(m)
    res = run_bass_kernel_spmd(nc, in_maps, core_ids=list(range(8)))
    return np.stack([np.ascontiguousarray(r["out"].T) for r in res.results], axis=0)


# version 12
# speedup vs baseline: 1.1947x; 1.0046x over previous
import numpy as np
import ml_dtypes
from contextlib import ExitStack

import concourse.bass as bass
import concourse.mybir as mybir
from concourse.bass_utils import run_bass_kernel_spmd

F32 = mybir.dt.float32
BF16 = mybir.dt.bfloat16
AF = mybir.ActivationFunctionType
ALU = mybir.AluOpType
AX = mybir.AxisListType

D_MODEL = 1024
D_INNER = 2048
NH = 32
HD = 64
DIP = 4384
HIDDEN = 2048
NE_R = 14
NE = 16
TC = 512
NQ = TC // 128
SEQ = 8192
NCHUNK = SEQ // TC

LN_EPS = 1e-5
MRMS_EPS = 1e-5
FRMS_EPS = 1.1920929e-07
GATE_EPS = 1e-9
MASK_NEG = -30000.0

NSLOT_F = 3
FSLOT_COLS = 4096
NSLOT_M = 2
MSLOT_COLS = 3072


class KB:

    COMPUTE = ("t", "v", "s", "p")

    def __init__(self, nc, ctx):
        self.nc = nc
        self.ctx = ctx
        self.eng = {"t": nc.tensor, "v": nc.vector, "s": nc.scalar,
                    "p": nc.gpsimd, "q": nc.sync}
        self.stream = "F"
        self.chain = {(st, e): [ctx.enter_context(nc.semaphore(f"ch_{st}{e}")), 0]
                      for st in ("F", "M") for e in self.COMPUTE}
        self.dsem = {}
        self.latest = {(st, e): None for st in ("F", "M") for e in self.COMPUTE}
        self.last_w = {}
        self.readers = {}
        self.waited = {e: {} for e in self.eng}

    @staticmethod
    def _k(h):
        return h if isinstance(h, str) else h.name

    def _edge(self, rec, ceng):
        if rec is None:
            return
        peng = rec["eng"]
        if rec["dma"]:
            sem, val = rec["sem"], rec["val"]
        else:
            if peng == "t" and ceng == "t" and rec["st"] == self.stream:
                return
            key = (rec["st"], peng)
            tgt = rec if rec["sem"] is not None else self.latest[key]
            if tgt["sem"] is None:
                cs = self.chain[key]
                cs[1] += 1
                tgt["ins"].then_inc(cs[0], 1)
                tgt["sem"], tgt["val"] = cs[0], cs[1]
            sem, val = tgt["sem"], tgt["val"]
        w = self.waited[ceng]
        if w.get(sem, 0) < val:
            self.eng[ceng].wait_ge(sem, val)
            w[sem] = val

    def emit(self, ceng, fn, r=(), w=(), dma=False):
        for h in r:
            self._edge(self.last_w.get(self._k(h)), ceng)
        for h in w:
            k = self._k(h)
            self._edge(self.last_w.get(k), ceng)
            for rec in self.readers.get(k, {}).values():
                self._edge(rec, ceng)
        ins = fn()
        rec = {"eng": ceng, "ins": ins, "sem": None, "val": None, "dma": bool(dma),
               "st": self.stream}
        if dma:
            if dma not in self.dsem:
                self.dsem[dma] = [self.ctx.enter_context(self.nc.semaphore(f"ds_{dma}")), 0]
            ds = self.dsem[dma]
            ds[1] += 16
            ins.then_inc(ds[0], 16)
            rec["sem"], rec["val"] = ds[0], ds[1]
        else:
            self.latest[(self.stream, ceng)] = rec
        for h in w:
            k = self._k(h)
            self.last_w[k] = rec
            self.readers[k] = {}
        for h in r:
            self.readers.setdefault(self._k(h), {})[ceng] = rec
        return ins

    def fence(self):
        nc = self.nc
        for role, (sem, cnt) in self.dsem.items():
            if cnt:
                nc.sync.wait_ge(sem, cnt)
        for key, (sem, cnt) in self.chain.items():
            if cnt:
                self.eng[key[1]].wait_ge(sem, cnt)
        nc.all_engine_barrier()
        for key in self.chain:
            nc.sync.sem_clear(self.chain[key][0])
        for role, (sem, cnt) in self.dsem.items():
            nc.sync.sem_clear(sem)
        nc.all_engine_barrier()
        for key in self.chain:
            self.chain[key][1] = 0
            self.latest[key] = None
        for role in self.dsem:
            self.dsem[role][1] = 0
        self.last_w = {}
        self.readers = {}
        self.waited = {e: {} for e in self.eng}


class Piece:
    def __init__(self, src, K, W, off, scales=None):
        self.src = src
        self.K = K
        self.W = W
        self.off = off
        self.scales = scales


class Ring:

    def __init__(self, kb, nc, wscr, slots, pieces, role):
        self.kb, self.nc, self.wscr = kb, nc, wscr
        self.slots = slots
        self.pieces = pieces
        self.role = role
        self.nxt = 0

    def reset(self):
        self.nxt = 0

    def need(self, i):
        while self.nxt <= min(i + len(self.slots) - 1, len(self.pieces) - 1):
            pc = self.pieces[self.nxt]
            slot = self.slots[self.nxt % len(self.slots)]
            self.kb.emit("q", lambda sl=slot, pc=pc: self.nc.sync.dma_start(
                out=sl[:, 0:pc.K * pc.W], in_=self.wscr[:, pc.off:pc.off + pc.K * pc.W]),
                r=["wscr"], w=[slot], dma=f"{self.role}{self.nxt % len(self.slots)}")
            self.nxt += 1
        return self.slots[i % len(self.slots)], self.pieces[i]


def build_program(n_chunks=NCHUNK, dbg=False):
    nc = bass.Bass("TRN2", target_bir_lowering=False, debug=False)

    x_d = nc.dram_tensor("x", [D_MODEL, SEQ], F32, kind="ExternalInput").ap()
    ipw_d = nc.dram_tensor("ipw", [D_MODEL, DIP], F32, kind="ExternalInput").ap()
    opw_d = nc.dram_tensor("opw", [D_INNER, D_MODEL], F32, kind="ExternalInput").ap()
    w1_d = nc.dram_tensor("w1", [NE_R, D_MODEL, HIDDEN], F32, kind="ExternalInput").ap()
    w2_d = nc.dram_tensor("w2", [NE_R, HIDDEN, D_MODEL], F32, kind="ExternalInput").ap()
    w1s_d = nc.dram_tensor("w1s", [2, D_MODEL, HIDDEN], F32, kind="ExternalInput").ap()
    w2s_d = nc.dram_tensor("w2s", [2, HIDDEN, D_MODEL], F32, kind="ExternalInput").ap()
    cwp_d = nc.dram_tensor("cwp", [128, 72], F32, kind="ExternalInput").ap()
    cbp_d = nc.dram_tensor("cbp", [128, 18], F32, kind="ExternalInput").ap()
    nwp_d = nc.dram_tensor("nwp", [128, 16], F32, kind="ExternalInput").ap()
    rmsp_d = nc.dram_tensor("rmsp", [128, 112], F32, kind="ExternalInput").ap()
    rmssp_d = nc.dram_tensor("rmssp", [128, 16], F32, kind="ExternalInput").ap()
    dcol_d = nc.dram_tensor("dcol", [128, 16], F32, kind="ExternalInput").ap()
    dtb_d = nc.dram_tensor("dtb", [32, 1], F32, kind="ExternalInput").ap()
    alog_d = nc.dram_tensor("alog", [32, 1], F32, kind="ExternalInput").ap()
    rwp_d = nc.dram_tensor("rwp", [128, 112], F32, kind="ExternalInput").ap()
    idf_d = nc.dram_tensor("idf", [128, 128], F32, kind="ExternalInput").ap()
    idb_d = nc.dram_tensor("idb", [128, 128], BF16, kind="ExternalInput").ap()
    mask_d = nc.dram_tensor("mask", [128, 128], F32, kind="ExternalInput").ap()
    out_d = nc.dram_tensor("out", [D_MODEL, SEQ], F32, kind="ExternalOutput").ap()

    total_cols = (11 * 3072 + 1280 + 8 * 2048) + NE * 8 * 4096
    wscr_d = nc.dram_tensor("wscr", [128, total_cols], BF16, kind="Internal").ap()
    cbounce_d = nc.dram_tensor("cbounce", [32, 128], F32, kind="Internal").ap()
    gemb_d = nc.dram_tensor("gemb", [16, TC], F32, kind="Internal").ap()

    dbg_outs = {}
    if dbg:
        for nm, shp in [("d_siluz", [128, 16 * TC]), ("d_xbc", [128, 18 * (TC + 4)]),
                        ("d_dt", [32, TC]), ("d_cc", [32, TC]), ("d_y", [128, 16 * TC]),
                        ("d_x1", [128, 8 * TC]), ("d_gem", [16, TC]),
                        ("d_he", [128, 16 * TC]), ("d_xh", [128, 8 * TC]),
                        ("d_s", [128, NH * HD])]:
            dbg_outs[nm] = nc.dram_tensor(nm, shp, F32, kind="ExternalOutput").ap()

    with ExitStack() as ctx:
        S = lambda *a: ctx.enter_context(nc.sbuf_tensor(*a))
        P = lambda *a: ctx.enter_context(nc.psum_tensor(*a))

        fring = [S(f"fr{i}", [128, FSLOT_COLS], BF16) for i in range(NSLOT_F)]
        mring = [S(f"mr{i}", [128, MSLOT_COLS], BF16) for i in range(NSLOT_M)]
        xf = [S("xf0", [128, 8, TC], F32), S("xf1", [128, 8, TC], F32)]
        x_bf = S("x_bf", [128, 8, TC], BF16)
        fb_bf = S("fb_bf", [128, 8, TC], BF16)
        xhat0 = S("xhat0", [128, 8, TC], BF16)
        siluz = S("siluz", [128, 16, TC], BF16)
        xbc = S("xbc", [128, 18, TC + 4], BF16)
        y = S("y", [128, 16, TC], BF16)
        he = S("he", [128, 16, TC], BF16)
        dt = S("dt", [32, TC], F32)
        cc = S("cc", [32, TC], F32)
        scr = S("scr", [128, 2, TC], F32)
        scrb = S("scrb", [128, 2, TC], BF16)
        arena_m = S("arena_m", [1, 2048], F32)
        arena_f = S("arena_f", [1, 2048], F32)
        gem = S("gem", [16, TC], F32)
        wg = S("wg", [128, TC], BF16)
        st = S("st", [128, NH * HD], BF16)
        m_s = S("m_s", [128, 128], BF16)
        btok = S("btok", [128, 128], BF16)
        lp2 = S("lp2", [128, 4, 128], BF16)
        dl = S("dl", [128, 128], F32)
        xdt2 = S("xdt2", [128, 4, HD], BF16)
        xdw2 = S("xdw2", [128, 4, HD], BF16)
        ytok2 = S("ytok2", [128, 4, HD], BF16)
        ytmp = S("ytmp", [128, 2, HD], BF16)
        dtT = S("dtT", [128, 32], F32)
        cT = S("cT", [128, 32], F32)
        ecT = S("ecT", [128, 32], F32)
        cqb = S("cqb", [128, 64], F32)
        xdf = S("xdf", [128, 32], F32)
        rows = S("rows", [1, 64], F32)
        w16 = S("w16", [128, 16], F32)
        gts = S("gts", [128, 14], F32)
        wtmp = S("wtmp", [128, 14], F32)
        r1 = S("r1", [128, 1], F32)
        r2 = S("r2", [128, 1], F32)
        rq = S("rq", [128, 1], F32)
        idf = S("idf_s", [128, 128], F32)
        idb = S("idb_s", [128, 128], BF16)
        mask = S("mask_s", [128, 128], F32)
        ones_row = S("ones_row", [1, 128], F32)
        ones_col = S("ones_col", [128, 1], BF16)
        zero32 = S("zero32", [32, 128], F32)
        cwp = S("cwp_s", [128, 72], F32)
        cbp = S("cbp_s", [128, 18], F32)
        nwp = S("nwp_s", [128, 16], F32)
        rmsp = S("rmsp_s", [128, 112], F32)
        rmssp = S("rmssp_s", [128, 16], F32)
        dcol = S("dcol_s", [128, 16], F32)
        dtb = S("dtb_s", [32, 1], F32)
        acol = S("acol_s", [32, 1], F32)
        rwp = S("rwp_s", [128, 8, NE_R], F32)

        pb = [P(f"pb{i}", [128, 512], F32) for i in range(6)]
        pb6 = P("pb6", [128, 1024], BF16)
        pb7 = P("pb7", [128, 1024], BF16)

        kb = KB(nc, ctx)
        E = kb.emit

        def scrF():
            return he[:].rearrange("p a b -> p (a b)")[:, 0:1024].bitcast(F32)

        mpieces = []
        fpieces = []
        cursor = [0]

        def add_piece(lst, src, K, W, scales=None):
            lst.append(Piece(src, K, W, cursor[0], scales))
            cursor[0] += K * W

        ipr = ipw_d.rearrange("(k p) f -> p k f", p=128)
        for g in range(12):
            f0 = g * 384
            f1 = min(DIP, f0 + 384)
            add_piece(mpieces, ipr[:, :, f0:f1], 8, f1 - f0)
        opr = opw_d.rearrange("(k p) o -> p k o", p=128)
        for o in range(8):
            add_piece(mpieces, opr[:, :, o * 128:(o + 1) * 128], 16, 128,
                      [nwp[:, k:k + 1] for k in range(16)])
        for e in range(NE):
            if e < NE_R:
                w1e = w1_d[e].rearrange("(k p) h -> p k h", p=128)
                w2e = w2_d[e].rearrange("(k p) o -> p k o", p=128)
                rc = lambda k, e=e: rmsp[:, e * 8 + k:e * 8 + k + 1]
            else:
                w1e = w1s_d[e - NE_R].rearrange("(k p) h -> p k h", p=128)
                w2e = w2s_d[e - NE_R].rearrange("(k p) o -> p k o", p=128)
                rc = lambda k, e=e: rmssp[:, (e - NE_R) * 8 + k:(e - NE_R) * 8 + k + 1]
            for qq in range(4):
                add_piece(fpieces, w1e[:, :, qq * 512:(qq + 1) * 512], 8, 512,
                          [rc(k) for k in range(8)])
            for qq in range(4):
                add_piece(fpieces, w2e[:, :, qq * 256:(qq + 1) * 256], 16, 256)
        assert cursor[0] == total_cols, (cursor[0], total_cols)

        for dst, src in [(cwp, cwp_d), (cbp, cbp_d), (nwp, nwp_d), (rmsp, rmsp_d),
                         (rmssp, rmssp_d), (dcol, dcol_d), (dtb, dtb_d),
                         (idf, idf_d), (idb, idb_d), (mask, mask_d)]:
            E("p", lambda d=dst, s=src: nc.gpsimd.dma_start(out=d[:], in_=s), w=[dst], dma="init")
        E("p", lambda: nc.gpsimd.dma_start(
            out=rwp[:], in_=rwp_d.rearrange("p (k e) -> p k e", k=8)), w=[rwp], dma="init")
        E("p", lambda: nc.gpsimd.dma_start(out=acol[:], in_=alog_d), w=[acol], dma="init")
        E("v", lambda: nc.vector.memset(ones_row[:], 1.0), w=[ones_row])
        E("v", lambda: nc.vector.memset(ones_col[:], 1.0), w=[ones_col])
        E("v", lambda: nc.vector.memset(zero32[:], 0.0), w=[zero32])
        E("v", lambda: nc.vector.memset(w16[:], 0.0), w=[w16])
        E("v", lambda: nc.vector.memset(st[:], 0.0), w=[st])
        E("v", lambda: nc.vector.memset(xbc[:], 0.0), w=[xbc])
        E("s", lambda: nc.scalar.activation(acol[:], acol[:], AF.Exp), r=[acol], w=[acol])
        E("v", lambda: nc.vector.tensor_scalar_mul(acol[:], acol[:], -1.0), r=[acol], w=[acol])

        def emit_convert(idx, pc):
            slot = fring[idx % NSLOT_F]
            v3 = slot[:, 0:pc.K * pc.W].rearrange("p (k w) -> p k w", k=pc.K)
            E("p", lambda v=v3, s=pc.src: nc.gpsimd.dma_start(out=v, in_=s),
              w=[slot], dma=f"cv{idx % NSLOT_F}")
            if pc.scales:
                for k in range(pc.K):
                    E("v", lambda v=v3, k=k, sc=pc.scales[k]:
                      nc.vector.tensor_scalar(v[:, k, :], v[:, k, :], sc, None, op0=ALU.mult),
                      r=[slot], w=[slot])
            E("q", lambda sl=slot, pc=pc: nc.sync.dma_start(
                out=wscr_d[:, pc.off:pc.off + pc.K * pc.W], in_=sl[:, 0:pc.K * pc.W]),
                r=[slot], w=["wscr"], dma=f"cw{idx % NSLOT_F}")
        for idx, pc in enumerate(mpieces + fpieces):
            emit_convert(idx, pc)
        kb.fence()

        RM = Ring(kb, nc, wscr_d, mring, mpieces, "ms")
        RF = Ring(kb, nc, wscr_d, fring, fpieces, "fs")

        xr = x_d.rearrange("(c p) t -> p c t", p=128)
        outr = out_d.rearrange("(c p) t -> p c t", p=128)

        def chunk_ap(base, cix):
            if isinstance(cix, int):
                return base[:, :, cix * TC:(cix + 1) * TC]
            k, sub = cix
            if sub < 2:
                return base[:, :, bass.ts(k, 2 * TC)][:, :, sub * TC:(sub + 1) * TC]
            return base[:, :, bass.ts(k + 1, 2 * TC)][:, :, 0:TC]

        def mamba_thunks(cix, par):
            T = []
            A = T.append
            xt = xf[par]
            A(lambda: E("s", lambda: nc.scalar.dma_start(out=xt[:], in_=chunk_ap(xr, cix)),
                        w=[xt], dma="x"))
            for c8 in range(8):
                A(lambda c8=c8: E("p", lambda: nc.gpsimd.tensor_copy(
                    x_bf[:, c8, :], xt[:, c8, :]), r=[xt], w=[x_bf]))
            for c0 in range(0, 18, 3):
                def halo(c0=c0):
                    for c in range(c0, min(18, c0 + 3)):
                        E("p", lambda c=c: nc.gpsimd.tensor_copy(
                            xbc[:, c, 0:4], xbc[:, c, TC:TC + 4]), r=[xbc], w=[xbc])
                A(halo)

            def ip_col(j):
                g, jl = j // 3, j % 3
                wcol = 128 if j < 34 else 32
                bank = pb[2 + j % 2]
                slot, pc = RM.need(g)
                for k in range(8):
                    base = k * pc.W + jl * 128
                    E("t", lambda b=bank, sl=slot, base=base, w=wcol, k=k:
                      nc.tensor.matmul(b[0:w, 0:TC], sl[:, base:base + w], x_bf[:, k, :],
                                       start=(k == 0), stop=(k == 7)),
                      r=[slot, x_bf], w=[bank])
                if j < 16:
                    E("s", lambda b=bank, j=j: nc.scalar.activation(
                        siluz[:, j, :], b[:, 0:TC], AF.Silu), r=[bank], w=[siluz])
                elif j < 34:
                    E("s", lambda b=bank, j=j: nc.scalar.copy(
                        xbc[:, j - 16, 4:4 + TC], b[:, 0:TC]), r=[bank], w=[xbc])
                else:
                    E("v", lambda b=bank: nc.vector.tensor_scalar(
                        scr[0:32, 0, :], b[0:32, 0:TC], dtb[:], None, op0=ALU.add),
                        r=[bank, dtb], w=["scr0"])
                    E("s", lambda: nc.scalar.activation(cc[:, :], scr[0:32, 0, :], AF.Abs),
                      r=["scr0"], w=[cc])
                    E("s", lambda: nc.scalar.activation(cc[:, :], cc[:, :], AF.Exp, scale=-1.0),
                      r=[cc], w=[cc])
                    E("s", lambda: nc.scalar.activation(cc[:, :], cc[:, :], AF.Ln, bias=1.0),
                      r=[cc], w=[cc])
                    E("v", lambda: nc.vector.tensor_scalar_max(
                        scr[0:32, 0, :], scr[0:32, 0, :], 0.0), r=["scr0"], w=["scr0"])
                    E("v", lambda: nc.vector.tensor_add(dt[:, :], scr[0:32, 0, :], cc[:, :]),
                      r=["scr0", cc], w=[dt])
            for j in range(35):
                A(lambda j=j: ip_col(j))

            def conv_c(c):
                pcy = c % 2
                E("v", lambda pcy=pcy: nc.vector.tensor_scalar(
                    scr[:, pcy, :], xbc[:, c, 1:1 + TC], cwp[:, c * 4 + 3:c * 4 + 4],
                    cbp[:, c:c + 1], op0=ALU.mult, op1=ALU.add),
                    r=[xbc, cwp, cbp], w=[f"scr{pcy}"])
                for jj in (2, 1, 0):
                    E("v", lambda jj=jj, pcy=pcy: nc.vector.scalar_tensor_tensor(
                        scr[:, pcy, :], xbc[:, c, 4 - jj:4 - jj + TC], cwp[:, c * 4 + jj:c * 4 + jj + 1],
                        scr[:, pcy, :], op0=ALU.mult, op1=ALU.add),
                        r=[xbc, cwp, f"scr{pcy}"], w=[f"scr{pcy}"])
                E("s", lambda pcy=pcy: nc.scalar.activation(
                    xbc[:, c, 4:4 + TC], scr[:, pcy, :], AF.Silu),
                  r=[f"scr{pcy}"], w=[xbc])
            for c in range(18):
                A(lambda c=c: conv_c(c))

            def dt_a():
                E("v", lambda: nc.vector.tensor_scalar(
                    scr[0:32, 0, :], dt[:, :], acol[:], None, op0=ALU.mult),
                    r=[dt, acol], w=["scr0"])
                for q in range(NQ):
                    E("v", lambda q=q: nc.vector.tensor_tensor_scan(
                        cc[:, q * 128:(q + 1) * 128], scr[0:32, 0, q * 128:(q + 1) * 128],
                        zero32[:, 0:128], 0.0, op0=ALU.add, op1=ALU.add),
                        r=["scr0", zero32], w=[cc])
            A(dt_a)

            for q in range(NQ):
                qs0 = q * 128
                xq = 4 + q * 128

                def qstart(q=q, qs0=qs0, xq=xq):
                    E("q", lambda: nc.sync.dma_start(out=cbounce_d, in_=cc[:, qs0:qs0 + 128]),
                      r=[cc], w=["cbounce"], dma="cb")
                    E("q", lambda: nc.sync.dma_start(
                        out=arena_m[0:1, :],
                        in_=cbounce_d[0:16].rearrange("h t -> (h t)")[None, :]),
                        r=["cbounce"], w=[arena_m], dma="ar")
                    E("t", lambda: nc.tensor.matmul(
                        pb[4][:, 0:128], xbc[:, 16, xq:xq + 128], xbc[:, 17, xq:xq + 128],
                        start=True, stop=True), r=[xbc], w=[pb[4]])
                    E("v", lambda: nc.vector.tensor_copy(m_s[:], pb[4][:, 0:128]),
                      r=[pb[4]], w=[m_s])
                    E("t", lambda: nc.tensor.transpose(
                        pb7[:, 0:128], xbc[:, 16, xq:xq + 128], idb[:]), r=[xbc, idb], w=[pb7])
                    E("s", lambda: nc.scalar.copy(btok[:], pb7[:, 0:128]), r=[pb7], w=[btok])
                    E("t", lambda: nc.tensor.transpose(
                        pb[4][:, 128:160], dt[:, qs0:qs0 + 128], idf[0:32, 0:32]),
                        r=[dt, idf], w=[pb[4]])
                    E("v", lambda: nc.vector.tensor_copy(dtT[:], pb[4][:, 128:160]),
                      r=[pb[4]], w=[dtT])
                    E("t", lambda: nc.tensor.transpose(
                        pb[4][:, 160:192], cc[:, qs0:qs0 + 128], idf[0:32, 0:32]),
                        r=[cc, idf], w=[pb[4]])
                    E("v", lambda: nc.vector.tensor_copy(cT[:], pb[4][:, 160:192]),
                      r=[pb[4]], w=[cT])
                    E("s", lambda: nc.scalar.activation(ecT[:], pb[4][:, 160:192], AF.Exp),
                      r=[pb[4]], w=[ecT])
                    E("t", lambda: nc.tensor.transpose(
                        pb[4][0:1, 192:224], cc[:, qs0 + 127:qs0 + 128], idf[0:32, 0:32]),
                        r=[cc, idf], w=[pb[4]])
                    E("s", lambda: nc.scalar.copy(rows[0:1, 0:32], pb[4][0:1, 192:224]),
                      r=[pb[4]], w=[rows])
                    E("s", lambda: nc.scalar.activation(rows[0:1, 32:64], pb[4][0:1, 192:224], AF.Exp),
                      r=[pb[4]], w=[rows])
                    E("t", lambda: nc.tensor.matmul(pb[4][:, 224:288], ones_row[:],
                                                    rows[0:1, 0:64], start=True, stop=True),
                      r=[ones_row, rows], w=[pb[4]])
                    E("v", lambda: nc.vector.tensor_copy(cqb[:], pb[4][:, 224:288]),
                      r=[pb[4]], w=[cqb])
                    E("v", lambda: nc.vector.tensor_sub(xdf[:], cqb[:, 0:32], cT[:]),
                      r=[cqb, cT], w=[xdf])
                    E("s", lambda: nc.scalar.activation(xdf[:], xdf[:], AF.Exp), r=[xdf], w=[xdf])
                A(qstart)

                def arena2(q=q):
                    E("q", lambda: nc.sync.dma_start(
                        out=arena_m[0:1, :],
                        in_=cbounce_d[16:32].rearrange("h t -> (h t)")[None, :]),
                        r=["cbounce"], w=[arena_m], dma="ar")

                def tCB(g4):
                    off = ((g4 * 4) % 16) * 128
                    E("t", lambda: nc.tensor.matmul(
                        pb[5][:, 0:512], ones_row[:], arena_m[0:1, off:off + 512],
                        start=True, stop=True), r=[ones_row, arena_m], w=[pb[5]])

                def tA(h, xq=xq):
                    hp = h % 4
                    hc, ho = h // 2, (h % 2) * 64
                    E("v", lambda: nc.vector.scalar_tensor_tensor(
                        dl[:], pb[5][:, (h % 4) * 128:(h % 4 + 1) * 128], cT[:, h:h + 1], mask[:],
                        op0=ALU.subtract, op1=ALU.add), r=[pb[5], cT, mask], w=[dl])
                    E("s", lambda: nc.scalar.activation(lp2[:, hp, :], dl[:], AF.Exp),
                      r=[dl], w=[lp2])
                    E("v", lambda: nc.vector.tensor_mul(lp2[:, hp, :], lp2[:, hp, :], m_s[:]),
                      r=[lp2, m_s], w=[lp2])
                    E("t", lambda: nc.tensor.transpose(
                        pb6[:, 0:64], xbc[ho:ho + 64, hc, xq:xq + 128],
                        idb[ho:ho + 64, ho:ho + 64]), r=[xbc, idb], w=[pb6])
                    E("s", lambda: nc.scalar.mul(xdt2[:, hp, :], pb6[:, 0:64], dtT[:, h:h + 1]),
                      r=[pb6, dtT], w=[xdt2])
                    E("v", lambda: nc.vector.tensor_scalar(
                        xdw2[:, hp, :], xdt2[:, hp, :], xdf[:, h:h + 1], None, op0=ALU.mult),
                        r=[xdt2, xdf], w=[xdw2])

                def tB(h, xq=xq):
                    hp = h % 4
                    E("t", lambda: nc.tensor.matmul(pb[2][:, 0:64], lp2[:, hp, :],
                                                    xdt2[:, hp, :], start=True, stop=True),
                      r=[lp2, xdt2], w=[pb[2]])
                    E("t", lambda: nc.tensor.matmul(
                        pb[3][:, 0:64], xbc[:, 17, xq:xq + 128], st[:, h * 64:(h + 1) * 64],
                        start=True, stop=True), r=[xbc, st], w=[pb[3]])
                    E("s", lambda: nc.scalar.mul(ytmp[:, h % 2, :], pb[3][:, 0:64], ecT[:, h:h + 1]),
                      r=[pb[3], ecT], w=[ytmp])
                    E("v", lambda: nc.vector.tensor_add(
                        ytok2[:, hp, :], pb[2][:, 0:64], ytmp[:, h % 2, :]),
                      r=[pb[2], ytmp], w=[ytok2])
                    E("t", lambda: nc.tensor.matmul(pb[4][:, 0:64], btok[:], xdw2[:, hp, :],
                                                    start=True, stop=True),
                      r=[btok, xdw2], w=[pb[4]])
                    E("v", lambda: nc.vector.scalar_tensor_tensor(
                        st[:, h * 64:(h + 1) * 64], st[:, h * 64:(h + 1) * 64],
                        cqb[:, 32 + h:33 + h], pb[4][:, 0:64], op0=ALU.mult, op1=ALU.add),
                        r=[st, cqb, pb[4]], w=[st])

                def tC(h, qs0=qs0, xq=xq):
                    hp = h % 4
                    hc, ho = h // 2, (h % 2) * 64
                    E("t", lambda: nc.tensor.transpose(pb7[0:64, 128:256], ytok2[:, hp, :], idb[:]),
                      r=[ytok2, idb], w=[pb7])
                    E("v", lambda: nc.vector.scalar_tensor_tensor(
                        y[ho:ho + 64, hc, qs0:qs0 + 128], xbc[ho:ho + 64, hc, xq:xq + 128],
                        dcol[ho:ho + 64, hc:hc + 1], pb7[0:64, 128:256],
                        op0=ALU.mult, op1=ALU.add), r=[xbc, dcol, pb7], w=[y])

                for h in range(NH + 5):
                    if h < NH:
                        if h % 4 == 0:
                            A(lambda g4=h // 4, tCB=tCB: tCB(g4))
                        A(lambda h=h, tA=tA: tA(h))
                    if h == 15:
                        A(arena2)
                    if 3 <= h < NH + 3:
                        A(lambda h=h, tB=tB: tB(h - 3))
                    if h >= 5:
                        A(lambda h=h, tC=tC: tC(h - 5))

            def gat_c(c):
                E("v", lambda: nc.vector.tensor_mul(y[:, c, :], y[:, c, :], siluz[:, c, :]),
                  r=[y, siluz], w=[y])
                E("s", lambda: nc.scalar.activation(scrb[:, c % 2, :], y[:, c, :], AF.Square),
                  r=[y], w=[scrb])
                E("t", lambda: nc.tensor.matmul(
                    pb[5][0:1, 0:TC], ones_col[:], scrb[:, c % 2, :],
                    start=(c == 0), stop=(c == 15)), r=[ones_col, scrb], w=[pb[5]])
            for c in range(16):
                A(lambda c=c: gat_c(c))

            def grms():
                E("v", lambda: nc.vector.tensor_scalar(
                    arena_m[0:1, 0:TC], pb[5][0:1, 0:TC], 1.0 / D_INNER, MRMS_EPS,
                    op0=ALU.mult, op1=ALU.add), r=[pb[5]], w=[arena_m])
                E("s", lambda: nc.scalar.activation(arena_m[0:1, 0:TC], arena_m[0:1, 0:TC], AF.Ln),
                  r=[arena_m], w=[arena_m])
                E("v", lambda: nc.vector.tensor_scalar_mul(
                    arena_m[0:1, 0:TC], arena_m[0:1, 0:TC], -0.5), r=[arena_m], w=[arena_m])
                E("s", lambda: nc.scalar.activation(arena_m[0:1, 0:TC], arena_m[0:1, 0:TC], AF.Exp),
                  r=[arena_m], w=[arena_m])
                E("t", lambda: nc.tensor.matmul(pb[5][:, 0:TC], ones_row[:], arena_m[0:1, 0:TC],
                                                start=True, stop=True),
                  r=[ones_row, arena_m], w=[pb[5]])
            A(grms)
            for c0 in range(0, 16, 4):
                def gapply(c0=c0):
                    for c in range(c0, c0 + 4):
                        E("v", lambda c=c: nc.vector.tensor_mul(
                            y[:, c, :], y[:, c, :], pb[5][:, 0:TC]), r=[y, pb[5]], w=[y])
                A(gapply)

            def op_col(o):
                slot, pc = RM.need(12 + o)
                bank = pb[2 + o % 2]
                for k in range(16):
                    base = k * pc.W
                    E("t", lambda b=bank, sl=slot, base=base, k=k:
                      nc.tensor.matmul(b[:, 0:TC], sl[:, base:base + 128], y[:, k, :],
                                       start=(k == 0), stop=(k == 15)),
                      r=[slot, y], w=[bank])
                E("v", lambda b=bank, o=o: nc.vector.tensor_add(
                    xt[:, o, :], xt[:, o, :], b[:, 0:TC]), r=[bank, xt], w=[xt])
            for o in range(8):
                A(lambda o=o: op_col(o))

            def ln1_c(c):
                E("s", lambda: nc.scalar.copy(x_bf[:, c, :], xt[:, c, :]), r=[xt], w=[x_bf])
                E("t", lambda: nc.tensor.matmul(
                    pb[4][0:1, 0:TC], ones_col[:], x_bf[:, c, :],
                    start=(c == 0), stop=(c == 7)), r=[ones_col, x_bf], w=[pb[4]])
                E("s", lambda: nc.scalar.activation(scrb[:, c % 2, :], x_bf[:, c, :], AF.Square),
                  r=[x_bf], w=[scrb])
                E("t", lambda: nc.tensor.matmul(
                    pb[5][0:1, 0:TC], ones_col[:], scrb[:, c % 2, :],
                    start=(c == 0), stop=(c == 7)), r=[ones_col, scrb], w=[pb[5]])
            for c in range(8):
                A(lambda c=c: ln1_c(c))

            def ln1_rows():
                E("v", lambda: nc.vector.tensor_scalar(
                    arena_m[0:1, 0:TC], pb[4][0:1, 0:TC], 1.0 / D_MODEL, None, op0=ALU.mult),
                    r=[pb[4]], w=[arena_m])
                E("v", lambda: nc.vector.tensor_scalar(
                    arena_m[0:1, TC:2 * TC], pb[5][0:1, 0:TC], 1.0 / D_MODEL, LN_EPS,
                    op0=ALU.mult, op1=ALU.add), r=[pb[5]], w=[arena_m])
                E("s", lambda: nc.scalar.activation(
                    scr[0:1, 0, 0:TC], arena_m[0:1, 0:TC], AF.Square), r=[arena_m], w=["scr0"])
                E("v", lambda: nc.vector.tensor_sub(
                    arena_m[0:1, TC:2 * TC], arena_m[0:1, TC:2 * TC], scr[0:1, 0, 0:TC]),
                    r=[arena_m, "scr0"], w=[arena_m])
                E("s", lambda: nc.scalar.activation(
                    arena_m[0:1, TC:2 * TC], arena_m[0:1, TC:2 * TC], AF.Ln),
                    r=[arena_m], w=[arena_m])
                E("v", lambda: nc.vector.tensor_scalar_mul(
                    arena_m[0:1, TC:2 * TC], arena_m[0:1, TC:2 * TC], -0.5),
                    r=[arena_m], w=[arena_m])
                E("s", lambda: nc.scalar.activation(
                    arena_m[0:1, TC:2 * TC], arena_m[0:1, TC:2 * TC], AF.Exp),
                    r=[arena_m], w=[arena_m])
                E("v", lambda: nc.vector.tensor_mul(
                    arena_m[0:1, 0:TC], arena_m[0:1, 0:TC], arena_m[0:1, TC:2 * TC]),
                    r=[arena_m], w=[arena_m])
                E("t", lambda: nc.tensor.matmul(pb[2][:, 0:TC], ones_row[:],
                                                arena_m[0:1, TC:2 * TC], start=True, stop=True),
                  r=[ones_row, arena_m], w=[pb[2]])
                E("t", lambda: nc.tensor.matmul(pb[3][:, 0:TC], ones_row[:],
                                                arena_m[0:1, 0:TC], start=True, stop=True),
                  r=[ones_row, arena_m], w=[pb[3]])
            A(ln1_rows)
            for c0 in range(0, 8, 2):
                def ln1_apply(c0=c0):
                    for c in range(c0, c0 + 2):
                        E("v", lambda c=c: nc.vector.tensor_mul(
                            scr[:, c % 2, :], xt[:, c, :], pb[2][:, 0:TC]),
                            r=[xt, pb[2]], w=[f"scr{c % 2}"])
                        E("v", lambda c=c: nc.vector.tensor_sub(
                            xt[:, c, :], scr[:, c % 2, :], pb[3][:, 0:TC]),
                            r=[f"scr{c % 2}", pb[3]], w=[xt])
                A(ln1_apply)
            return T

        def emit_ffn(cix, par, pump):
            if pump is None:
                pump = lambda: None
            xt = xf[par]
            for g4 in range(4):
                for k in range(8):
                    E("t", lambda g4=g4, k=k: nc.tensor.matmul(
                        pb[0][:, g4 * 14:(g4 + 1) * 14], xt[:, k, g4 * 128:(g4 + 1) * 128],
                        rwp[:, k, :], start=(k == 0), stop=(k == 7)),
                        r=[xt, rwp], w=[pb[0]])
                pump()
            for g4 in range(4):
                E("v", lambda g4=g4: nc.vector.reduce_max(
                    r1[:], pb[0][:, g4 * 14:(g4 + 1) * 14], axis=AX.X, negate=True),
                    r=[pb[0]], w=[r1])
                E("s", lambda g4=g4: nc.scalar.activation(
                    gts[:], pb[0][:, g4 * 14:(g4 + 1) * 14], AF.Exp, bias=r1[:]),
                    r=[pb[0], r1], w=[gts])
                E("v", lambda: nc.vector.reduce_sum(rq[:], gts[:], axis=AX.X), r=[gts], w=[rq])
                E("v", lambda: nc.vector.reciprocal(rq[:], rq[:]), r=[rq], w=[rq])
                E("v", lambda: nc.vector.tensor_scalar(gts[:], gts[:], rq[:], None, op0=ALU.mult),
                  r=[gts, rq], w=[gts])
                E("v", lambda: nc.vector.reduce_max(r1[:], gts[:], axis=AX.X), r=[gts], w=[r1])
                E("v", lambda: nc.vector.tensor_scalar(wtmp[:], gts[:], r1[:], None, op0=ALU.is_ge),
                  r=[gts, r1], w=[wtmp])
                E("v", lambda: nc.vector.scalar_tensor_tensor(
                    wtmp[:], wtmp[:], -1e30, gts[:], op0=ALU.mult, op1=ALU.add),
                    r=[wtmp, gts], w=[wtmp])
                E("v", lambda: nc.vector.reduce_max(r2[:], wtmp[:], axis=AX.X), r=[wtmp], w=[r2])
                E("v", lambda: nc.vector.tensor_scalar(wtmp[:], gts[:], r2[:], None, op0=ALU.is_ge),
                  r=[gts, r2], w=[wtmp])
                E("v", lambda: nc.vector.tensor_mul(wtmp[:], gts[:], wtmp[:]),
                  r=[gts, wtmp], w=[wtmp])
                E("v", lambda: nc.vector.reduce_sum(rq[:], wtmp[:], axis=AX.X), r=[wtmp], w=[rq])
                E("v", lambda: nc.vector.tensor_scalar_add(rq[:], rq[:], GATE_EPS), r=[rq], w=[rq])
                E("v", lambda: nc.vector.reciprocal(rq[:], rq[:]), r=[rq], w=[rq])
                E("v", lambda: nc.vector.tensor_scalar(
                    w16[:, 0:14], wtmp[:], rq[:], None, op0=ALU.mult), r=[wtmp, rq], w=[w16])
                E("t", lambda: nc.tensor.transpose(pb[1][0:16, 0:128], w16[:], idf[:]),
                  r=[w16, idf], w=[pb[1]])
                E("v", lambda g4=g4: nc.vector.tensor_copy(
                    gem[:, g4 * 128:(g4 + 1) * 128], pb[1][0:16, 0:128]), r=[pb[1]], w=[gem])
                pump()
            E("q", lambda: nc.sync.dma_start(out=gemb_d, in_=gem[:]),
              r=[gem], w=["gemb"], dma="gemb")
            for c in range(8):
                E("s", lambda c=c: nc.scalar.activation(
                    fb_bf[:, c % 2, :], xt[:, c, :], AF.Square), r=[xt], w=[fb_bf])
                E("t", lambda c=c: nc.tensor.matmul(
                    pb[1][0:1, 0:TC], ones_col[:], fb_bf[:, c % 2, :],
                    start=(c == 0), stop=(c == 7)), r=[ones_col, fb_bf], w=[pb[1]])
            E("v", lambda: nc.vector.tensor_scalar(
                arena_f[0:1, 0:TC], pb[1][0:1, 0:TC], 1.0 / D_MODEL, FRMS_EPS,
                op0=ALU.mult, op1=ALU.add), r=[pb[1]], w=[arena_f])
            E("s", lambda: nc.scalar.activation(arena_f[0:1, 0:TC], arena_f[0:1, 0:TC], AF.Ln),
              r=[arena_f], w=[arena_f])
            E("v", lambda: nc.vector.tensor_scalar_mul(
                arena_f[0:1, 0:TC], arena_f[0:1, 0:TC], -0.5), r=[arena_f], w=[arena_f])
            E("s", lambda: nc.scalar.activation(arena_f[0:1, 0:TC], arena_f[0:1, 0:TC], AF.Exp),
              r=[arena_f], w=[arena_f])
            E("t", lambda: nc.tensor.matmul(pb[0][:, 0:TC], ones_row[:], arena_f[0:1, 0:TC],
                                            start=True, stop=True),
              r=[ones_row, arena_f], w=[pb[0]])
            pump()
            for c in range(8):
                E("v", lambda c=c: nc.vector.tensor_mul(
                    xhat0[:, c, :], xt[:, c, :], pb[0][:, 0:TC]), r=[xt, pb[0]], w=[xhat0])
            pump()

            for e in range(NE):
                if e < NE_R and e % 2 == 0:
                    E("q", lambda e=e: nc.sync.dma_start(
                        out=arena_f[0:1, 0:2 * TC],
                        in_=gemb_d[e:e + 2, :].rearrange("a t -> (a t)")[None, :]),
                        r=["gemb"], w=[arena_f], dma="ar2")
                if e < NE_R:
                    E("t", lambda e=e: nc.tensor.matmul(
                        pb[1][:, 0:TC], ones_row[:],
                        arena_f[0:1, (e % 2) * TC:(e % 2 + 1) * TC],
                        start=True, stop=True), r=[ones_row, arena_f], w=[pb[1]])
                    E("s", lambda: nc.scalar.copy(wg[:], pb[1][:, 0:TC]), r=[pb[1]], w=[wg])
                p1a = 8 * e
                for hcol in range(16):
                    slot, pc = RF.need(p1a + hcol // 4)
                    hl = hcol % 4
                    bank = pb[hcol % 2]
                    for k in range(8):
                        base = k * pc.W + hl * 128
                        E("t", lambda b=bank, sl=slot, base=base, k=k:
                          nc.tensor.matmul(b[:, 0:TC], sl[:, base:base + 128],
                                           xhat0[:, k, :], start=(k == 0), stop=(k == 7)),
                          r=[slot, xhat0], w=[bank])
                    E("s", lambda b=bank, hcol=hcol: nc.scalar.activation(
                        he[:, hcol, :], b[:, 0:TC], AF.Silu), r=[bank], w=[f"he{hcol}"])
                    if e < NE_R:
                        E("v", lambda hcol=hcol: nc.vector.tensor_mul(
                            he[:, hcol, :], he[:, hcol, :], wg[:]),
                            r=[f"he{hcol}", wg], w=[f"he{hcol}"])
                    pump()
                p2a = 8 * e + 4
                for o in range(8):
                    slot, pc = RF.need(p2a + o // 2)
                    ol = o % 2
                    bank = pb[o % 2]
                    for k in range(16):
                        base = k * pc.W + ol * 128
                        E("t", lambda b=bank, sl=slot, base=base, k=k:
                          nc.tensor.matmul(b[:, 0:TC], sl[:, base:base + 128],
                                           he[:, k, :], start=(k == 0), stop=(k == 15)),
                          r=[slot, f"he{k}"], w=[bank])
                    E("v", lambda b=bank, o=o: nc.vector.tensor_add(
                        xt[:, o, :], xt[:, o, :], b[:, 0:TC]), r=[bank, xt], w=[xt])
                    pump()

            for c in range(8):
                E("s", lambda c=c: nc.scalar.copy(fb_bf[:, c, :], xt[:, c, :]), r=[xt], w=[fb_bf])
                E("t", lambda c=c: nc.tensor.matmul(
                    pb[0][0:1, 0:TC], ones_col[:], fb_bf[:, c, :],
                    start=(c == 0), stop=(c == 7)), r=[ones_col, fb_bf], w=[pb[0]])
                E("s", lambda c=c: nc.scalar.activation(
                    xhat0[:, c % 2, :], fb_bf[:, c, :], AF.Square), r=[fb_bf], w=[xhat0])
                E("t", lambda c=c: nc.tensor.matmul(
                    pb[1][0:1, 0:TC], ones_col[:], xhat0[:, c % 2, :],
                    start=(c == 0), stop=(c == 7)), r=[ones_col, xhat0], w=[pb[1]])
            pump()
            E("v", lambda: nc.vector.tensor_scalar(
                arena_f[0:1, 0:TC], pb[0][0:1, 0:TC], 1.0 / D_MODEL, None, op0=ALU.mult),
                r=[pb[0]], w=[arena_f])
            E("v", lambda: nc.vector.tensor_scalar(
                arena_f[0:1, TC:2 * TC], pb[1][0:1, 0:TC], 1.0 / D_MODEL, LN_EPS,
                op0=ALU.mult, op1=ALU.add), r=[pb[1]], w=[arena_f])
            E("s", lambda: nc.scalar.activation(
                scrF()[0:1, 0:TC], arena_f[0:1, 0:TC], AF.Square),
                r=[arena_f, "he0", "he1"], w=["he0", "he1"])
            E("v", lambda: nc.vector.tensor_sub(
                arena_f[0:1, TC:2 * TC], arena_f[0:1, TC:2 * TC], scrF()[0:1, 0:TC]),
                r=[arena_f, "he0"], w=[arena_f])
            E("s", lambda: nc.scalar.activation(
                arena_f[0:1, TC:2 * TC], arena_f[0:1, TC:2 * TC], AF.Ln), r=[arena_f], w=[arena_f])
            E("v", lambda: nc.vector.tensor_scalar_mul(
                arena_f[0:1, TC:2 * TC], arena_f[0:1, TC:2 * TC], -0.5), r=[arena_f], w=[arena_f])
            E("s", lambda: nc.scalar.activation(
                arena_f[0:1, TC:2 * TC], arena_f[0:1, TC:2 * TC], AF.Exp), r=[arena_f], w=[arena_f])
            E("v", lambda: nc.vector.tensor_mul(
                arena_f[0:1, 0:TC], arena_f[0:1, 0:TC], arena_f[0:1, TC:2 * TC]),
                r=[arena_f], w=[arena_f])
            E("t", lambda: nc.tensor.matmul(pb[0][:, 0:TC], ones_row[:], arena_f[0:1, TC:2 * TC],
                                            start=True, stop=True), r=[ones_row, arena_f], w=[pb[0]])
            E("t", lambda: nc.tensor.matmul(pb[1][:, 0:TC], ones_row[:], arena_f[0:1, 0:TC],
                                            start=True, stop=True), r=[ones_row, arena_f], w=[pb[1]])
            for c in range(8):
                E("v", lambda c=c: nc.vector.tensor_mul(
                    scrF()[:, 0:TC], xt[:, c, :], pb[0][:, 0:TC]),
                    r=[xt, pb[0], "he0", "he1"], w=["he0", "he1"])
                E("v", lambda c=c: nc.vector.tensor_sub(
                    xt[:, c, :], scrF()[:, 0:TC], pb[1][:, 0:TC]),
                    r=["he0", pb[1]], w=[xt])
            pump()
            E("s", lambda: nc.scalar.dma_start(out=chunk_ap(outr, cix), in_=xt[:]),
              r=[xt], dma="out")

        def run_pipelined(f_cix, f_par, m_cix, m_par):
            RM.reset()
            RF.reset()
            thunks = mamba_thunks(m_cix, m_par) if m_cix is not None else []
            pos = [0.0]
            idx = [0]
            per = (len(thunks) / 420.0) if (thunks and f_cix is not None) else 0.0

            def pump():
                pos[0] += per
                kb.stream = "M"
                while idx[0] < len(thunks) and idx[0] < pos[0]:
                    thunks[idx[0]]()
                    idx[0] += 1
                kb.stream = "F"
            if f_cix is not None:
                kb.stream = "F"
                emit_ffn(f_cix, f_par, pump if thunks else None)
            kb.stream = "M"
            while idx[0] < len(thunks):
                thunks[idx[0]]()
                idx[0] += 1
            kb.stream = "F"
            kb.fence()

        if n_chunks == NCHUNK:
            run_pipelined(None, 0, 0, 0)
            with nc.Fori(0, 7) as k:
                run_pipelined((k, 0), 0, (k, 1), 1)
                run_pipelined((k, 1), 1, (k, 2), 0)
            run_pipelined(14, 0, 15, 1)
            run_pipelined(15, 1, None, 0)
        else:
            run_pipelined(None, 0, 0, 0)
            for c in range(n_chunks):
                run_pipelined(c, c % 2, c + 1 if c + 1 < n_chunks else None, (c + 1) % 2)

        if dbg:
            last_par = (n_chunks - 1) % 2
            dumps = [("d_siluz", siluz[:].rearrange("p a b -> p (a b)")),
                     ("d_xbc", xbc[:].rearrange("p a b -> p (a b)")),
                     ("d_dt", dt[:]), ("d_cc", cc[:]),
                     ("d_y", y[:].rearrange("p a b -> p (a b)")),
                     ("d_x1", xf[last_par][:].rearrange("p a b -> p (a b)")),
                     ("d_gem", gem[:]),
                     ("d_he", he[:].rearrange("p a b -> p (a b)")),
                     ("d_xh", xhat0[:].rearrange("p a b -> p (a b)")),
                     ("d_s", st[:])]
            for nm, ap in dumps:
                E("p", lambda nm=nm, ap=ap: nc.gpsimd.dma_start(
                    out=dbg_outs[nm], in_=ap), r=[], dma="dbg")
            kb.fence()

    return nc


def _host_packs(inputs):
    f = np.float32
    conv_w = np.asarray(inputs["L0_attn_conv_w"], f)
    conv_b = np.asarray(inputs["L0_attn_conv_b"], f)
    return {
        "ipw": np.ascontiguousarray(np.asarray(inputs["L0_attn_in_proj_w"], f)),
        "opw": np.ascontiguousarray(np.asarray(inputs["L0_attn_out_proj_w"], f)),
        "w1": np.ascontiguousarray(np.asarray(inputs["L0_ffn_w1"], f)),
        "w2": np.ascontiguousarray(np.asarray(inputs["L0_ffn_w2"], f)),
        "w1s": np.ascontiguousarray(np.asarray(inputs["L0_ffn_w1_s"], f)),
        "w2s": np.ascontiguousarray(np.asarray(inputs["L0_ffn_w2_s"], f)),
        "cwp": np.ascontiguousarray(conv_w.reshape(4, 18, 128).transpose(2, 1, 0).reshape(128, 72)),
        "cbp": np.ascontiguousarray(conv_b.reshape(18, 128).T),
        "nwp": np.ascontiguousarray(np.asarray(inputs["L0_attn_norm_w"], f).reshape(16, 128).T),
        "rmsp": np.ascontiguousarray(
            np.asarray(inputs["L0_ffn_rms_w"], f).reshape(14, 8, 128).transpose(2, 0, 1).reshape(128, 112)),
        "rmssp": np.ascontiguousarray(
            np.asarray(inputs["L0_ffn_rms_w_s"], f).reshape(2, 8, 128).transpose(2, 0, 1).reshape(128, 16)),
        "dcol": np.ascontiguousarray(
            np.repeat(np.asarray(inputs["L0_attn_D"], f), 64).reshape(16, 128).T),
        "dtb": np.ascontiguousarray(np.asarray(inputs["L0_attn_dt_bias"], f).reshape(32, 1)),
        "alog": np.ascontiguousarray(np.asarray(inputs["L0_attn_A_log"], f).reshape(32, 1)),
        "rwp": np.ascontiguousarray(
            np.asarray(inputs["L0_ffn_router_w"], f).reshape(8, 128, 14).transpose(1, 0, 2).reshape(128, 112)),
        "idf": np.eye(128, dtype=f),
        "idb": np.eye(128, dtype=ml_dtypes.bfloat16),
        "mask": np.where(np.arange(128)[None, :] >= np.arange(128)[:, None],
                         np.float32(0.0), np.float32(MASK_NEG)),
    }


def kernel(**inputs):
    x = np.asarray(inputs["L0_attn_x"], np.float32)
    packs = _host_packs(inputs)
    nc = build_program(NCHUNK, dbg=False)
    in_maps = []
    for b in range(8):
        m = dict(packs)
        m["x"] = np.ascontiguousarray(x[b].T)
        in_maps.append(m)
    res = run_bass_kernel_spmd(nc, in_maps, core_ids=list(range(8)))
    return np.stack([np.ascontiguousarray(r["out"].T) for r in res.results], axis=0)


# version 13
# speedup vs baseline: 1.2833x; 1.0742x over previous
import numpy as np
import ml_dtypes
from contextlib import ExitStack

import concourse.bass as bass
import concourse.mybir as mybir
from concourse.bass_utils import run_bass_kernel_spmd

F32 = mybir.dt.float32
BF16 = mybir.dt.bfloat16
AF = mybir.ActivationFunctionType
ALU = mybir.AluOpType
AX = mybir.AxisListType

D_MODEL = 1024
D_INNER = 2048
NH = 32
HD = 64
DIP = 4384
HIDDEN = 2048
NE_R = 14
NE = 16
TC = 512
NQ = TC // 128
SEQ = 8192
NCHUNK = SEQ // TC

LN_EPS = 1e-5
MRMS_EPS = 1e-5
FRMS_EPS = 1.1920929e-07
GATE_EPS = 1e-9
MASK_NEG = -30000.0

NSLOT_F = 3
FSLOT_COLS = 4096
NSLOT_M = 2
MSLOT_COLS = 3072


class KB:

    COMPUTE = ("t", "v", "s", "p")

    def __init__(self, nc, ctx):
        self.nc = nc
        self.ctx = ctx
        self.eng = {"t": nc.tensor, "v": nc.vector, "s": nc.scalar,
                    "p": nc.gpsimd, "q": nc.sync}
        self.chain = {e: [ctx.enter_context(nc.semaphore(f"ch_{e}")), 0]
                      for e in self.COMPUTE}
        self.dsem = {}
        self.latest = {e: None for e in self.COMPUTE}
        self.eager = False
        self.last_w = {}
        self.readers = {}
        self.waited = {e: {} for e in self.eng}

    @staticmethod
    def _k(h):
        return h if isinstance(h, str) else h.name

    def _edge(self, rec, ceng):
        if rec is None:
            return
        peng = rec["eng"]
        if rec["dma"]:
            sem, val = rec["sem"], rec["val"]
        else:
            if peng == "t" and ceng == "t":
                return
            tgt = rec if rec["sem"] is not None else self.latest[peng]
            if tgt["sem"] is None:
                cs = self.chain[peng]
                cs[1] += 1
                tgt["ins"].then_inc(cs[0], 1)
                tgt["sem"], tgt["val"] = cs[0], cs[1]
            sem, val = tgt["sem"], tgt["val"]
        w = self.waited[ceng]
        if w.get(sem, 0) < val:
            self.eng[ceng].wait_ge(sem, val)
            w[sem] = val

    def emit(self, ceng, fn, r=(), w=(), dma=False):
        for h in r:
            self._edge(self.last_w.get(self._k(h)), ceng)
        for h in w:
            k = self._k(h)
            self._edge(self.last_w.get(k), ceng)
            for rec in self.readers.get(k, {}).values():
                self._edge(rec, ceng)
        ins = fn()
        rec = {"eng": ceng, "ins": ins, "sem": None, "val": None, "dma": bool(dma)}
        if not dma and self.eager:
            cs = self.chain[ceng]
            cs[1] += 1
            ins.then_inc(cs[0], 1)
            rec["sem"], rec["val"] = cs[0], cs[1]
        if dma:
            if dma not in self.dsem:
                self.dsem[dma] = [self.ctx.enter_context(self.nc.semaphore(f"ds_{dma}")), 0]
            ds = self.dsem[dma]
            ds[1] += 16
            ins.then_inc(ds[0], 16)
            rec["sem"], rec["val"] = ds[0], ds[1]
        else:
            self.latest[ceng] = rec
        for h in w:
            k = self._k(h)
            self.last_w[k] = rec
            self.readers[k] = {}
        for h in r:
            self.readers.setdefault(self._k(h), {})[ceng] = rec
        return ins

    def fence(self):
        nc = self.nc
        for role, (sem, cnt) in self.dsem.items():
            if cnt:
                nc.sync.wait_ge(sem, cnt)
        for e in self.COMPUTE:
            sem, cnt = self.chain[e]
            if cnt:
                self.eng[e].wait_ge(sem, cnt)
        nc.all_engine_barrier()
        for e in self.COMPUTE:
            nc.sync.sem_clear(self.chain[e][0])
        for role, (sem, cnt) in self.dsem.items():
            nc.sync.sem_clear(sem)
        nc.all_engine_barrier()
        for e in self.COMPUTE:
            self.chain[e][1] = 0
            self.latest[e] = None
        for role in self.dsem:
            self.dsem[role][1] = 0
        self.last_w = {}
        self.readers = {}
        self.waited = {e: {} for e in self.eng}


class Piece:
    def __init__(self, src, K, W, off, scales=None):
        self.src = src
        self.K = K
        self.W = W
        self.off = off
        self.scales = scales


class Ring:

    def __init__(self, kb, nc, wscr, slots, pieces, role):
        self.kb, self.nc, self.wscr = kb, nc, wscr
        self.slots = slots
        self.pieces = pieces
        self.role = role
        self.nxt = 0

    def reset(self):
        self.nxt = 0

    def need(self, i):
        while self.nxt <= min(i + len(self.slots) - 1, len(self.pieces) - 1):
            pc = self.pieces[self.nxt]
            slot = self.slots[self.nxt % len(self.slots)]
            self.kb.emit("q", lambda sl=slot, pc=pc: self.nc.sync.dma_start(
                out=sl[:, 0:pc.K * pc.W], in_=self.wscr[:, pc.off:pc.off + pc.K * pc.W]),
                r=["wscr"], w=[slot], dma=f"{self.role}{self.nxt % len(self.slots)}")
            self.nxt += 1
        return self.slots[i % len(self.slots)], self.pieces[i]


def build_program(n_chunks=NCHUNK, dbg=False):
    nc = bass.Bass("TRN2", target_bir_lowering=False, debug=False)

    x_d = nc.dram_tensor("x", [D_MODEL, SEQ], F32, kind="ExternalInput").ap()
    ipw_d = nc.dram_tensor("ipw", [D_MODEL, DIP], F32, kind="ExternalInput").ap()
    opw_d = nc.dram_tensor("opw", [D_INNER, D_MODEL], F32, kind="ExternalInput").ap()
    w1_d = nc.dram_tensor("w1", [NE_R, D_MODEL, HIDDEN], F32, kind="ExternalInput").ap()
    w2_d = nc.dram_tensor("w2", [NE_R, HIDDEN, D_MODEL], F32, kind="ExternalInput").ap()
    w1s_d = nc.dram_tensor("w1s", [2, D_MODEL, HIDDEN], F32, kind="ExternalInput").ap()
    w2s_d = nc.dram_tensor("w2s", [2, HIDDEN, D_MODEL], F32, kind="ExternalInput").ap()
    cwp_d = nc.dram_tensor("cwp", [128, 72], F32, kind="ExternalInput").ap()
    cbp_d = nc.dram_tensor("cbp", [128, 18], F32, kind="ExternalInput").ap()
    nwp_d = nc.dram_tensor("nwp", [128, 16], F32, kind="ExternalInput").ap()
    rmsp_d = nc.dram_tensor("rmsp", [128, 112], F32, kind="ExternalInput").ap()
    rmssp_d = nc.dram_tensor("rmssp", [128, 16], F32, kind="ExternalInput").ap()
    dcol_d = nc.dram_tensor("dcol", [128, 16], F32, kind="ExternalInput").ap()
    dtb_d = nc.dram_tensor("dtb", [32, 1], F32, kind="ExternalInput").ap()
    alog_d = nc.dram_tensor("alog", [32, 1], F32, kind="ExternalInput").ap()
    rwp_d = nc.dram_tensor("rwp", [128, 112], F32, kind="ExternalInput").ap()
    idf_d = nc.dram_tensor("idf", [128, 128], F32, kind="ExternalInput").ap()
    idb_d = nc.dram_tensor("idb", [128, 128], BF16, kind="ExternalInput").ap()
    mask_d = nc.dram_tensor("mask", [128, 128], F32, kind="ExternalInput").ap()
    out_d = nc.dram_tensor("out", [D_MODEL, SEQ], F32, kind="ExternalOutput").ap()

    total_cols = (11 * 3072 + 1280 + 8 * 2048) + NE * 8 * 4096
    wscr_d = nc.dram_tensor("wscr", [128, total_cols], BF16, kind="Internal").ap()
    cbounce_d = nc.dram_tensor("cbounce", [32, 128], F32, kind="Internal").ap()
    gemb_d = nc.dram_tensor("gemb", [16, TC], F32, kind="Internal").ap()

    dbg_outs = {}
    if dbg:
        for nm, shp in [("d_siluz", [128, 16 * TC]), ("d_xbc", [128, 18 * (TC + 4)]),
                        ("d_dt", [32, TC]), ("d_cc", [32, TC]), ("d_y", [128, 16 * TC]),
                        ("d_x1", [128, 8 * TC]), ("d_gem", [16, TC]),
                        ("d_he", [128, 16 * TC]), ("d_xh", [128, 8 * TC]),
                        ("d_s", [128, NH * HD])]:
            dbg_outs[nm] = nc.dram_tensor(nm, shp, F32, kind="ExternalOutput").ap()

    with ExitStack() as ctx:
        S = lambda *a: ctx.enter_context(nc.sbuf_tensor(*a))
        P = lambda *a: ctx.enter_context(nc.psum_tensor(*a))

        fring = [S(f"fr{i}", [128, FSLOT_COLS], BF16) for i in range(NSLOT_F)]
        mring = [S(f"mr{i}", [128, MSLOT_COLS], BF16) for i in range(NSLOT_M)]
        xf = [S("xf0", [128, 8, TC], F32), S("xf1", [128, 8, TC], F32)]
        x_bf = S("x_bf", [128, 8, TC], BF16)
        fb_bf = S("fb_bf", [128, 8, TC], BF16)
        xhat0 = S("xhat0", [128, 8, TC], BF16)
        siluz = S("siluz", [128, 16, TC], BF16)
        xbc = S("xbc", [128, 18, TC + 4], BF16)
        y = S("y", [128, 16, TC], BF16)
        he = S("he", [128, 16, TC], BF16)
        dt = S("dt", [32, TC], F32)
        cc = S("cc", [32, TC], F32)
        scr = S("scr", [128, TC], F32)
        scrb = S("scrb", [128, 2, TC], BF16)
        arena_m = S("arena_m", [1, 2048], F32)
        arena_f = S("arena_f", [1, 2048], F32)
        gem = S("gem", [16, TC], F32)
        wg = S("wg", [128, TC], BF16)
        st = S("st", [128, NH * HD], BF16)
        m_s = S("m_s", [128, 128], BF16)
        btok = S("btok", [128, 128], BF16)
        lp2 = S("lp2", [128, 4, 128], BF16)
        dl = S("dl", [128, 128], F32)
        xdt2 = S("xdt2", [128, 4, HD], BF16)
        xdw2 = S("xdw2", [128, 4, HD], BF16)
        ytok2 = S("ytok2", [128, 4, HD], BF16)
        ytmp = S("ytmp", [128, 2, HD], BF16)
        dtT = S("dtT", [128, 32], F32)
        cT = S("cT", [128, 32], F32)
        ecT = S("ecT", [128, 32], F32)
        cqb = S("cqb", [128, 64], F32)
        xdf = S("xdf", [128, 32], F32)
        rows = S("rows", [1, 64], F32)
        w16 = S("w16", [128, 16], F32)
        gts = S("gts", [128, 14], F32)
        wtmp = S("wtmp", [128, 14], F32)
        r1 = S("r1", [128, 1], F32)
        r2 = S("r2", [128, 1], F32)
        rq = S("rq", [128, 1], F32)
        idf = S("idf_s", [128, 128], F32)
        idb = S("idb_s", [128, 128], BF16)
        mask = S("mask_s", [128, 128], F32)
        ones_row = S("ones_row", [1, 128], F32)
        ones_col = S("ones_col", [128, 1], BF16)
        zero32 = S("zero32", [32, 128], F32)
        cwp = S("cwp_s", [128, 72], F32)
        cbp = S("cbp_s", [128, 18], F32)
        nwp = S("nwp_s", [128, 16], F32)
        rmsp = S("rmsp_s", [128, 112], F32)
        rmssp = S("rmssp_s", [128, 16], F32)
        dcol = S("dcol_s", [128, 16], F32)
        dtb = S("dtb_s", [32, 1], F32)
        acol = S("acol_s", [32, 1], F32)
        rwp = S("rwp_s", [128, 8, NE_R], F32)

        pb = [P(f"pb{i}", [128, 512], F32) for i in range(6)]
        pb6 = P("pb6", [128, 1024], BF16)
        pb7 = P("pb7", [128, 1024], BF16)

        kb = KB(nc, ctx)
        E = kb.emit

        def scrF():
            return he[:].rearrange("p a b -> p (a b)")[:, 0:1024].bitcast(F32)

        mpieces = []
        fpieces = []
        cursor = [0]

        def add_piece(lst, src, K, W, scales=None):
            lst.append(Piece(src, K, W, cursor[0], scales))
            cursor[0] += K * W

        ipr = ipw_d.rearrange("(k p) f -> p k f", p=128)
        for g in range(12):
            f0 = g * 384
            f1 = min(DIP, f0 + 384)
            add_piece(mpieces, ipr[:, :, f0:f1], 8, f1 - f0)
        opr = opw_d.rearrange("(k p) o -> p k o", p=128)
        for o in range(8):
            add_piece(mpieces, opr[:, :, o * 128:(o + 1) * 128], 16, 128,
                      [nwp[:, k:k + 1] for k in range(16)])
        for e in range(NE):
            if e < NE_R:
                w1e = w1_d[e].rearrange("(k p) h -> p k h", p=128)
                w2e = w2_d[e].rearrange("(k p) o -> p k o", p=128)
                rc = lambda k, e=e: rmsp[:, e * 8 + k:e * 8 + k + 1]
            else:
                w1e = w1s_d[e - NE_R].rearrange("(k p) h -> p k h", p=128)
                w2e = w2s_d[e - NE_R].rearrange("(k p) o -> p k o", p=128)
                rc = lambda k, e=e: rmssp[:, (e - NE_R) * 8 + k:(e - NE_R) * 8 + k + 1]
            for qq in range(4):
                add_piece(fpieces, w1e[:, :, qq * 512:(qq + 1) * 512], 8, 512,
                          [rc(k) for k in range(8)])
            for qq in range(4):
                add_piece(fpieces, w2e[:, :, qq * 256:(qq + 1) * 256], 16, 256)
        assert cursor[0] == total_cols, (cursor[0], total_cols)

        for dst, src in [(cwp, cwp_d), (cbp, cbp_d), (nwp, nwp_d), (rmsp, rmsp_d),
                         (rmssp, rmssp_d), (dcol, dcol_d), (dtb, dtb_d),
                         (idf, idf_d), (idb, idb_d), (mask, mask_d)]:
            E("p", lambda d=dst, s=src: nc.gpsimd.dma_start(out=d[:], in_=s), w=[dst], dma="init")
        E("p", lambda: nc.gpsimd.dma_start(
            out=rwp[:], in_=rwp_d.rearrange("p (k e) -> p k e", k=8)), w=[rwp], dma="init")
        E("p", lambda: nc.gpsimd.dma_start(out=acol[:], in_=alog_d), w=[acol], dma="init")
        E("v", lambda: nc.vector.memset(ones_row[:], 1.0), w=[ones_row])
        E("v", lambda: nc.vector.memset(ones_col[:], 1.0), w=[ones_col])
        E("v", lambda: nc.vector.memset(zero32[:], 0.0), w=[zero32])
        E("v", lambda: nc.vector.memset(w16[:], 0.0), w=[w16])
        E("v", lambda: nc.vector.memset(st[:], 0.0), w=[st])
        E("v", lambda: nc.vector.memset(xbc[:], 0.0), w=[xbc])
        E("s", lambda: nc.scalar.activation(acol[:], acol[:], AF.Exp), r=[acol], w=[acol])
        E("v", lambda: nc.vector.tensor_scalar_mul(acol[:], acol[:], -1.0), r=[acol], w=[acol])

        for idx, pc in enumerate(mpieces + fpieces):
            slot = fring[idx % NSLOT_F]
            v3 = slot[:, 0:pc.K * pc.W].rearrange("p (k w) -> p k w", k=pc.K)
            E("p", lambda v=v3, s=pc.src: nc.gpsimd.dma_start(out=v, in_=s),
              w=[slot], dma=f"cv{idx % NSLOT_F}")
            if pc.scales:
                for k in range(pc.K):
                    E("v", lambda v=v3, k=k, sc=pc.scales[k]:
                      nc.vector.tensor_scalar(v[:, k, :], v[:, k, :], sc, None, op0=ALU.mult),
                      r=[slot], w=[slot])
            E("q", lambda sl=slot, pc=pc: nc.sync.dma_start(
                out=wscr_d[:, pc.off:pc.off + pc.K * pc.W], in_=sl[:, 0:pc.K * pc.W]),
                r=[slot], w=["wscr"], dma=f"cw{idx % NSLOT_F}")
        kb.fence()

        RM = Ring(kb, nc, wscr_d, mring, mpieces, "ms")
        RF = Ring(kb, nc, wscr_d, fring, fpieces, "fs")

        xr = x_d.rearrange("(c p) t -> p c t", p=128)
        outr = out_d.rearrange("(c p) t -> p c t", p=128)

        def chunk_ap(base, cix):
            if isinstance(cix, int):
                return base[:, :, cix * TC:(cix + 1) * TC]
            k, sub = cix
            if sub < 2:
                return base[:, :, bass.ts(k, 2 * TC)][:, :, sub * TC:(sub + 1) * TC]
            return base[:, :, bass.ts(k + 1, 2 * TC)][:, :, 0:TC]

        def mamba_thunks(cix, par):
            T = []
            A = T.append
            xt = xf[par]
            A(lambda: E("s", lambda: nc.scalar.dma_start(out=xt[:], in_=chunk_ap(xr, cix)),
                        w=[xt], dma="x"))
            for c8 in range(8):
                A(lambda c8=c8: E("p", lambda: nc.gpsimd.tensor_copy(
                    x_bf[:, c8, :], xt[:, c8, :]), r=[xt], w=[x_bf]))
            for c0 in range(0, 18, 3):
                def halo(c0=c0):
                    for c in range(c0, min(18, c0 + 3)):
                        E("p", lambda c=c: nc.gpsimd.tensor_copy(
                            xbc[:, c, 0:4], xbc[:, c, TC:TC + 4]), r=[xbc], w=[xbc])
                A(halo)

            def ip_col(j):
                g, jl = j // 3, j % 3
                wcol = 128 if j < 34 else 32
                bank = pb[2 + j % 2]
                slot, pc = RM.need(g)
                for k in range(8):
                    base = k * pc.W + jl * 128
                    E("t", lambda b=bank, sl=slot, base=base, w=wcol, k=k:
                      nc.tensor.matmul(b[0:w, 0:TC], sl[:, base:base + w], x_bf[:, k, :],
                                       start=(k == 0), stop=(k == 7)),
                      r=[slot, x_bf], w=[bank])
                if j < 16:
                    E("s", lambda b=bank, j=j: nc.scalar.activation(
                        siluz[:, j, :], b[:, 0:TC], AF.Silu), r=[bank], w=[siluz])
                elif j < 34:
                    E("s", lambda b=bank, j=j: nc.scalar.copy(
                        xbc[:, j - 16, 4:4 + TC], b[:, 0:TC]), r=[bank], w=[xbc])
                else:
                    E("v", lambda b=bank: nc.vector.tensor_scalar(
                        scr[0:32, :], b[0:32, 0:TC], dtb[:], None, op0=ALU.add),
                        r=[bank, dtb], w=[scr])
                    E("s", lambda: nc.scalar.activation(cc[:, :], scr[0:32, :], AF.Abs),
                      r=[scr], w=[cc])
                    E("s", lambda: nc.scalar.activation(cc[:, :], cc[:, :], AF.Exp, scale=-1.0),
                      r=[cc], w=[cc])
                    E("s", lambda: nc.scalar.activation(cc[:, :], cc[:, :], AF.Ln, bias=1.0),
                      r=[cc], w=[cc])
                    E("v", lambda: nc.vector.tensor_scalar_max(scr[0:32, :], scr[0:32, :], 0.0),
                      r=[scr], w=[scr])
                    E("v", lambda: nc.vector.tensor_add(dt[:, :], scr[0:32, :], cc[:, :]),
                      r=[scr, cc], w=[dt])
            for j in range(35):
                A(lambda j=j: ip_col(j))

            def conv_c(c):
                E("v", lambda: nc.vector.tensor_scalar(
                    scr[:, :], xbc[:, c, 1:1 + TC], cwp[:, c * 4 + 3:c * 4 + 4],
                    cbp[:, c:c + 1], op0=ALU.mult, op1=ALU.add),
                    r=[xbc, cwp, cbp], w=[scr])
                for jj in (2, 1, 0):
                    E("v", lambda jj=jj: nc.vector.scalar_tensor_tensor(
                        scr[:, :], xbc[:, c, 4 - jj:4 - jj + TC], cwp[:, c * 4 + jj:c * 4 + jj + 1],
                        scr[:, :], op0=ALU.mult, op1=ALU.add),
                        r=[xbc, cwp, scr], w=[scr])
                E("s", lambda: nc.scalar.activation(xbc[:, c, 4:4 + TC], scr[:, :], AF.Silu),
                  r=[scr], w=[xbc])
            for c in range(18):
                A(lambda c=c: conv_c(c))

            def dt_a():
                E("v", lambda: nc.vector.tensor_scalar(
                    scr[0:32, :], dt[:, :], acol[:], None, op0=ALU.mult),
                    r=[dt, acol], w=[scr])
                for q in range(NQ):
                    E("v", lambda q=q: nc.vector.tensor_tensor_scan(
                        cc[:, q * 128:(q + 1) * 128], scr[0:32, q * 128:(q + 1) * 128],
                        zero32[:, 0:128], 0.0, op0=ALU.add, op1=ALU.add),
                        r=[scr, zero32], w=[cc])
            A(dt_a)

            for q in range(NQ):
                qs0 = q * 128
                xq = 4 + q * 128

                def qstart(q=q, qs0=qs0, xq=xq):
                    E("q", lambda: nc.sync.dma_start(out=cbounce_d, in_=cc[:, qs0:qs0 + 128]),
                      r=[cc], w=["cbounce"], dma="cb")
                    E("q", lambda: nc.sync.dma_start(
                        out=arena_m[0:1, :],
                        in_=cbounce_d[0:16].rearrange("h t -> (h t)")[None, :]),
                        r=["cbounce"], w=[arena_m], dma="ar")
                    E("t", lambda: nc.tensor.matmul(
                        pb[4][:, 0:128], xbc[:, 16, xq:xq + 128], xbc[:, 17, xq:xq + 128],
                        start=True, stop=True), r=[xbc], w=[pb[4]])
                    E("v", lambda: nc.vector.tensor_copy(m_s[:], pb[4][:, 0:128]),
                      r=[pb[4]], w=[m_s])
                    E("t", lambda: nc.tensor.transpose(
                        pb7[:, 0:128], xbc[:, 16, xq:xq + 128], idb[:]), r=[xbc, idb], w=[pb7])
                    E("s", lambda: nc.scalar.copy(btok[:], pb7[:, 0:128]), r=[pb7], w=[btok])
                    E("t", lambda: nc.tensor.transpose(
                        pb[4][:, 128:160], dt[:, qs0:qs0 + 128], idf[0:32, 0:32]),
                        r=[dt, idf], w=[pb[4]])
                    E("v", lambda: nc.vector.tensor_copy(dtT[:], pb[4][:, 128:160]),
                      r=[pb[4]], w=[dtT])
                    E("t", lambda: nc.tensor.transpose(
                        pb[4][:, 160:192], cc[:, qs0:qs0 + 128], idf[0:32, 0:32]),
                        r=[cc, idf], w=[pb[4]])
                    E("v", lambda: nc.vector.tensor_copy(cT[:], pb[4][:, 160:192]),
                      r=[pb[4]], w=[cT])
                    E("s", lambda: nc.scalar.activation(ecT[:], pb[4][:, 160:192], AF.Exp),
                      r=[pb[4]], w=[ecT])
                    E("t", lambda: nc.tensor.transpose(
                        pb[4][0:1, 192:224], cc[:, qs0 + 127:qs0 + 128], idf[0:32, 0:32]),
                        r=[cc, idf], w=[pb[4]])
                    E("s", lambda: nc.scalar.copy(rows[0:1, 0:32], pb[4][0:1, 192:224]),
                      r=[pb[4]], w=[rows])
                    E("s", lambda: nc.scalar.activation(rows[0:1, 32:64], pb[4][0:1, 192:224], AF.Exp),
                      r=[pb[4]], w=[rows])
                    E("t", lambda: nc.tensor.matmul(pb[4][:, 224:288], ones_row[:],
                                                    rows[0:1, 0:64], start=True, stop=True),
                      r=[ones_row, rows], w=[pb[4]])
                    E("v", lambda: nc.vector.tensor_copy(cqb[:], pb[4][:, 224:288]),
                      r=[pb[4]], w=[cqb])
                    E("v", lambda: nc.vector.tensor_sub(xdf[:], cqb[:, 0:32], cT[:]),
                      r=[cqb, cT], w=[xdf])
                    E("s", lambda: nc.scalar.activation(xdf[:], xdf[:], AF.Exp), r=[xdf], w=[xdf])
                A(qstart)

                def arena2(q=q):
                    E("q", lambda: nc.sync.dma_start(
                        out=arena_m[0:1, :],
                        in_=cbounce_d[16:32].rearrange("h t -> (h t)")[None, :]),
                        r=["cbounce"], w=[arena_m], dma="ar")

                def tA(h, xq=xq):
                    hp = h % 4
                    hc, ho = h // 2, (h % 2) * 64
                    E("t", lambda: nc.tensor.matmul(
                        pb[5][:, 0:128], ones_row[:],
                        arena_m[0:1, (h % 16) * 128:(h % 16 + 1) * 128],
                        start=True, stop=True), r=[ones_row, arena_m], w=[pb[5]])
                    E("v", lambda: nc.vector.scalar_tensor_tensor(
                        dl[:], pb[5][:, 0:128], cT[:, h:h + 1], mask[:],
                        op0=ALU.subtract, op1=ALU.add), r=[pb[5], cT, mask], w=[dl])
                    E("s", lambda: nc.scalar.activation(lp2[:, hp, :], dl[:], AF.Exp),
                      r=[dl], w=[lp2])
                    E("v", lambda: nc.vector.tensor_mul(lp2[:, hp, :], lp2[:, hp, :], m_s[:]),
                      r=[lp2, m_s], w=[lp2])
                    E("t", lambda: nc.tensor.transpose(
                        pb6[:, 0:64], xbc[ho:ho + 64, hc, xq:xq + 128],
                        idb[ho:ho + 64, ho:ho + 64]), r=[xbc, idb], w=[pb6])
                    E("s", lambda: nc.scalar.mul(xdt2[:, hp, :], pb6[:, 0:64], dtT[:, h:h + 1]),
                      r=[pb6, dtT], w=[xdt2])
                    E("v", lambda: nc.vector.tensor_scalar(
                        xdw2[:, hp, :], xdt2[:, hp, :], xdf[:, h:h + 1], None, op0=ALU.mult),
                        r=[xdt2, xdf], w=[xdw2])

                def tB(h, xq=xq):
                    hp = h % 4
                    E("t", lambda: nc.tensor.matmul(pb[2][:, 0:64], lp2[:, hp, :],
                                                    xdt2[:, hp, :], start=True, stop=True),
                      r=[lp2, xdt2], w=[pb[2]])
                    E("t", lambda: nc.tensor.matmul(
                        pb[3][:, 0:64], xbc[:, 17, xq:xq + 128], st[:, h * 64:(h + 1) * 64],
                        start=True, stop=True), r=[xbc, st], w=[pb[3]])
                    E("s", lambda: nc.scalar.mul(ytmp[:, h % 2, :], pb[3][:, 0:64], ecT[:, h:h + 1]),
                      r=[pb[3], ecT], w=[ytmp])
                    E("v", lambda: nc.vector.tensor_add(
                        ytok2[:, hp, :], pb[2][:, 0:64], ytmp[:, h % 2, :]),
                      r=[pb[2], ytmp], w=[ytok2])
                    E("t", lambda: nc.tensor.matmul(pb[4][:, 0:64], btok[:], xdw2[:, hp, :],
                                                    start=True, stop=True),
                      r=[btok, xdw2], w=[pb[4]])
                    E("v", lambda: nc.vector.scalar_tensor_tensor(
                        st[:, h * 64:(h + 1) * 64], st[:, h * 64:(h + 1) * 64],
                        cqb[:, 32 + h:33 + h], pb[4][:, 0:64], op0=ALU.mult, op1=ALU.add),
                        r=[st, cqb, pb[4]], w=[st])

                def tC(h, qs0=qs0, xq=xq):
                    hp = h % 4
                    hc, ho = h // 2, (h % 2) * 64
                    E("t", lambda: nc.tensor.transpose(pb7[0:64, 128:256], ytok2[:, hp, :], idb[:]),
                      r=[ytok2, idb], w=[pb7])
                    E("v", lambda: nc.vector.scalar_tensor_tensor(
                        y[ho:ho + 64, hc, qs0:qs0 + 128], xbc[ho:ho + 64, hc, xq:xq + 128],
                        dcol[ho:ho + 64, hc:hc + 1], pb7[0:64, 128:256],
                        op0=ALU.mult, op1=ALU.add), r=[xbc, dcol, pb7], w=[y])

                for h in range(NH + 5):
                    if h < NH:
                        A(lambda h=h, tA=tA: tA(h))
                    if h == 15:
                        A(arena2)
                    if 3 <= h < NH + 3:
                        A(lambda h=h, tB=tB: tB(h - 3))
                    if h >= 5:
                        A(lambda h=h, tC=tC: tC(h - 5))

            def gat_c(c):
                E("v", lambda: nc.vector.tensor_mul(y[:, c, :], y[:, c, :], siluz[:, c, :]),
                  r=[y, siluz], w=[y])
                E("s", lambda: nc.scalar.activation(scrb[:, c % 2, :], y[:, c, :], AF.Square),
                  r=[y], w=[scrb])
                E("t", lambda: nc.tensor.matmul(
                    pb[5][0:1, 0:TC], ones_col[:], scrb[:, c % 2, :],
                    start=(c == 0), stop=(c == 15)), r=[ones_col, scrb], w=[pb[5]])
            for c in range(16):
                A(lambda c=c: gat_c(c))

            def grms():
                E("v", lambda: nc.vector.tensor_scalar(
                    arena_m[0:1, 0:TC], pb[5][0:1, 0:TC], 1.0 / D_INNER, MRMS_EPS,
                    op0=ALU.mult, op1=ALU.add), r=[pb[5]], w=[arena_m])
                E("s", lambda: nc.scalar.activation(arena_m[0:1, 0:TC], arena_m[0:1, 0:TC], AF.Ln),
                  r=[arena_m], w=[arena_m])
                E("v", lambda: nc.vector.tensor_scalar_mul(
                    arena_m[0:1, 0:TC], arena_m[0:1, 0:TC], -0.5), r=[arena_m], w=[arena_m])
                E("s", lambda: nc.scalar.activation(arena_m[0:1, 0:TC], arena_m[0:1, 0:TC], AF.Exp),
                  r=[arena_m], w=[arena_m])
                E("t", lambda: nc.tensor.matmul(pb[5][:, 0:TC], ones_row[:], arena_m[0:1, 0:TC],
                                                start=True, stop=True),
                  r=[ones_row, arena_m], w=[pb[5]])
            A(grms)
            for c0 in range(0, 16, 4):
                def gapply(c0=c0):
                    for c in range(c0, c0 + 4):
                        E("v", lambda c=c: nc.vector.tensor_mul(
                            y[:, c, :], y[:, c, :], pb[5][:, 0:TC]), r=[y, pb[5]], w=[y])
                A(gapply)

            def op_col(o):
                slot, pc = RM.need(12 + o)
                bank = pb[2 + o % 2]
                for k in range(16):
                    base = k * pc.W
                    E("t", lambda b=bank, sl=slot, base=base, k=k:
                      nc.tensor.matmul(b[:, 0:TC], sl[:, base:base + 128], y[:, k, :],
                                       start=(k == 0), stop=(k == 15)),
                      r=[slot, y], w=[bank])
                E("v", lambda b=bank, o=o: nc.vector.tensor_add(
                    xt[:, o, :], xt[:, o, :], b[:, 0:TC]), r=[bank, xt], w=[xt])
            for o in range(8):
                A(lambda o=o: op_col(o))

            def ln1_c(c):
                E("s", lambda: nc.scalar.copy(x_bf[:, c, :], xt[:, c, :]), r=[xt], w=[x_bf])
                E("t", lambda: nc.tensor.matmul(
                    pb[4][0:1, 0:TC], ones_col[:], x_bf[:, c, :],
                    start=(c == 0), stop=(c == 7)), r=[ones_col, x_bf], w=[pb[4]])
                E("s", lambda: nc.scalar.activation(scrb[:, c % 2, :], x_bf[:, c, :], AF.Square),
                  r=[x_bf], w=[scrb])
                E("t", lambda: nc.tensor.matmul(
                    pb[5][0:1, 0:TC], ones_col[:], scrb[:, c % 2, :],
                    start=(c == 0), stop=(c == 7)), r=[ones_col, scrb], w=[pb[5]])
            for c in range(8):
                A(lambda c=c: ln1_c(c))

            def ln1_rows():
                E("v", lambda: nc.vector.tensor_scalar(
                    arena_m[0:1, 0:TC], pb[4][0:1, 0:TC], 1.0 / D_MODEL, None, op0=ALU.mult),
                    r=[pb[4]], w=[arena_m])
                E("v", lambda: nc.vector.tensor_scalar(
                    arena_m[0:1, TC:2 * TC], pb[5][0:1, 0:TC], 1.0 / D_MODEL, LN_EPS,
                    op0=ALU.mult, op1=ALU.add), r=[pb[5]], w=[arena_m])
                E("s", lambda: nc.scalar.activation(
                    scr[0:1, 0:TC], arena_m[0:1, 0:TC], AF.Square), r=[arena_m], w=[scr])
                E("v", lambda: nc.vector.tensor_sub(
                    arena_m[0:1, TC:2 * TC], arena_m[0:1, TC:2 * TC], scr[0:1, 0:TC]),
                    r=[arena_m, scr], w=[arena_m])
                E("s", lambda: nc.scalar.activation(
                    arena_m[0:1, TC:2 * TC], arena_m[0:1, TC:2 * TC], AF.Ln),
                    r=[arena_m], w=[arena_m])
                E("v", lambda: nc.vector.tensor_scalar_mul(
                    arena_m[0:1, TC:2 * TC], arena_m[0:1, TC:2 * TC], -0.5),
                    r=[arena_m], w=[arena_m])
                E("s", lambda: nc.scalar.activation(
                    arena_m[0:1, TC:2 * TC], arena_m[0:1, TC:2 * TC], AF.Exp),
                    r=[arena_m], w=[arena_m])
                E("v", lambda: nc.vector.tensor_mul(
                    arena_m[0:1, 0:TC], arena_m[0:1, 0:TC], arena_m[0:1, TC:2 * TC]),
                    r=[arena_m], w=[arena_m])
                E("t", lambda: nc.tensor.matmul(pb[2][:, 0:TC], ones_row[:],
                                                arena_m[0:1, TC:2 * TC], start=True, stop=True),
                  r=[ones_row, arena_m], w=[pb[2]])
                E("t", lambda: nc.tensor.matmul(pb[3][:, 0:TC], ones_row[:],
                                                arena_m[0:1, 0:TC], start=True, stop=True),
                  r=[ones_row, arena_m], w=[pb[3]])
            A(ln1_rows)
            for c0 in range(0, 8, 2):
                def ln1_apply(c0=c0):
                    for c in range(c0, c0 + 2):
                        E("v", lambda c=c: nc.vector.tensor_mul(
                            scr[:, :], xt[:, c, :], pb[2][:, 0:TC]), r=[xt, pb[2]], w=[scr])
                        E("v", lambda c=c: nc.vector.tensor_sub(
                            xt[:, c, :], scr[:, :], pb[3][:, 0:TC]), r=[scr, pb[3]], w=[xt])
                A(ln1_apply)
            return T

        def emit_ffn(cix, par, pump):
            if pump is None:
                pump = lambda: None
            xt = xf[par]
            for g4 in range(4):
                for k in range(8):
                    E("t", lambda g4=g4, k=k: nc.tensor.matmul(
                        pb[0][:, g4 * 14:(g4 + 1) * 14], xt[:, k, g4 * 128:(g4 + 1) * 128],
                        rwp[:, k, :], start=(k == 0), stop=(k == 7)),
                        r=[xt, rwp], w=[pb[0]])
                pump()
            for g4 in range(4):
                E("v", lambda g4=g4: nc.vector.reduce_max(
                    r1[:], pb[0][:, g4 * 14:(g4 + 1) * 14], axis=AX.X, negate=True),
                    r=[pb[0]], w=[r1])
                E("s", lambda g4=g4: nc.scalar.activation(
                    gts[:], pb[0][:, g4 * 14:(g4 + 1) * 14], AF.Exp, bias=r1[:]),
                    r=[pb[0], r1], w=[gts])
                E("v", lambda: nc.vector.reduce_sum(rq[:], gts[:], axis=AX.X), r=[gts], w=[rq])
                E("v", lambda: nc.vector.reciprocal(rq[:], rq[:]), r=[rq], w=[rq])
                E("v", lambda: nc.vector.tensor_scalar(gts[:], gts[:], rq[:], None, op0=ALU.mult),
                  r=[gts, rq], w=[gts])
                E("v", lambda: nc.vector.reduce_max(r1[:], gts[:], axis=AX.X), r=[gts], w=[r1])
                E("v", lambda: nc.vector.tensor_scalar(wtmp[:], gts[:], r1[:], None, op0=ALU.is_ge),
                  r=[gts, r1], w=[wtmp])
                E("v", lambda: nc.vector.scalar_tensor_tensor(
                    wtmp[:], wtmp[:], -1e30, gts[:], op0=ALU.mult, op1=ALU.add),
                    r=[wtmp, gts], w=[wtmp])
                E("v", lambda: nc.vector.reduce_max(r2[:], wtmp[:], axis=AX.X), r=[wtmp], w=[r2])
                E("v", lambda: nc.vector.tensor_scalar(wtmp[:], gts[:], r2[:], None, op0=ALU.is_ge),
                  r=[gts, r2], w=[wtmp])
                E("v", lambda: nc.vector.tensor_mul(wtmp[:], gts[:], wtmp[:]),
                  r=[gts, wtmp], w=[wtmp])
                E("v", lambda: nc.vector.reduce_sum(rq[:], wtmp[:], axis=AX.X), r=[wtmp], w=[rq])
                E("v", lambda: nc.vector.tensor_scalar_add(rq[:], rq[:], GATE_EPS), r=[rq], w=[rq])
                E("v", lambda: nc.vector.reciprocal(rq[:], rq[:]), r=[rq], w=[rq])
                E("v", lambda: nc.vector.tensor_scalar(
                    w16[:, 0:14], wtmp[:], rq[:], None, op0=ALU.mult), r=[wtmp, rq], w=[w16])
                E("t", lambda: nc.tensor.transpose(pb[1][0:16, 0:128], w16[:], idf[:]),
                  r=[w16, idf], w=[pb[1]])
                E("v", lambda g4=g4: nc.vector.tensor_copy(
                    gem[:, g4 * 128:(g4 + 1) * 128], pb[1][0:16, 0:128]), r=[pb[1]], w=[gem])
                pump()
            E("q", lambda: nc.sync.dma_start(out=gemb_d, in_=gem[:]),
              r=[gem], w=["gemb"], dma="gemb")
            for c in range(8):
                E("s", lambda c=c: nc.scalar.activation(
                    fb_bf[:, c % 2, :], xt[:, c, :], AF.Square), r=[xt], w=[fb_bf])
                E("t", lambda c=c: nc.tensor.matmul(
                    pb[1][0:1, 0:TC], ones_col[:], fb_bf[:, c % 2, :],
                    start=(c == 0), stop=(c == 7)), r=[ones_col, fb_bf], w=[pb[1]])
            E("v", lambda: nc.vector.tensor_scalar(
                arena_f[0:1, 0:TC], pb[1][0:1, 0:TC], 1.0 / D_MODEL, FRMS_EPS,
                op0=ALU.mult, op1=ALU.add), r=[pb[1]], w=[arena_f])
            E("s", lambda: nc.scalar.activation(arena_f[0:1, 0:TC], arena_f[0:1, 0:TC], AF.Ln),
              r=[arena_f], w=[arena_f])
            E("v", lambda: nc.vector.tensor_scalar_mul(
                arena_f[0:1, 0:TC], arena_f[0:1, 0:TC], -0.5), r=[arena_f], w=[arena_f])
            E("s", lambda: nc.scalar.activation(arena_f[0:1, 0:TC], arena_f[0:1, 0:TC], AF.Exp),
              r=[arena_f], w=[arena_f])
            E("t", lambda: nc.tensor.matmul(pb[0][:, 0:TC], ones_row[:], arena_f[0:1, 0:TC],
                                            start=True, stop=True),
              r=[ones_row, arena_f], w=[pb[0]])
            pump()
            for c in range(8):
                E("v", lambda c=c: nc.vector.tensor_mul(
                    xhat0[:, c, :], xt[:, c, :], pb[0][:, 0:TC]), r=[xt, pb[0]], w=[xhat0])
            pump()

            for e in range(NE):
                if e < NE_R and e % 2 == 0:
                    E("q", lambda e=e: nc.sync.dma_start(
                        out=arena_f[0:1, 0:2 * TC],
                        in_=gemb_d[e:e + 2, :].rearrange("a t -> (a t)")[None, :]),
                        r=["gemb"], w=[arena_f], dma="ar2")
                if e < NE_R:
                    E("t", lambda e=e: nc.tensor.matmul(
                        pb[2][:, 0:TC], ones_row[:],
                        arena_f[0:1, (e % 2) * TC:(e % 2 + 1) * TC],
                        start=True, stop=True), r=[ones_row, arena_f], w=[pb[2]])
                    E("s", lambda: nc.scalar.copy(wg[:], pb[2][:, 0:TC]), r=[pb[2]], w=[wg])
                p1a = 8 * e
                for hcol in range(16):
                    slot, pc = RF.need(p1a + hcol // 4)
                    hl = hcol % 4
                    bank = pb[hcol % 2]
                    for k in range(8):
                        base = k * pc.W + hl * 128
                        E("t", lambda b=bank, sl=slot, base=base, k=k:
                          nc.tensor.matmul(b[:, 0:TC], sl[:, base:base + 128],
                                           xhat0[:, k, :], start=(k == 0), stop=(k == 7)),
                          r=[slot, xhat0], w=[bank])
                    E("s", lambda b=bank, hcol=hcol: nc.scalar.activation(
                        he[:, hcol, :], b[:, 0:TC], AF.Silu), r=[bank], w=[f"he{hcol}"])
                    if e < NE_R:
                        E("v", lambda hcol=hcol: nc.vector.tensor_mul(
                            he[:, hcol, :], he[:, hcol, :], wg[:]),
                            r=[f"he{hcol}", wg], w=[f"he{hcol}"])
                    pump()
                p2a = 8 * e + 4
                for o in range(8):
                    slot, pc = RF.need(p2a + o // 2)
                    ol = o % 2
                    bank = pb[o % 2]
                    for k in range(16):
                        base = k * pc.W + ol * 128
                        E("t", lambda b=bank, sl=slot, base=base, k=k:
                          nc.tensor.matmul(b[:, 0:TC], sl[:, base:base + 128],
                                           he[:, k, :], start=(k == 0), stop=(k == 15)),
                          r=[slot, f"he{k}"], w=[bank])
                    E("v", lambda b=bank, o=o: nc.vector.tensor_add(
                        xt[:, o, :], xt[:, o, :], b[:, 0:TC]), r=[bank, xt], w=[xt])
                    pump()

            for c in range(8):
                E("s", lambda c=c: nc.scalar.copy(fb_bf[:, c, :], xt[:, c, :]), r=[xt], w=[fb_bf])
                E("t", lambda c=c: nc.tensor.matmul(
                    pb[0][0:1, 0:TC], ones_col[:], fb_bf[:, c, :],
                    start=(c == 0), stop=(c == 7)), r=[ones_col, fb_bf], w=[pb[0]])
                E("s", lambda c=c: nc.scalar.activation(
                    xhat0[:, c % 2, :], fb_bf[:, c, :], AF.Square), r=[fb_bf], w=[xhat0])
                E("t", lambda c=c: nc.tensor.matmul(
                    pb[1][0:1, 0:TC], ones_col[:], xhat0[:, c % 2, :],
                    start=(c == 0), stop=(c == 7)), r=[ones_col, xhat0], w=[pb[1]])
            pump()
            E("v", lambda: nc.vector.tensor_scalar(
                arena_f[0:1, 0:TC], pb[0][0:1, 0:TC], 1.0 / D_MODEL, None, op0=ALU.mult),
                r=[pb[0]], w=[arena_f])
            E("v", lambda: nc.vector.tensor_scalar(
                arena_f[0:1, TC:2 * TC], pb[1][0:1, 0:TC], 1.0 / D_MODEL, LN_EPS,
                op0=ALU.mult, op1=ALU.add), r=[pb[1]], w=[arena_f])
            E("s", lambda: nc.scalar.activation(
                scrF()[0:1, 0:TC], arena_f[0:1, 0:TC], AF.Square),
                r=[arena_f, "he0", "he1"], w=["he0", "he1"])
            E("v", lambda: nc.vector.tensor_sub(
                arena_f[0:1, TC:2 * TC], arena_f[0:1, TC:2 * TC], scrF()[0:1, 0:TC]),
                r=[arena_f, "he0"], w=[arena_f])
            E("s", lambda: nc.scalar.activation(
                arena_f[0:1, TC:2 * TC], arena_f[0:1, TC:2 * TC], AF.Ln), r=[arena_f], w=[arena_f])
            E("v", lambda: nc.vector.tensor_scalar_mul(
                arena_f[0:1, TC:2 * TC], arena_f[0:1, TC:2 * TC], -0.5), r=[arena_f], w=[arena_f])
            E("s", lambda: nc.scalar.activation(
                arena_f[0:1, TC:2 * TC], arena_f[0:1, TC:2 * TC], AF.Exp), r=[arena_f], w=[arena_f])
            E("v", lambda: nc.vector.tensor_mul(
                arena_f[0:1, 0:TC], arena_f[0:1, 0:TC], arena_f[0:1, TC:2 * TC]),
                r=[arena_f], w=[arena_f])
            E("t", lambda: nc.tensor.matmul(pb[0][:, 0:TC], ones_row[:], arena_f[0:1, TC:2 * TC],
                                            start=True, stop=True), r=[ones_row, arena_f], w=[pb[0]])
            E("t", lambda: nc.tensor.matmul(pb[1][:, 0:TC], ones_row[:], arena_f[0:1, 0:TC],
                                            start=True, stop=True), r=[ones_row, arena_f], w=[pb[1]])
            for c in range(8):
                E("v", lambda c=c: nc.vector.tensor_mul(
                    scrF()[:, 0:TC], xt[:, c, :], pb[0][:, 0:TC]),
                    r=[xt, pb[0], "he0", "he1"], w=["he0", "he1"])
                E("v", lambda c=c: nc.vector.tensor_sub(
                    xt[:, c, :], scrF()[:, 0:TC], pb[1][:, 0:TC]),
                    r=["he0", pb[1]], w=[xt])
            pump()
            E("s", lambda: nc.scalar.dma_start(out=chunk_ap(outr, cix), in_=xt[:]),
              r=[xt], dma="out")

        def run_pipelined(f_cix, f_par, m_cix, m_par):
            RM.reset()
            RF.reset()
            thunks = mamba_thunks(m_cix, m_par) if m_cix is not None else []
            pos = [0.0]
            idx = [0]
            per = 0.0

            def pump():
                pos[0] += per
                kb.eager = True
                while idx[0] < len(thunks) and idx[0] < pos[0]:
                    thunks[idx[0]]()
                    idx[0] += 1
                kb.eager = False
            if f_cix is not None:
                emit_ffn(f_cix, f_par, pump if thunks else None)
            kb.eager = True
            while idx[0] < len(thunks):
                thunks[idx[0]]()
                idx[0] += 1
            kb.eager = False
            kb.fence()

        if n_chunks == NCHUNK:
            run_pipelined(None, 0, 0, 0)
            with nc.Fori(0, 7) as k:
                run_pipelined((k, 0), 0, (k, 1), 1)
                run_pipelined((k, 1), 1, (k, 2), 0)
            run_pipelined(14, 0, 15, 1)
            run_pipelined(15, 1, None, 0)
        else:
            run_pipelined(None, 0, 0, 0)
            for c in range(n_chunks):
                run_pipelined(c, c % 2, c + 1 if c + 1 < n_chunks else None, (c + 1) % 2)

        if dbg:
            last_par = (n_chunks - 1) % 2
            dumps = [("d_siluz", siluz[:].rearrange("p a b -> p (a b)")),
                     ("d_xbc", xbc[:].rearrange("p a b -> p (a b)")),
                     ("d_dt", dt[:]), ("d_cc", cc[:]),
                     ("d_y", y[:].rearrange("p a b -> p (a b)")),
                     ("d_x1", xf[last_par][:].rearrange("p a b -> p (a b)")),
                     ("d_gem", gem[:]),
                     ("d_he", he[:].rearrange("p a b -> p (a b)")),
                     ("d_xh", xhat0[:].rearrange("p a b -> p (a b)")),
                     ("d_s", st[:])]
            for nm, ap in dumps:
                E("p", lambda nm=nm, ap=ap: nc.gpsimd.dma_start(
                    out=dbg_outs[nm], in_=ap), r=[], dma="dbg")
            kb.fence()

    return nc


def _host_packs(inputs):
    f = np.float32
    conv_w = np.asarray(inputs["L0_attn_conv_w"], f)
    conv_b = np.asarray(inputs["L0_attn_conv_b"], f)
    return {
        "ipw": np.ascontiguousarray(np.asarray(inputs["L0_attn_in_proj_w"], f)),
        "opw": np.ascontiguousarray(np.asarray(inputs["L0_attn_out_proj_w"], f)),
        "w1": np.ascontiguousarray(np.asarray(inputs["L0_ffn_w1"], f)),
        "w2": np.ascontiguousarray(np.asarray(inputs["L0_ffn_w2"], f)),
        "w1s": np.ascontiguousarray(np.asarray(inputs["L0_ffn_w1_s"], f)),
        "w2s": np.ascontiguousarray(np.asarray(inputs["L0_ffn_w2_s"], f)),
        "cwp": np.ascontiguousarray(conv_w.reshape(4, 18, 128).transpose(2, 1, 0).reshape(128, 72)),
        "cbp": np.ascontiguousarray(conv_b.reshape(18, 128).T),
        "nwp": np.ascontiguousarray(np.asarray(inputs["L0_attn_norm_w"], f).reshape(16, 128).T),
        "rmsp": np.ascontiguousarray(
            np.asarray(inputs["L0_ffn_rms_w"], f).reshape(14, 8, 128).transpose(2, 0, 1).reshape(128, 112)),
        "rmssp": np.ascontiguousarray(
            np.asarray(inputs["L0_ffn_rms_w_s"], f).reshape(2, 8, 128).transpose(2, 0, 1).reshape(128, 16)),
        "dcol": np.ascontiguousarray(
            np.repeat(np.asarray(inputs["L0_attn_D"], f), 64).reshape(16, 128).T),
        "dtb": np.ascontiguousarray(np.asarray(inputs["L0_attn_dt_bias"], f).reshape(32, 1)),
        "alog": np.ascontiguousarray(np.asarray(inputs["L0_attn_A_log"], f).reshape(32, 1)),
        "rwp": np.ascontiguousarray(
            np.asarray(inputs["L0_ffn_router_w"], f).reshape(8, 128, 14).transpose(1, 0, 2).reshape(128, 112)),
        "idf": np.eye(128, dtype=f),
        "idb": np.eye(128, dtype=ml_dtypes.bfloat16),
        "mask": np.where(np.arange(128)[None, :] >= np.arange(128)[:, None],
                         np.float32(0.0), np.float32(MASK_NEG)),
    }


def kernel(**inputs):
    x = np.asarray(inputs["L0_attn_x"], np.float32)
    packs = _host_packs(inputs)
    nc = build_program(NCHUNK, dbg=False)
    in_maps = []
    for b in range(8):
        m = dict(packs)
        m["x"] = np.ascontiguousarray(x[b].T)
        in_maps.append(m)
    res = run_bass_kernel_spmd(nc, in_maps, core_ids=list(range(8)))
    return np.stack([np.ascontiguousarray(r["out"].T) for r in res.results], axis=0)


# version 14
# speedup vs baseline: 1.3093x; 1.0202x over previous
import numpy as np
import ml_dtypes
from contextlib import ExitStack

import concourse.bass as bass
import concourse.mybir as mybir
from concourse.bass_utils import run_bass_kernel_spmd

F32 = mybir.dt.float32
BF16 = mybir.dt.bfloat16
AF = mybir.ActivationFunctionType
ALU = mybir.AluOpType
AX = mybir.AxisListType

D_MODEL = 1024
D_INNER = 2048
NH = 32
HD = 64
DIP = 4384
HIDDEN = 2048
NE_R = 14
NE = 16
TC = 512
NQ = TC // 128
SEQ = 8192
NCHUNK = SEQ // TC

LN_EPS = 1e-5
MRMS_EPS = 1e-5
FRMS_EPS = 1.1920929e-07
GATE_EPS = 1e-9
MASK_NEG = -30000.0

NSLOT_F = 3
FSLOT_COLS = 4096
NSLOT_M = 2
MSLOT_COLS = 3072


class KB:

    COMPUTE = ("t", "v", "s", "p")

    def __init__(self, nc, ctx):
        self.nc = nc
        self.ctx = ctx
        self.eng = {"t": nc.tensor, "v": nc.vector, "s": nc.scalar,
                    "p": nc.gpsimd, "q": nc.sync}
        self.chain = {e: [ctx.enter_context(nc.semaphore(f"ch_{e}")), 0]
                      for e in self.COMPUTE}
        self.dsem = {}
        self.latest = {e: None for e in self.COMPUTE}
        self.eager = False
        self.last_w = {}
        self.readers = {}
        self.waited = {e: {} for e in self.eng}

    @staticmethod
    def _k(h):
        return h if isinstance(h, str) else h.name

    def _edge(self, rec, ceng):
        if rec is None:
            return
        peng = rec["eng"]
        if rec["dma"]:
            sem, val = rec["sem"], rec["val"]
        else:
            if peng == "t" and ceng == "t":
                return
            tgt = rec if rec["sem"] is not None else self.latest[peng]
            if tgt["sem"] is None:
                cs = self.chain[peng]
                cs[1] += 1
                tgt["ins"].then_inc(cs[0], 1)
                tgt["sem"], tgt["val"] = cs[0], cs[1]
            sem, val = tgt["sem"], tgt["val"]
        w = self.waited[ceng]
        if w.get(sem, 0) < val:
            self.eng[ceng].wait_ge(sem, val)
            w[sem] = val

    def emit(self, ceng, fn, r=(), w=(), dma=False):
        for h in r:
            self._edge(self.last_w.get(self._k(h)), ceng)
        for h in w:
            k = self._k(h)
            self._edge(self.last_w.get(k), ceng)
            for rec in self.readers.get(k, {}).values():
                self._edge(rec, ceng)
        ins = fn()
        rec = {"eng": ceng, "ins": ins, "sem": None, "val": None, "dma": bool(dma)}
        if not dma and self.eager:
            cs = self.chain[ceng]
            cs[1] += 1
            ins.then_inc(cs[0], 1)
            rec["sem"], rec["val"] = cs[0], cs[1]
        if dma:
            if dma not in self.dsem:
                self.dsem[dma] = [self.ctx.enter_context(self.nc.semaphore(f"ds_{dma}")), 0]
            ds = self.dsem[dma]
            ds[1] += 16
            ins.then_inc(ds[0], 16)
            rec["sem"], rec["val"] = ds[0], ds[1]
        else:
            self.latest[ceng] = rec
        for h in w:
            k = self._k(h)
            self.last_w[k] = rec
            self.readers[k] = {}
        for h in r:
            self.readers.setdefault(self._k(h), {})[ceng] = rec
        return ins

    def fence(self):
        nc = self.nc
        for role, (sem, cnt) in self.dsem.items():
            if cnt:
                nc.sync.wait_ge(sem, cnt)
        for e in self.COMPUTE:
            sem, cnt = self.chain[e]
            if cnt:
                self.eng[e].wait_ge(sem, cnt)
        nc.all_engine_barrier()
        for e in self.COMPUTE:
            nc.sync.sem_clear(self.chain[e][0])
        for role, (sem, cnt) in self.dsem.items():
            nc.sync.sem_clear(sem)
        nc.all_engine_barrier()
        for e in self.COMPUTE:
            self.chain[e][1] = 0
            self.latest[e] = None
        for role in self.dsem:
            self.dsem[role][1] = 0
        self.last_w = {}
        self.readers = {}
        self.waited = {e: {} for e in self.eng}


class Piece:
    def __init__(self, src, K, W, off, scales=None):
        self.src = src
        self.K = K
        self.W = W
        self.off = off
        self.scales = scales


class Ring:

    def __init__(self, kb, nc, wscr, slots, pieces, role):
        self.kb, self.nc, self.wscr = kb, nc, wscr
        self.slots = slots
        self.pieces = pieces
        self.role = role
        self.nxt = 0

    def reset(self):
        self.nxt = 0

    def need(self, i):
        while self.nxt <= min(i + len(self.slots) - 1, len(self.pieces) - 1):
            pc = self.pieces[self.nxt]
            slot = self.slots[self.nxt % len(self.slots)]
            self.kb.emit("q", lambda sl=slot, pc=pc: self.nc.sync.dma_start(
                out=sl[:, 0:pc.K * pc.W], in_=self.wscr[:, pc.off:pc.off + pc.K * pc.W]),
                r=["wscr"], w=[slot], dma=f"{self.role}{self.nxt % len(self.slots)}")
            self.nxt += 1
        return self.slots[i % len(self.slots)], self.pieces[i]


def build_program(n_chunks=NCHUNK, dbg=False):
    nc = bass.Bass("TRN2", target_bir_lowering=False, debug=False)

    x_d = nc.dram_tensor("x", [D_MODEL, SEQ], F32, kind="ExternalInput").ap()
    ipw_d = nc.dram_tensor("ipw", [D_MODEL, DIP], F32, kind="ExternalInput").ap()
    opw_d = nc.dram_tensor("opw", [D_INNER, D_MODEL], F32, kind="ExternalInput").ap()
    w1_d = nc.dram_tensor("w1", [NE_R, D_MODEL, HIDDEN], F32, kind="ExternalInput").ap()
    w2_d = nc.dram_tensor("w2", [NE_R, HIDDEN, D_MODEL], F32, kind="ExternalInput").ap()
    w1s_d = nc.dram_tensor("w1s", [2, D_MODEL, HIDDEN], F32, kind="ExternalInput").ap()
    w2s_d = nc.dram_tensor("w2s", [2, HIDDEN, D_MODEL], F32, kind="ExternalInput").ap()
    cwp_d = nc.dram_tensor("cwp", [128, 72], F32, kind="ExternalInput").ap()
    cbp_d = nc.dram_tensor("cbp", [128, 18], F32, kind="ExternalInput").ap()
    nwp_d = nc.dram_tensor("nwp", [128, 16], F32, kind="ExternalInput").ap()
    rmsp_d = nc.dram_tensor("rmsp", [128, 112], F32, kind="ExternalInput").ap()
    rmssp_d = nc.dram_tensor("rmssp", [128, 16], F32, kind="ExternalInput").ap()
    dcol_d = nc.dram_tensor("dcol", [128, 16], F32, kind="ExternalInput").ap()
    dtb_d = nc.dram_tensor("dtb", [32, 1], F32, kind="ExternalInput").ap()
    alog_d = nc.dram_tensor("alog", [32, 1], F32, kind="ExternalInput").ap()
    rwp_d = nc.dram_tensor("rwp", [128, 112], F32, kind="ExternalInput").ap()
    idf_d = nc.dram_tensor("idf", [128, 128], F32, kind="ExternalInput").ap()
    idb_d = nc.dram_tensor("idb", [128, 128], BF16, kind="ExternalInput").ap()
    mask_d = nc.dram_tensor("mask", [128, 128], F32, kind="ExternalInput").ap()
    out_d = nc.dram_tensor("out", [D_MODEL, SEQ], F32, kind="ExternalOutput").ap()

    total_cols = (11 * 3072 + 1280 + 8 * 2048) + NE * 8 * 4096
    wscr_d = nc.dram_tensor("wscr", [128, total_cols], BF16, kind="Internal").ap()
    cbounce_d = nc.dram_tensor("cbounce", [32, 128], F32, kind="Internal").ap()
    gemb_d = nc.dram_tensor("gemb", [16, TC], F32, kind="Internal").ap()

    dbg_outs = {}
    if dbg:
        for nm, shp in [("d_siluz", [128, 16 * TC]), ("d_xbc", [128, 18 * (TC + 4)]),
                        ("d_dt", [32, TC]), ("d_cc", [32, TC]), ("d_y", [128, 16 * TC]),
                        ("d_x1", [128, 8 * TC]), ("d_gem", [16, TC]),
                        ("d_he", [128, 16 * TC]), ("d_xh", [128, 8 * TC]),
                        ("d_s", [128, NH * HD])]:
            dbg_outs[nm] = nc.dram_tensor(nm, shp, F32, kind="ExternalOutput").ap()

    with ExitStack() as ctx:
        S = lambda *a: ctx.enter_context(nc.sbuf_tensor(*a))
        P = lambda *a: ctx.enter_context(nc.psum_tensor(*a))

        fring = [S(f"fr{i}", [128, FSLOT_COLS], BF16) for i in range(NSLOT_F)]
        mring = [S(f"mr{i}", [128, MSLOT_COLS], BF16) for i in range(NSLOT_M)]
        xf = [S("xf0", [128, 8, TC], F32), S("xf1", [128, 8, TC], F32)]
        x_bf = S("x_bf", [128, 8, TC], BF16)
        fb_bf = S("fb_bf", [128, 8, TC], BF16)
        xhat0 = S("xhat0", [128, 8, TC], BF16)
        siluz = S("siluz", [128, 16, TC], BF16)
        xbc = S("xbc", [128, 18, TC + 4], BF16)
        y = S("y", [128, 16, TC], BF16)
        he = S("he", [128, 16, TC], BF16)
        dt = S("dt", [32, TC], F32)
        cc = S("cc", [32, TC], F32)
        scr = S("scr", [128, TC], F32)
        scrb = S("scrb", [128, 2, TC], BF16)
        arena_m = S("arena_m", [1, 2048], F32)
        arena_f = S("arena_f", [1, 2048], F32)
        gem = S("gem", [16, TC], F32)
        wg = S("wg", [128, TC], BF16)
        st = S("st", [128, NH * HD], BF16)
        m_s = S("m_s", [128, 128], BF16)
        btok = S("btok", [128, 128], BF16)
        lp2 = S("lp2", [128, 8, 128], BF16)
        dl = S("dl", [128, 128], F32)
        xdt2 = S("xdt2", [128, 8, HD], BF16)
        xdw2 = S("xdw2", [128, 8, HD], BF16)
        ytok2 = S("ytok2", [128, 8, HD], BF16)
        ytmp = S("ytmp", [128, 2, HD], BF16)
        dtT = S("dtT", [128, 32], F32)
        cT = S("cT", [128, 32], F32)
        ecT = S("ecT", [128, 32], F32)
        cqb = S("cqb", [128, 64], F32)
        xdf = S("xdf", [128, 32], F32)
        rows = S("rows", [1, 64], F32)
        w16 = S("w16", [128, 16], F32)
        gts = S("gts", [128, 14], F32)
        wtmp = S("wtmp", [128, 14], F32)
        r1 = S("r1", [128, 1], F32)
        r2 = S("r2", [128, 1], F32)
        rq = S("rq", [128, 1], F32)
        idf = S("idf_s", [128, 128], F32)
        idb = S("idb_s", [128, 128], BF16)
        mask = S("mask_s", [128, 128], F32)
        ones_row = S("ones_row", [1, 128], F32)
        ones_col = S("ones_col", [128, 1], BF16)
        zero32 = S("zero32", [32, 128], F32)
        cwp = S("cwp_s", [128, 72], F32)
        cbp = S("cbp_s", [128, 18], F32)
        nwp = S("nwp_s", [128, 16], F32)
        rmsp = S("rmsp_s", [128, 112], F32)
        rmssp = S("rmssp_s", [128, 16], F32)
        dcol = S("dcol_s", [128, 16], F32)
        dtb = S("dtb_s", [32, 1], F32)
        acol = S("acol_s", [32, 1], F32)
        rwp = S("rwp_s", [128, 8, NE_R], F32)

        pb = [P(f"pb{i}", [128, 512], F32) for i in range(6)]
        pb6 = P("pb6", [128, 1024], BF16)
        pb7 = P("pb7", [128, 1024], BF16)

        kb = KB(nc, ctx)
        E = kb.emit

        def scrF():
            return he[:].rearrange("p a b -> p (a b)")[:, 0:1024].bitcast(F32)

        mpieces = []
        fpieces = []
        cursor = [0]

        def add_piece(lst, src, K, W, scales=None):
            lst.append(Piece(src, K, W, cursor[0], scales))
            cursor[0] += K * W

        ipr = ipw_d.rearrange("(k p) f -> p k f", p=128)
        for g in range(12):
            f0 = g * 384
            f1 = min(DIP, f0 + 384)
            add_piece(mpieces, ipr[:, :, f0:f1], 8, f1 - f0)
        opr = opw_d.rearrange("(k p) o -> p k o", p=128)
        for o in range(8):
            add_piece(mpieces, opr[:, :, o * 128:(o + 1) * 128], 16, 128,
                      [nwp[:, k:k + 1] for k in range(16)])
        for e in range(NE):
            if e < NE_R:
                w1e = w1_d[e].rearrange("(k p) h -> p k h", p=128)
                w2e = w2_d[e].rearrange("(k p) o -> p k o", p=128)
                rc = lambda k, e=e: rmsp[:, e * 8 + k:e * 8 + k + 1]
            else:
                w1e = w1s_d[e - NE_R].rearrange("(k p) h -> p k h", p=128)
                w2e = w2s_d[e - NE_R].rearrange("(k p) o -> p k o", p=128)
                rc = lambda k, e=e: rmssp[:, (e - NE_R) * 8 + k:(e - NE_R) * 8 + k + 1]
            for qq in range(4):
                add_piece(fpieces, w1e[:, :, qq * 512:(qq + 1) * 512], 8, 512,
                          [rc(k) for k in range(8)])
            for qq in range(4):
                add_piece(fpieces, w2e[:, :, qq * 256:(qq + 1) * 256], 16, 256)
        assert cursor[0] == total_cols, (cursor[0], total_cols)

        for dst, src in [(cwp, cwp_d), (cbp, cbp_d), (nwp, nwp_d), (rmsp, rmsp_d),
                         (rmssp, rmssp_d), (dcol, dcol_d), (dtb, dtb_d),
                         (idf, idf_d), (idb, idb_d), (mask, mask_d)]:
            E("p", lambda d=dst, s=src: nc.gpsimd.dma_start(out=d[:], in_=s), w=[dst], dma="init")
        E("p", lambda: nc.gpsimd.dma_start(
            out=rwp[:], in_=rwp_d.rearrange("p (k e) -> p k e", k=8)), w=[rwp], dma="init")
        E("p", lambda: nc.gpsimd.dma_start(out=acol[:], in_=alog_d), w=[acol], dma="init")
        E("v", lambda: nc.vector.memset(ones_row[:], 1.0), w=[ones_row])
        E("v", lambda: nc.vector.memset(ones_col[:], 1.0), w=[ones_col])
        E("v", lambda: nc.vector.memset(zero32[:], 0.0), w=[zero32])
        E("v", lambda: nc.vector.memset(w16[:], 0.0), w=[w16])
        E("v", lambda: nc.vector.memset(st[:], 0.0), w=[st])
        E("v", lambda: nc.vector.memset(xbc[:], 0.0), w=[xbc])
        E("s", lambda: nc.scalar.activation(acol[:], acol[:], AF.Exp), r=[acol], w=[acol])
        E("v", lambda: nc.vector.tensor_scalar_mul(acol[:], acol[:], -1.0), r=[acol], w=[acol])

        for idx, pc in enumerate(mpieces + fpieces):
            slot = fring[idx % NSLOT_F]
            v3 = slot[:, 0:pc.K * pc.W].rearrange("p (k w) -> p k w", k=pc.K)
            E("p", lambda v=v3, s=pc.src: nc.gpsimd.dma_start(out=v, in_=s),
              w=[slot], dma=f"cv{idx % NSLOT_F}")
            if pc.scales:
                for k in range(pc.K):
                    E("v", lambda v=v3, k=k, sc=pc.scales[k]:
                      nc.vector.tensor_scalar(v[:, k, :], v[:, k, :], sc, None, op0=ALU.mult),
                      r=[slot], w=[slot])
            E("q", lambda sl=slot, pc=pc: nc.sync.dma_start(
                out=wscr_d[:, pc.off:pc.off + pc.K * pc.W], in_=sl[:, 0:pc.K * pc.W]),
                r=[slot], w=["wscr"], dma=f"cw{idx % NSLOT_F}")
        kb.fence()

        RM = Ring(kb, nc, wscr_d, mring, mpieces, "ms")
        RF = Ring(kb, nc, wscr_d, fring, fpieces, "fs")

        xr = x_d.rearrange("(c p) t -> p c t", p=128)
        outr = out_d.rearrange("(c p) t -> p c t", p=128)

        def chunk_ap(base, cix):
            if isinstance(cix, int):
                return base[:, :, cix * TC:(cix + 1) * TC]
            k, sub = cix
            if sub < 2:
                return base[:, :, bass.ts(k, 2 * TC)][:, :, sub * TC:(sub + 1) * TC]
            return base[:, :, bass.ts(k + 1, 2 * TC)][:, :, 0:TC]

        def mamba_thunks(cix, par):
            T = []
            A = T.append
            xt = xf[par]
            A(lambda: E("s", lambda: nc.scalar.dma_start(out=xt[:], in_=chunk_ap(xr, cix)),
                        w=[xt], dma="x"))
            for c8 in range(8):
                A(lambda c8=c8: E("p", lambda: nc.gpsimd.tensor_copy(
                    x_bf[:, c8, :], xt[:, c8, :]), r=[xt], w=[x_bf]))
            for c0 in range(0, 18, 3):
                def halo(c0=c0):
                    for c in range(c0, min(18, c0 + 3)):
                        E("p", lambda c=c: nc.gpsimd.tensor_copy(
                            xbc[:, c, 0:4], xbc[:, c, TC:TC + 4]), r=[xbc], w=[xbc])
                A(halo)

            def ip_col(j):
                g, jl = j // 3, j % 3
                wcol = 128 if j < 34 else 32
                bank = pb[2 + j % 2]
                slot, pc = RM.need(g)
                for k in range(8):
                    base = k * pc.W + jl * 128
                    E("t", lambda b=bank, sl=slot, base=base, w=wcol, k=k:
                      nc.tensor.matmul(b[0:w, 0:TC], sl[:, base:base + w], x_bf[:, k, :],
                                       start=(k == 0), stop=(k == 7)),
                      r=[slot, x_bf], w=[bank])
                if j < 16:
                    E("s", lambda b=bank, j=j: nc.scalar.activation(
                        siluz[:, j, :], b[:, 0:TC], AF.Silu), r=[bank], w=[siluz])
                elif j < 34:
                    E("s", lambda b=bank, j=j: nc.scalar.copy(
                        xbc[:, j - 16, 4:4 + TC], b[:, 0:TC]), r=[bank], w=[xbc])
                else:
                    E("v", lambda b=bank: nc.vector.tensor_scalar(
                        scr[0:32, :], b[0:32, 0:TC], dtb[:], None, op0=ALU.add),
                        r=[bank, dtb], w=[scr])
                    E("s", lambda: nc.scalar.activation(cc[:, :], scr[0:32, :], AF.Abs),
                      r=[scr], w=[cc])
                    E("s", lambda: nc.scalar.activation(cc[:, :], cc[:, :], AF.Exp, scale=-1.0),
                      r=[cc], w=[cc])
                    E("s", lambda: nc.scalar.activation(cc[:, :], cc[:, :], AF.Ln, bias=1.0),
                      r=[cc], w=[cc])
                    E("v", lambda: nc.vector.tensor_scalar_max(scr[0:32, :], scr[0:32, :], 0.0),
                      r=[scr], w=[scr])
                    E("v", lambda: nc.vector.tensor_add(dt[:, :], scr[0:32, :], cc[:, :]),
                      r=[scr, cc], w=[dt])
            for j in range(35):
                A(lambda j=j: ip_col(j))

            def conv_c(c):
                E("v", lambda: nc.vector.tensor_scalar(
                    scr[:, :], xbc[:, c, 1:1 + TC], cwp[:, c * 4 + 3:c * 4 + 4],
                    cbp[:, c:c + 1], op0=ALU.mult, op1=ALU.add),
                    r=[xbc, cwp, cbp], w=[scr])
                for jj in (2, 1, 0):
                    E("v", lambda jj=jj: nc.vector.scalar_tensor_tensor(
                        scr[:, :], xbc[:, c, 4 - jj:4 - jj + TC], cwp[:, c * 4 + jj:c * 4 + jj + 1],
                        scr[:, :], op0=ALU.mult, op1=ALU.add),
                        r=[xbc, cwp, scr], w=[scr])
                E("s", lambda: nc.scalar.activation(xbc[:, c, 4:4 + TC], scr[:, :], AF.Silu),
                  r=[scr], w=[xbc])
            for c in range(18):
                A(lambda c=c: conv_c(c))

            def dt_a():
                E("v", lambda: nc.vector.tensor_scalar(
                    scr[0:32, :], dt[:, :], acol[:], None, op0=ALU.mult),
                    r=[dt, acol], w=[scr])
                for q in range(NQ):
                    E("v", lambda q=q: nc.vector.tensor_tensor_scan(
                        cc[:, q * 128:(q + 1) * 128], scr[0:32, q * 128:(q + 1) * 128],
                        zero32[:, 0:128], 0.0, op0=ALU.add, op1=ALU.add),
                        r=[scr, zero32], w=[cc])
            A(dt_a)

            for q in range(NQ):
                qs0 = q * 128
                xq = 4 + q * 128

                def qstart(q=q, qs0=qs0, xq=xq):
                    E("q", lambda: nc.sync.dma_start(out=cbounce_d, in_=cc[:, qs0:qs0 + 128]),
                      r=[cc], w=["cbounce"], dma="cb")
                    E("q", lambda: nc.sync.dma_start(
                        out=arena_m[0:1, :],
                        in_=cbounce_d[0:16].rearrange("h t -> (h t)")[None, :]),
                        r=["cbounce"], w=[arena_m], dma="ar")
                    E("t", lambda: nc.tensor.matmul(
                        pb[4][:, 0:128], xbc[:, 16, xq:xq + 128], xbc[:, 17, xq:xq + 128],
                        start=True, stop=True), r=[xbc], w=[pb[4]])
                    E("v", lambda: nc.vector.tensor_copy(m_s[:], pb[4][:, 0:128]),
                      r=[pb[4]], w=[m_s])
                    E("t", lambda: nc.tensor.transpose(
                        pb7[:, 0:128], xbc[:, 16, xq:xq + 128], idb[:]), r=[xbc, idb], w=[pb7])
                    E("s", lambda: nc.scalar.copy(btok[:], pb7[:, 0:128]), r=[pb7], w=[btok])
                    E("t", lambda: nc.tensor.transpose(
                        pb[4][:, 128:160], dt[:, qs0:qs0 + 128], idf[0:32, 0:32]),
                        r=[dt, idf], w=[pb[4]])
                    E("v", lambda: nc.vector.tensor_copy(dtT[:], pb[4][:, 128:160]),
                      r=[pb[4]], w=[dtT])
                    E("t", lambda: nc.tensor.transpose(
                        pb[4][:, 160:192], cc[:, qs0:qs0 + 128], idf[0:32, 0:32]),
                        r=[cc, idf], w=[pb[4]])
                    E("v", lambda: nc.vector.tensor_copy(cT[:], pb[4][:, 160:192]),
                      r=[pb[4]], w=[cT])
                    E("s", lambda: nc.scalar.activation(ecT[:], pb[4][:, 160:192], AF.Exp),
                      r=[pb[4]], w=[ecT])
                    E("t", lambda: nc.tensor.transpose(
                        pb[4][0:1, 192:224], cc[:, qs0 + 127:qs0 + 128], idf[0:32, 0:32]),
                        r=[cc, idf], w=[pb[4]])
                    E("s", lambda: nc.scalar.copy(rows[0:1, 0:32], pb[4][0:1, 192:224]),
                      r=[pb[4]], w=[rows])
                    E("s", lambda: nc.scalar.activation(rows[0:1, 32:64], pb[4][0:1, 192:224], AF.Exp),
                      r=[pb[4]], w=[rows])
                    E("t", lambda: nc.tensor.matmul(pb[4][:, 224:288], ones_row[:],
                                                    rows[0:1, 0:64], start=True, stop=True),
                      r=[ones_row, rows], w=[pb[4]])
                    E("v", lambda: nc.vector.tensor_copy(cqb[:], pb[4][:, 224:288]),
                      r=[pb[4]], w=[cqb])
                    E("v", lambda: nc.vector.tensor_sub(xdf[:], cqb[:, 0:32], cT[:]),
                      r=[cqb, cT], w=[xdf])
                    E("s", lambda: nc.scalar.activation(xdf[:], xdf[:], AF.Exp), r=[xdf], w=[xdf])
                A(qstart)

                def arena2(q=q):
                    E("q", lambda: nc.sync.dma_start(
                        out=arena_m[0:1, :],
                        in_=cbounce_d[16:32].rearrange("h t -> (h t)")[None, :]),
                        r=["cbounce"], w=[arena_m], dma="ar")

                def tA(h, xq=xq):
                    hp = h % 8
                    hc, ho = h // 2, (h % 2) * 64
                    E("t", lambda: nc.tensor.matmul(
                        pb[5][:, 0:128], ones_row[:],
                        arena_m[0:1, (h % 16) * 128:(h % 16 + 1) * 128],
                        start=True, stop=True), r=[ones_row, arena_m], w=[pb[5]])
                    E("v", lambda: nc.vector.scalar_tensor_tensor(
                        dl[:], pb[5][:, 0:128], cT[:, h:h + 1], mask[:],
                        op0=ALU.subtract, op1=ALU.add), r=[pb[5], cT, mask], w=[dl])
                    E("s", lambda: nc.scalar.activation(lp2[:, hp, :], dl[:], AF.Exp),
                      r=[dl], w=[lp2])
                    E("v", lambda: nc.vector.tensor_mul(lp2[:, hp, :], lp2[:, hp, :], m_s[:]),
                      r=[lp2, m_s], w=[lp2])
                    E("t", lambda: nc.tensor.transpose(
                        pb6[:, 0:64], xbc[ho:ho + 64, hc, xq:xq + 128],
                        idb[ho:ho + 64, ho:ho + 64]), r=[xbc, idb], w=[pb6])
                    E("s", lambda: nc.scalar.mul(xdt2[:, hp, :], pb6[:, 0:64], dtT[:, h:h + 1]),
                      r=[pb6, dtT], w=[xdt2])
                    E("v", lambda: nc.vector.tensor_scalar(
                        xdw2[:, hp, :], xdt2[:, hp, :], xdf[:, h:h + 1], None, op0=ALU.mult),
                        r=[xdt2, xdf], w=[xdw2])

                def tB(h, xq=xq):
                    hp = h % 8
                    E("t", lambda: nc.tensor.matmul(pb[2][:, 0:64], lp2[:, hp, :],
                                                    xdt2[:, hp, :], start=True, stop=True),
                      r=[lp2, xdt2], w=[pb[2]])
                    E("t", lambda: nc.tensor.matmul(
                        pb[3][:, 0:64], xbc[:, 17, xq:xq + 128], st[:, h * 64:(h + 1) * 64],
                        start=True, stop=True), r=[xbc, st], w=[pb[3]])
                    E("s", lambda: nc.scalar.mul(ytmp[:, h % 2, :], pb[3][:, 0:64], ecT[:, h:h + 1]),
                      r=[pb[3], ecT], w=[ytmp])
                    E("v", lambda: nc.vector.tensor_add(
                        ytok2[:, hp, :], pb[2][:, 0:64], ytmp[:, h % 2, :]),
                      r=[pb[2], ytmp], w=[ytok2])
                    E("t", lambda: nc.tensor.matmul(pb[4][:, 0:64], btok[:], xdw2[:, hp, :],
                                                    start=True, stop=True),
                      r=[btok, xdw2], w=[pb[4]])
                    E("v", lambda: nc.vector.scalar_tensor_tensor(
                        st[:, h * 64:(h + 1) * 64], st[:, h * 64:(h + 1) * 64],
                        cqb[:, 32 + h:33 + h], pb[4][:, 0:64], op0=ALU.mult, op1=ALU.add),
                        r=[st, cqb, pb[4]], w=[st])

                def tC(h, qs0=qs0, xq=xq):
                    hp = h % 8
                    hc, ho = h // 2, (h % 2) * 64
                    E("t", lambda: nc.tensor.transpose(pb7[0:64, 128:256], ytok2[:, hp, :], idb[:]),
                      r=[ytok2, idb], w=[pb7])
                    E("v", lambda: nc.vector.scalar_tensor_tensor(
                        y[ho:ho + 64, hc, qs0:qs0 + 128], xbc[ho:ho + 64, hc, xq:xq + 128],
                        dcol[ho:ho + 64, hc:hc + 1], pb7[0:64, 128:256],
                        op0=ALU.mult, op1=ALU.add), r=[xbc, dcol, pb7], w=[y])

                for h in range(NH + 7):
                    if h < NH:
                        A(lambda h=h, tA=tA: tA(h))
                    if h == 15:
                        A(arena2)
                    if 5 <= h < NH + 5:
                        A(lambda h=h, tB=tB: tB(h - 5))
                    if h >= 7:
                        A(lambda h=h, tC=tC: tC(h - 7))

            def gat_c(c):
                E("v", lambda: nc.vector.tensor_mul(y[:, c, :], y[:, c, :], siluz[:, c, :]),
                  r=[y, siluz], w=[y])
                E("s", lambda: nc.scalar.activation(scrb[:, c % 2, :], y[:, c, :], AF.Square),
                  r=[y], w=[scrb])
                E("t", lambda: nc.tensor.matmul(
                    pb[5][0:1, 0:TC], ones_col[:], scrb[:, c % 2, :],
                    start=(c == 0), stop=(c == 15)), r=[ones_col, scrb], w=[pb[5]])
            for c in range(16):
                A(lambda c=c: gat_c(c))

            def grms():
                E("v", lambda: nc.vector.tensor_scalar(
                    arena_m[0:1, 0:TC], pb[5][0:1, 0:TC], 1.0 / D_INNER, MRMS_EPS,
                    op0=ALU.mult, op1=ALU.add), r=[pb[5]], w=[arena_m])
                E("s", lambda: nc.scalar.activation(arena_m[0:1, 0:TC], arena_m[0:1, 0:TC], AF.Ln),
                  r=[arena_m], w=[arena_m])
                E("v", lambda: nc.vector.tensor_scalar_mul(
                    arena_m[0:1, 0:TC], arena_m[0:1, 0:TC], -0.5), r=[arena_m], w=[arena_m])
                E("s", lambda: nc.scalar.activation(arena_m[0:1, 0:TC], arena_m[0:1, 0:TC], AF.Exp),
                  r=[arena_m], w=[arena_m])
                E("t", lambda: nc.tensor.matmul(pb[5][:, 0:TC], ones_row[:], arena_m[0:1, 0:TC],
                                                start=True, stop=True),
                  r=[ones_row, arena_m], w=[pb[5]])
            A(grms)
            for c0 in range(0, 16, 4):
                def gapply(c0=c0):
                    for c in range(c0, c0 + 4):
                        E("v", lambda c=c: nc.vector.tensor_mul(
                            y[:, c, :], y[:, c, :], pb[5][:, 0:TC]), r=[y, pb[5]], w=[y])
                A(gapply)

            def op_col(o):
                slot, pc = RM.need(12 + o)
                bank = pb[2 + o % 2]
                for k in range(16):
                    base = k * pc.W
                    E("t", lambda b=bank, sl=slot, base=base, k=k:
                      nc.tensor.matmul(b[:, 0:TC], sl[:, base:base + 128], y[:, k, :],
                                       start=(k == 0), stop=(k == 15)),
                      r=[slot, y], w=[bank])
                E("v", lambda b=bank, o=o: nc.vector.tensor_add(
                    xt[:, o, :], xt[:, o, :], b[:, 0:TC]), r=[bank, xt], w=[xt])
            for o in range(8):
                A(lambda o=o: op_col(o))

            def ln1_c(c):
                E("s", lambda: nc.scalar.copy(x_bf[:, c, :], xt[:, c, :]), r=[xt], w=[x_bf])
                E("t", lambda: nc.tensor.matmul(
                    pb[4][0:1, 0:TC], ones_col[:], x_bf[:, c, :],
                    start=(c == 0), stop=(c == 7)), r=[ones_col, x_bf], w=[pb[4]])
                E("s", lambda: nc.scalar.activation(scrb[:, c % 2, :], x_bf[:, c, :], AF.Square),
                  r=[x_bf], w=[scrb])
                E("t", lambda: nc.tensor.matmul(
                    pb[5][0:1, 0:TC], ones_col[:], scrb[:, c % 2, :],
                    start=(c == 0), stop=(c == 7)), r=[ones_col, scrb], w=[pb[5]])
            for c in range(8):
                A(lambda c=c: ln1_c(c))

            def ln1_rows():
                E("v", lambda: nc.vector.tensor_scalar(
                    arena_m[0:1, 0:TC], pb[4][0:1, 0:TC], 1.0 / D_MODEL, None, op0=ALU.mult),
                    r=[pb[4]], w=[arena_m])
                E("v", lambda: nc.vector.tensor_scalar(
                    arena_m[0:1, TC:2 * TC], pb[5][0:1, 0:TC], 1.0 / D_MODEL, LN_EPS,
                    op0=ALU.mult, op1=ALU.add), r=[pb[5]], w=[arena_m])
                E("s", lambda: nc.scalar.activation(
                    scr[0:1, 0:TC], arena_m[0:1, 0:TC], AF.Square), r=[arena_m], w=[scr])
                E("v", lambda: nc.vector.tensor_sub(
                    arena_m[0:1, TC:2 * TC], arena_m[0:1, TC:2 * TC], scr[0:1, 0:TC]),
                    r=[arena_m, scr], w=[arena_m])
                E("s", lambda: nc.scalar.activation(
                    arena_m[0:1, TC:2 * TC], arena_m[0:1, TC:2 * TC], AF.Ln),
                    r=[arena_m], w=[arena_m])
                E("v", lambda: nc.vector.tensor_scalar_mul(
                    arena_m[0:1, TC:2 * TC], arena_m[0:1, TC:2 * TC], -0.5),
                    r=[arena_m], w=[arena_m])
                E("s", lambda: nc.scalar.activation(
                    arena_m[0:1, TC:2 * TC], arena_m[0:1, TC:2 * TC], AF.Exp),
                    r=[arena_m], w=[arena_m])
                E("v", lambda: nc.vector.tensor_mul(
                    arena_m[0:1, 0:TC], arena_m[0:1, 0:TC], arena_m[0:1, TC:2 * TC]),
                    r=[arena_m], w=[arena_m])
                E("t", lambda: nc.tensor.matmul(pb[2][:, 0:TC], ones_row[:],
                                                arena_m[0:1, TC:2 * TC], start=True, stop=True),
                  r=[ones_row, arena_m], w=[pb[2]])
                E("t", lambda: nc.tensor.matmul(pb[3][:, 0:TC], ones_row[:],
                                                arena_m[0:1, 0:TC], start=True, stop=True),
                  r=[ones_row, arena_m], w=[pb[3]])
            A(ln1_rows)
            for c0 in range(0, 8, 2):
                def ln1_apply(c0=c0):
                    for c in range(c0, c0 + 2):
                        E("v", lambda c=c: nc.vector.tensor_mul(
                            scr[:, :], xt[:, c, :], pb[2][:, 0:TC]), r=[xt, pb[2]], w=[scr])
                        E("v", lambda c=c: nc.vector.tensor_sub(
                            xt[:, c, :], scr[:, :], pb[3][:, 0:TC]), r=[scr, pb[3]], w=[xt])
                A(ln1_apply)
            return T

        def emit_ffn(cix, par, pump):
            if pump is None:
                pump = lambda: None
            xt = xf[par]
            for g4 in range(4):
                for k in range(8):
                    E("t", lambda g4=g4, k=k: nc.tensor.matmul(
                        pb[0][:, g4 * 14:(g4 + 1) * 14], xt[:, k, g4 * 128:(g4 + 1) * 128],
                        rwp[:, k, :], start=(k == 0), stop=(k == 7)),
                        r=[xt, rwp], w=[pb[0]])
                pump()
            for g4 in range(4):
                E("v", lambda g4=g4: nc.vector.reduce_max(
                    r1[:], pb[0][:, g4 * 14:(g4 + 1) * 14], axis=AX.X, negate=True),
                    r=[pb[0]], w=[r1])
                E("s", lambda g4=g4: nc.scalar.activation(
                    gts[:], pb[0][:, g4 * 14:(g4 + 1) * 14], AF.Exp, bias=r1[:]),
                    r=[pb[0], r1], w=[gts])
                E("v", lambda: nc.vector.reduce_sum(rq[:], gts[:], axis=AX.X), r=[gts], w=[rq])
                E("v", lambda: nc.vector.reciprocal(rq[:], rq[:]), r=[rq], w=[rq])
                E("v", lambda: nc.vector.tensor_scalar(gts[:], gts[:], rq[:], None, op0=ALU.mult),
                  r=[gts, rq], w=[gts])
                E("v", lambda: nc.vector.reduce_max(r1[:], gts[:], axis=AX.X), r=[gts], w=[r1])
                E("v", lambda: nc.vector.tensor_scalar(wtmp[:], gts[:], r1[:], None, op0=ALU.is_ge),
                  r=[gts, r1], w=[wtmp])
                E("v", lambda: nc.vector.scalar_tensor_tensor(
                    wtmp[:], wtmp[:], -1e30, gts[:], op0=ALU.mult, op1=ALU.add),
                    r=[wtmp, gts], w=[wtmp])
                E("v", lambda: nc.vector.reduce_max(r2[:], wtmp[:], axis=AX.X), r=[wtmp], w=[r2])
                E("v", lambda: nc.vector.tensor_scalar(wtmp[:], gts[:], r2[:], None, op0=ALU.is_ge),
                  r=[gts, r2], w=[wtmp])
                E("v", lambda: nc.vector.tensor_mul(wtmp[:], gts[:], wtmp[:]),
                  r=[gts, wtmp], w=[wtmp])
                E("v", lambda: nc.vector.reduce_sum(rq[:], wtmp[:], axis=AX.X), r=[wtmp], w=[rq])
                E("v", lambda: nc.vector.tensor_scalar_add(rq[:], rq[:], GATE_EPS), r=[rq], w=[rq])
                E("v", lambda: nc.vector.reciprocal(rq[:], rq[:]), r=[rq], w=[rq])
                E("v", lambda: nc.vector.tensor_scalar(
                    w16[:, 0:14], wtmp[:], rq[:], None, op0=ALU.mult), r=[wtmp, rq], w=[w16])
                E("t", lambda: nc.tensor.transpose(pb[1][0:16, 0:128], w16[:], idf[:]),
                  r=[w16, idf], w=[pb[1]])
                E("v", lambda g4=g4: nc.vector.tensor_copy(
                    gem[:, g4 * 128:(g4 + 1) * 128], pb[1][0:16, 0:128]), r=[pb[1]], w=[gem])
                pump()
            E("q", lambda: nc.sync.dma_start(out=gemb_d, in_=gem[:]),
              r=[gem], w=["gemb"], dma="gemb")
            for c in range(8):
                E("s", lambda c=c: nc.scalar.activation(
                    fb_bf[:, c % 2, :], xt[:, c, :], AF.Square), r=[xt], w=[fb_bf])
                E("t", lambda c=c: nc.tensor.matmul(
                    pb[1][0:1, 0:TC], ones_col[:], fb_bf[:, c % 2, :],
                    start=(c == 0), stop=(c == 7)), r=[ones_col, fb_bf], w=[pb[1]])
            E("v", lambda: nc.vector.tensor_scalar(
                arena_f[0:1, 0:TC], pb[1][0:1, 0:TC], 1.0 / D_MODEL, FRMS_EPS,
                op0=ALU.mult, op1=ALU.add), r=[pb[1]], w=[arena_f])
            E("s", lambda: nc.scalar.activation(arena_f[0:1, 0:TC], arena_f[0:1, 0:TC], AF.Ln),
              r=[arena_f], w=[arena_f])
            E("v", lambda: nc.vector.tensor_scalar_mul(
                arena_f[0:1, 0:TC], arena_f[0:1, 0:TC], -0.5), r=[arena_f], w=[arena_f])
            E("s", lambda: nc.scalar.activation(arena_f[0:1, 0:TC], arena_f[0:1, 0:TC], AF.Exp),
              r=[arena_f], w=[arena_f])
            E("t", lambda: nc.tensor.matmul(pb[0][:, 0:TC], ones_row[:], arena_f[0:1, 0:TC],
                                            start=True, stop=True),
              r=[ones_row, arena_f], w=[pb[0]])
            pump()
            for c in range(8):
                E("v", lambda c=c: nc.vector.tensor_mul(
                    xhat0[:, c, :], xt[:, c, :], pb[0][:, 0:TC]), r=[xt, pb[0]], w=[xhat0])
            pump()

            for e in range(NE):
                if e < NE_R and e % 2 == 0:
                    E("q", lambda e=e: nc.sync.dma_start(
                        out=arena_f[0:1, 0:2 * TC],
                        in_=gemb_d[e:e + 2, :].rearrange("a t -> (a t)")[None, :]),
                        r=["gemb"], w=[arena_f], dma="ar2")
                if e < NE_R:
                    E("t", lambda e=e: nc.tensor.matmul(
                        pb[2][:, 0:TC], ones_row[:],
                        arena_f[0:1, (e % 2) * TC:(e % 2 + 1) * TC],
                        start=True, stop=True), r=[ones_row, arena_f], w=[pb[2]])
                    E("s", lambda: nc.scalar.copy(wg[:], pb[2][:, 0:TC]), r=[pb[2]], w=[wg])
                p1a = 8 * e
                for hcol in range(16):
                    slot, pc = RF.need(p1a + hcol // 4)
                    hl = hcol % 4
                    bank = pb[hcol % 2]
                    for k in range(8):
                        base = k * pc.W + hl * 128
                        E("t", lambda b=bank, sl=slot, base=base, k=k:
                          nc.tensor.matmul(b[:, 0:TC], sl[:, base:base + 128],
                                           xhat0[:, k, :], start=(k == 0), stop=(k == 7)),
                          r=[slot, xhat0], w=[bank])
                    E("s", lambda b=bank, hcol=hcol: nc.scalar.activation(
                        he[:, hcol, :], b[:, 0:TC], AF.Silu), r=[bank], w=[f"he{hcol}"])
                    if e < NE_R:
                        E("v", lambda hcol=hcol: nc.vector.tensor_mul(
                            he[:, hcol, :], he[:, hcol, :], wg[:]),
                            r=[f"he{hcol}", wg], w=[f"he{hcol}"])
                    pump()
                p2a = 8 * e + 4
                for o in range(8):
                    slot, pc = RF.need(p2a + o // 2)
                    ol = o % 2
                    bank = pb[o % 2]
                    for k in range(16):
                        base = k * pc.W + ol * 128
                        E("t", lambda b=bank, sl=slot, base=base, k=k:
                          nc.tensor.matmul(b[:, 0:TC], sl[:, base:base + 128],
                                           he[:, k, :], start=(k == 0), stop=(k == 15)),
                          r=[slot, f"he{k}"], w=[bank])
                    E("v", lambda b=bank, o=o: nc.vector.tensor_add(
                        xt[:, o, :], xt[:, o, :], b[:, 0:TC]), r=[bank, xt], w=[xt])
                    pump()

            for c in range(8):
                E("s", lambda c=c: nc.scalar.copy(fb_bf[:, c, :], xt[:, c, :]), r=[xt], w=[fb_bf])
                E("t", lambda c=c: nc.tensor.matmul(
                    pb[0][0:1, 0:TC], ones_col[:], fb_bf[:, c, :],
                    start=(c == 0), stop=(c == 7)), r=[ones_col, fb_bf], w=[pb[0]])
                E("s", lambda c=c: nc.scalar.activation(
                    xhat0[:, c % 2, :], fb_bf[:, c, :], AF.Square), r=[fb_bf], w=[xhat0])
                E("t", lambda c=c: nc.tensor.matmul(
                    pb[1][0:1, 0:TC], ones_col[:], xhat0[:, c % 2, :],
                    start=(c == 0), stop=(c == 7)), r=[ones_col, xhat0], w=[pb[1]])
            pump()
            E("v", lambda: nc.vector.tensor_scalar(
                arena_f[0:1, 0:TC], pb[0][0:1, 0:TC], 1.0 / D_MODEL, None, op0=ALU.mult),
                r=[pb[0]], w=[arena_f])
            E("v", lambda: nc.vector.tensor_scalar(
                arena_f[0:1, TC:2 * TC], pb[1][0:1, 0:TC], 1.0 / D_MODEL, LN_EPS,
                op0=ALU.mult, op1=ALU.add), r=[pb[1]], w=[arena_f])
            E("s", lambda: nc.scalar.activation(
                scrF()[0:1, 0:TC], arena_f[0:1, 0:TC], AF.Square),
                r=[arena_f, "he0", "he1"], w=["he0", "he1"])
            E("v", lambda: nc.vector.tensor_sub(
                arena_f[0:1, TC:2 * TC], arena_f[0:1, TC:2 * TC], scrF()[0:1, 0:TC]),
                r=[arena_f, "he0"], w=[arena_f])
            E("s", lambda: nc.scalar.activation(
                arena_f[0:1, TC:2 * TC], arena_f[0:1, TC:2 * TC], AF.Ln), r=[arena_f], w=[arena_f])
            E("v", lambda: nc.vector.tensor_scalar_mul(
                arena_f[0:1, TC:2 * TC], arena_f[0:1, TC:2 * TC], -0.5), r=[arena_f], w=[arena_f])
            E("s", lambda: nc.scalar.activation(
                arena_f[0:1, TC:2 * TC], arena_f[0:1, TC:2 * TC], AF.Exp), r=[arena_f], w=[arena_f])
            E("v", lambda: nc.vector.tensor_mul(
                arena_f[0:1, 0:TC], arena_f[0:1, 0:TC], arena_f[0:1, TC:2 * TC]),
                r=[arena_f], w=[arena_f])
            E("t", lambda: nc.tensor.matmul(pb[0][:, 0:TC], ones_row[:], arena_f[0:1, TC:2 * TC],
                                            start=True, stop=True), r=[ones_row, arena_f], w=[pb[0]])
            E("t", lambda: nc.tensor.matmul(pb[1][:, 0:TC], ones_row[:], arena_f[0:1, 0:TC],
                                            start=True, stop=True), r=[ones_row, arena_f], w=[pb[1]])
            for c in range(8):
                E("v", lambda c=c: nc.vector.tensor_mul(
                    scrF()[:, 0:TC], xt[:, c, :], pb[0][:, 0:TC]),
                    r=[xt, pb[0], "he0", "he1"], w=["he0", "he1"])
                E("v", lambda c=c: nc.vector.tensor_sub(
                    xt[:, c, :], scrF()[:, 0:TC], pb[1][:, 0:TC]),
                    r=["he0", pb[1]], w=[xt])
            pump()
            E("s", lambda: nc.scalar.dma_start(out=chunk_ap(outr, cix), in_=xt[:]),
              r=[xt], dma="out")

        def run_pipelined(f_cix, f_par, m_cix, m_par):
            RM.reset()
            RF.reset()
            thunks = mamba_thunks(m_cix, m_par) if m_cix is not None else []
            pos = [0.0]
            idx = [0]
            per = 0.0

            def pump():
                pos[0] += per
                kb.eager = True
                while idx[0] < len(thunks) and idx[0] < pos[0]:
                    thunks[idx[0]]()
                    idx[0] += 1
                kb.eager = False
            if f_cix is not None:
                emit_ffn(f_cix, f_par, pump if thunks else None)
            kb.eager = True
            while idx[0] < len(thunks):
                thunks[idx[0]]()
                idx[0] += 1
            kb.eager = False
            kb.fence()

        if n_chunks == NCHUNK:
            run_pipelined(None, 0, 0, 0)
            with nc.Fori(0, 7) as k:
                run_pipelined((k, 0), 0, (k, 1), 1)
                run_pipelined((k, 1), 1, (k, 2), 0)
            run_pipelined(14, 0, 15, 1)
            run_pipelined(15, 1, None, 0)
        else:
            run_pipelined(None, 0, 0, 0)
            for c in range(n_chunks):
                run_pipelined(c, c % 2, c + 1 if c + 1 < n_chunks else None, (c + 1) % 2)

        if dbg:
            last_par = (n_chunks - 1) % 2
            dumps = [("d_siluz", siluz[:].rearrange("p a b -> p (a b)")),
                     ("d_xbc", xbc[:].rearrange("p a b -> p (a b)")),
                     ("d_dt", dt[:]), ("d_cc", cc[:]),
                     ("d_y", y[:].rearrange("p a b -> p (a b)")),
                     ("d_x1", xf[last_par][:].rearrange("p a b -> p (a b)")),
                     ("d_gem", gem[:]),
                     ("d_he", he[:].rearrange("p a b -> p (a b)")),
                     ("d_xh", xhat0[:].rearrange("p a b -> p (a b)")),
                     ("d_s", st[:])]
            for nm, ap in dumps:
                E("p", lambda nm=nm, ap=ap: nc.gpsimd.dma_start(
                    out=dbg_outs[nm], in_=ap), r=[], dma="dbg")
            kb.fence()

    return nc


def _host_packs(inputs):
    f = np.float32
    conv_w = np.asarray(inputs["L0_attn_conv_w"], f)
    conv_b = np.asarray(inputs["L0_attn_conv_b"], f)
    return {
        "ipw": np.ascontiguousarray(np.asarray(inputs["L0_attn_in_proj_w"], f)),
        "opw": np.ascontiguousarray(np.asarray(inputs["L0_attn_out_proj_w"], f)),
        "w1": np.ascontiguousarray(np.asarray(inputs["L0_ffn_w1"], f)),
        "w2": np.ascontiguousarray(np.asarray(inputs["L0_ffn_w2"], f)),
        "w1s": np.ascontiguousarray(np.asarray(inputs["L0_ffn_w1_s"], f)),
        "w2s": np.ascontiguousarray(np.asarray(inputs["L0_ffn_w2_s"], f)),
        "cwp": np.ascontiguousarray(conv_w.reshape(4, 18, 128).transpose(2, 1, 0).reshape(128, 72)),
        "cbp": np.ascontiguousarray(conv_b.reshape(18, 128).T),
        "nwp": np.ascontiguousarray(np.asarray(inputs["L0_attn_norm_w"], f).reshape(16, 128).T),
        "rmsp": np.ascontiguousarray(
            np.asarray(inputs["L0_ffn_rms_w"], f).reshape(14, 8, 128).transpose(2, 0, 1).reshape(128, 112)),
        "rmssp": np.ascontiguousarray(
            np.asarray(inputs["L0_ffn_rms_w_s"], f).reshape(2, 8, 128).transpose(2, 0, 1).reshape(128, 16)),
        "dcol": np.ascontiguousarray(
            np.repeat(np.asarray(inputs["L0_attn_D"], f), 64).reshape(16, 128).T),
        "dtb": np.ascontiguousarray(np.asarray(inputs["L0_attn_dt_bias"], f).reshape(32, 1)),
        "alog": np.ascontiguousarray(np.asarray(inputs["L0_attn_A_log"], f).reshape(32, 1)),
        "rwp": np.ascontiguousarray(
            np.asarray(inputs["L0_ffn_router_w"], f).reshape(8, 128, 14).transpose(1, 0, 2).reshape(128, 112)),
        "idf": np.eye(128, dtype=f),
        "idb": np.eye(128, dtype=ml_dtypes.bfloat16),
        "mask": np.where(np.arange(128)[None, :] >= np.arange(128)[:, None],
                         np.float32(0.0), np.float32(MASK_NEG)),
    }


def kernel(**inputs):
    x = np.asarray(inputs["L0_attn_x"], np.float32)
    packs = _host_packs(inputs)
    nc = build_program(NCHUNK, dbg=False)
    in_maps = []
    for b in range(8):
        m = dict(packs)
        m["x"] = np.ascontiguousarray(x[b].T)
        in_maps.append(m)
    res = run_bass_kernel_spmd(nc, in_maps, core_ids=list(range(8)))
    return np.stack([np.ascontiguousarray(r["out"].T) for r in res.results], axis=0)
